# version 05
# baseline (speedup 1.0000x reference)
import math
import contextlib
import numpy as np
import ml_dtypes
import concourse.bass as bass
import concourse.mybir as mybir
from concourse.bass_utils import run_bass_kernel_spmd

F32 = mybir.dt.float32
BF16 = mybir.dt.bfloat16
I32 = mybir.dt.int32
U8 = mybir.dt.uint8
AF = mybir.ActivationFunctionType
ALU = mybir.AluOpType
AX = mybir.AxisListType.X
BF = ml_dtypes.bfloat16

D = 4096
T = 2048
TG = 4096
NTT = T // 128
CD = D // 128
HQ, HKV, DH = 32, 8, 128
HID = 11008
HB = HID // 128
EPS = 1e-5
NEG = -1e30
DTS = {F32: 4, BF16: 2, I32: 4, U8: 1}
POOLB = 212736

SEMNAMES = ["ld1", "ld2", "wld", "mm", "dve", "acs", "pml", "st1", "st2",
            "x1", "x2", "x3", "bar"]

IDENT = np.eye(128).astype(BF)
IDENTF = np.eye(128, dtype=np.float32)


class Sem:
    def __init__(self, es, nc, name):
        self.h = es.enter_context(nc.semaphore(name))
        self.v = 0

    def inc(self, instr, n=1):
        instr.then_inc(self.h, n)
        self.v += n
        return instr


def _dma(q, out, in_, sem):
    sem.inc(q.dma_start(out=out, in_=in_), 16)


class Ctx:

    def __init__(self):
        self.nc = bass.Bass("TRN2", target_bir_lowering=False)
        self.es = contextlib.ExitStack()
        self._off = 0
        self._psum_i = 0
        self._inline = {}

    def setup(self):
        nc = self.nc
        self.pool = self.es.enter_context(nc.sbuf_tensor("pool", [128, POOLB], U8))
        self.banks = [self.es.enter_context(nc.psum_tensor(f"pp{i}", [128, 512], F32))
                      for i in range(8)]
        self.s = {n: Sem(self.es, nc, n) for n in SEMNAMES}
        self.es.enter_context(nc.Block())

    def inline(self, name, arr):
        if name not in self._inline:
            self._inline[name] = self.nc.inline_tensor(arr, name).ap()
        return self._inline[name]

    def section(self):
        self._off = 0
        self._psum_i = 0

    def carve(self, shape, dtype):
        nb = int(np.prod(shape[1:])) * DTS[dtype]
        off = (self._off + 63) & ~63
        self._off = off + nb
        assert self._off <= POOLB, f"pool overflow: {self._off}"
        v = self.pool[0:shape[0], off:off + nb].bitcast(dtype)
        if len(shape) > 2:
            dims = " ".join(f"d{i}" for i in range(1, len(shape)))
            kw = {f"d{i}": shape[i] for i in range(1, len(shape))}
            v = v.rearrange(f"p ({dims}) -> p {dims}", **kw)
        return v

    def psum(self, dtype=F32):
        t = self.banks[self._psum_i]
        self._psum_i += 1
        if dtype == F32:
            return t[:]
        return t[:].bitcast(dtype)[:, 0:512]

    def join(self):
        nc = self.nc
        b = self.s["bar"]
        engs = [nc.tensor, nc.vector, nc.scalar, nc.gpsimd, nc.sync]
        for e in engs:
            e.nop()
            b.inc(e.nop(), 1)
        for e in engs:
            e.wait_ge(b.h, b.v)

    def finish(self):
        self.es.close()
        return self.nc


def tok_positions(role):
    return np.concatenate([np.arange(256 * i + 128 * role, 256 * i + 128 * (role + 1))
                           for i in range(NTT)])


def key_globals(role):
    g = np.empty(TG, np.int64)
    for u in range(16):
        g[256 * u:256 * u + 128] = 256 * u + 128 * role + np.arange(128)
        g[256 * u + 128:256 * u + 256] = 256 * u + 128 * (1 - role) + np.arange(128)
    return g


def diag_masks(role):
    kg = key_globals(role)
    m = np.zeros((NTT, 128, 512), np.float32)
    for ti in range(NTT):
        g0 = 256 * ti + 128 * role
        nk5 = ti // 2 + 1
        k0 = 512 * (nk5 - 1)
        kj = kg[k0:k0 + 512][None, :]
        qi = np.arange(128)[:, None]
        m[ti] = np.where(kj <= g0 + qi, 0.0, NEG)
    return m.astype(BF)


def rope_tables(layer, role):
    pos = tok_positions(role).astype(np.float64)
    return rope_tables_pos(layer, pos)


def rope_tables_pos(layer, pos):
    if layer == 0:
        theta = 500000.0
        idx = np.arange(0, DH, 2, dtype=np.float64) / DH
        freq = 1.0 / (2.0 * math.pi) * theta ** (-idx)
        factor, lo, hi, L0 = 32.0, 1.0, 4.0, 8192.0
        freq_low, freq_high = lo / L0, hi / L0
        freq_scaled = np.where(freq < freq_low, freq / factor, freq)
        smooth = np.clip((L0 * freq - lo) / (hi - lo), 0.0, 1.0)
        freq_smooth = (1.0 - smooth) * (freq / factor) + smooth * freq
        is_mid = (freq >= freq_low) & (freq <= freq_high)
        freq = np.where(is_mid, freq_smooth, freq_scaled)
        phase = 2.0 * math.pi * pos[:, None] * freq[None, :]
    else:
        theta = 10000.0
        inv = 1.0 / theta ** (np.arange(0, DH, 2, dtype=np.float64)[: DH // 2] / DH)
        phase = pos[:, None] * inv[None, :]
    cos = np.cos(phase)
    sin = np.sin(phase)
    Ct = np.empty((128, T), np.float64)
    St = np.empty((128, T), np.float64)
    Ct[:64] = cos.T
    Ct[64:] = cos.T
    St[:64] = -sin.T
    St[64:] = sin.T
    return Ct.astype(np.float32), St.astype(np.float32)


def emit_ln_stats(nc, xt, xsq, st):
    sx, sxx, m, r, m2, sxxd, ve = (st[:, i:i + 1] for i in range(7))
    nc.vector.tensor_scalar(out=xsq, in0=xt, scalar1=1.0, scalar2=0.0,
                            op0=ALU.mult, op1=ALU.add, accum_out=sx)
    last = nc.vector.scalar_tensor_tensor(out=xsq, in0=xt, scalar=1.0, in1=xt,
                                          op0=ALU.mult, op1=ALU.mult, accum_out=sxx)

    def act1(d=D):
        nc.scalar.activation(out=m2, in_=sx, func=AF.Square, scale=1.0 / d)
        nc.scalar.activation(out=sxxd, in_=sxx, func=AF.Copy, scale=1.0 / d, bias=EPS)
        return nc.scalar.activation(out=m, in_=sx, func=AF.Copy, scale=1.0 / d)

    def vec2():
        return nc.vector.tensor_tensor(out=ve, in0=sxxd, in1=m2, op=ALU.subtract)

    def act2():
        nc.scalar.activation(out=ve, in_=ve, func=AF.Ln)
        return nc.scalar.activation(out=r, in_=ve, func=AF.Exp, scale=-0.5)

    return m, r, last, (act1, vec2, act2)



def emit_ln0(C, x, xn):
    nc = C.nc
    xt = [C.carve([128, D], F32) for _ in range(2)]
    xb = C.carve([128, D], BF16)
    xsq = C.carve([128, D], F32)
    st = C.carve([128, 8], F32)
    idt = C.carve([128, 128], BF16)
    stage = [C.carve([128, CD, 128], BF16) for _ in range(2)]
    pst = [C.psum(BF16) for _ in range(2)]
    ldx, cmp, act, trn, evc, stx = (C.s[n] for n in
                                    ("ld1", "x1", "acs", "mm", "x2", "st1"))
    trn0, evc0, stx0 = trn.v, evc.v, stx.v
    _dma(nc.sync, idt[:], C.inline("identb", IDENT)[:], ldx)
    applied_v = {}
    for tt in range(NTT):
        b = tt % 2
        if tt >= 2:
            nc.sync.wait_ge(cmp.h, applied_v[tt - 2])
        _dma(nc.sync, xt[b][:], x[tt * 128:(tt + 1) * 128, :], ldx)
        nc.vector.wait_ge(ldx.h, ldx.v)
        if tt >= 1:
            nc.vector.wait_ge(trn.h, trn0 + 32 * tt)
        m, r, last, (a1, v2, a3) = emit_ln_stats(nc, xt[b][:], xsq[:], st[:])
        cmp.inc(last, 1)
        nc.scalar.wait_ge(cmp.h, cmp.v)
        act.inc(a1(), 1)
        nc.vector.wait_ge(act.h, act.v)
        cmp.inc(v2(), 1)
        nc.scalar.wait_ge(cmp.h, cmp.v)
        act.inc(a3(), 1)
        nc.vector.wait_ge(act.h, act.v)
        cmp.inc(nc.vector.tensor_scalar(out=xb[:], in0=xt[b][:], scalar1=m, scalar2=r,
                                        op0=ALU.subtract, op1=ALU.mult), 1)
        applied_v[tt] = cmp.v
        nc.tensor.wait_ge(cmp.h, cmp.v)
        if tt >= 2:
            nc.scalar.wait_ge(stx.h, stx0 + 16 * (tt - 1))
        for g in range(8):
            bank = pst[g % 2]
            need = 8 * tt + g - 1
            if need > 0:
                nc.tensor.wait_ge(evc.h, evc0 + need)
            for j in range(4):
                c = g * 4 + j
                trn.inc(nc.tensor.transpose(bank[:, j * 128:(j + 1) * 128],
                                            xb[:, c * 128:(c + 1) * 128], idt[:]), 1)
            nc.scalar.wait_ge(trn.h, trn.v)
            evc.inc(nc.scalar.copy(
                out=stage[b][:, g * 4:(g + 1) * 4, :].rearrange("p c q -> p (c q)"),
                in_=bank[:]), 1)
        nc.gpsimd.wait_ge(evc.h, evc.v)
        _dma(nc.gpsimd,
             xn[:, :, tt * 128:(tt + 1) * 128].rearrange("c p q -> p c q"),
             stage[b][:], stx)
    nc.sync.wait_ge(stx.h, stx.v)
    nc.gpsimd.wait_ge(stx.h, stx.v)


def emit_qkv(C, xn, Wq, Wk, Wv, cq, sq, ck, sk, Qo, Ko, Vo):
    nc = C.nc
    xn_sb = C.carve([128, CD, T], BF16)
    tabs = {n: C.carve([128, T], F32) for n in ("cq", "sq", "ck", "sk")}
    wbuf = [C.carve([128, CD, 128], BF16) for _ in range(2)]
    rp1 = [C.carve([128, 512], F32) for _ in range(2)]
    prt = [C.carve([128, 512], F32) for _ in range(2)]
    rp2 = [C.carve([128, 512], F32) for _ in range(2)]
    qstage = [C.carve([128, 512], BF16) for _ in range(2)]
    vf = C.carve([128, T], F32)
    idtf = C.carve([128, 128], F32)
    vts = [C.carve([128, 4, 128], BF16) for _ in range(2)]
    ps = [C.psum(F32) for _ in range(8)]
    ldc, wld, mm, evq, pse, stq = (C.s[n] for n in
                                   ("ld1", "wld", "mm", "x1", "x2", "st1"))
    wld0, evq0, stq0 = wld.v, evq.v, stq.v

    _dma(nc.sync, xn_sb[:], xn[:].rearrange("c p t -> p c t"), ldc)
    for nm, src in (("cq", cq), ("sq", sq), ("ck", ck), ("sk", sk)):
        if src is not None:
            _dma(nc.sync, tabs[nm][:], src[:], ldc)
    _dma(nc.sync, idtf[:], C.inline("identf", IDENTF)[:], ldc)

    mmv = {}
    rope_i = 0
    nload = 0

    def qk_part(W, nOC, ctab, stab, OUT, base_oc, wslice):
        nonlocal rope_i, nload
        for j in range(nOC):
            oc = base_oc + j
            par = oc % 2
            banks = ps[par * 4:par * 4 + 4]
            if oc - 2 in mmv:
                nc.gpsimd.wait_ge(mm.h, mmv[oc - 2])
            _dma(nc.gpsimd, wbuf[par][:],
                 W[:, wslice + j * 128:wslice + (j + 1) * 128]
                 .rearrange("(c p) o -> p c o", p=128), wld)
            nload += 1
            nc.tensor.wait_ge(wld.h, wld0 + 16 * nload)
            if oc == 0:
                nc.tensor.wait_ge(ldc.h, ldc.v)
            if oc >= 2:
                nc.tensor.wait_ge(evq.h, evq0 + 4 * (oc - 1))
            for c in range(CD):
                for t4 in range(4):
                    ins = nc.tensor.matmul(banks[t4][:], wbuf[par][:, c, :],
                                           xn_sb[:, c, t4 * 512:(t4 + 1) * 512],
                                           start=(c == 0), stop=(c == CD - 1))
            mm.inc(ins, 1)
            mmv[oc] = mm.v
            for t4 in range(4):
                rpar = rope_i % 2
                sl = slice(t4 * 512, (t4 + 1) * 512)
                nc.vector.wait_ge(mm.h, mmv[oc])
                if rope_i >= 2:
                    nc.vector.wait_ge(stq.h, stq0 + 16 * (rope_i - 1))
                nc.vector.tensor_copy(prt[rpar][0:64, :], banks[t4][64:128, :])
                nc.vector.tensor_copy(prt[rpar][64:128, :], banks[t4][0:64, :])
                nc.vector.tensor_tensor(out=rp1[rpar][:], in0=banks[t4][:],
                                        in1=ctab[:, sl], op=ALU.mult)
                evq.inc(nc.vector.tensor_tensor(out=rp2[rpar][:], in0=prt[rpar][:],
                                                in1=stab[:, sl], op=ALU.mult), 1)
                pse.inc(nc.vector.tensor_tensor(out=qstage[rpar][:], in0=rp1[rpar][:],
                                                in1=rp2[rpar][:], op=ALU.add), 1)
                nc.gpsimd.wait_ge(pse.h, pse.v)
                _dma(nc.gpsimd, OUT[j, :, sl], qstage[rpar][:], stq)
                rope_i += 1

    nq = 0
    if Qo is not None:
        qk_part(Wq, HQ, tabs["cq"], tabs["sq"], Qo, 0, 0)
        nq = HQ
    qk_part(Wk, HKV, tabs["ck"], tabs["sk"], Ko, nq, 0)

    vstore = {}
    vevac = {}
    for j in range(HKV):
        oc = nq + HKV + j
        par = oc % 2
        banks = ps[par * 4:par * 4 + 4]
        if oc - 2 in mmv:
            nc.gpsimd.wait_ge(mm.h, mmv[oc - 2])
        _dma(nc.gpsimd, wbuf[par][:],
             Wv[:, j * 128:(j + 1) * 128].rearrange("(c p) o -> p c o", p=128), wld)
        nload += 1
        nc.tensor.wait_ge(wld.h, wld0 + 16 * nload)
        if j < 2:
            nc.tensor.wait_ge(evq.h, evq0 + 4 * (nq + HKV))
        if j >= 2:
            nc.tensor.wait_ge(evq.h, vevac[(j - 2, 1)])
        for c in range(CD):
            for t4 in range(4):
                ins = nc.tensor.matmul(banks[t4][:], wbuf[par][:, c, :],
                                       xn_sb[:, c, t4 * 512:(t4 + 1) * 512],
                                       start=(c == 0), stop=(c == CD - 1))
        mm.inc(ins, 1)
        mmv[oc] = mm.v
        nc.scalar.wait_ge(mm.h, mmv[oc])
        for t4 in range(4):
            evq.inc(nc.scalar.copy(out=vf[:, t4 * 512:(t4 + 1) * 512],
                                   in_=banks[t4][:]), 1)
        nc.tensor.wait_ge(evq.h, evq.v)
        for g in range(4):
            slot = g % 2
            tb = banks[slot]
            if g >= 2:
                nc.tensor.wait_ge(evq.h, vevac[(j, slot)])
            for t4 in range(4):
                tt = g * 4 + t4
                ins = nc.tensor.transpose(tb[:, t4 * 128:(t4 + 1) * 128],
                                          vf[:, tt * 128:(tt + 1) * 128], idtf[:])
            mm.inc(ins, 1)
            nc.vector.wait_ge(mm.h, mm.v)
            if slot in vstore:
                nc.vector.wait_ge(stq.h, vstore[slot])
            evq.inc(nc.vector.tensor_copy(
                vts[slot][:].rearrange("p a d -> p (a d)"), tb[:]), 1)
            vevac[(j, slot)] = evq.v
            nc.gpsimd.wait_ge(evq.h, evq.v)
            _dma(nc.gpsimd,
                 Vo[g * 512:(g + 1) * 512, j * 128:(j + 1) * 128]
                 .rearrange("(a p) d -> p a d", p=128),
                 vts[slot][:], stq)
            vstore[slot] = stq.v
    nc.sync.wait_ge(stq.h, stq.v)
    nc.gpsimd.wait_ge(stq.h, stq.v)


def emit_attn(C, Qi, Kown, Kpart, Vown, Vpart, Mi, Co):
    nc = C.nc
    K_sb = C.carve([128, HKV, TG], BF16)
    V_sb = C.carve([128, TG // 128, HKV * DH], BF16)
    M_sb = [C.carve([128, 512], BF16) for _ in range(2)]
    idt = C.carve([128, 128], BF16)
    qa = C.carve([128, HQ, 128], BF16)
    S_sb = C.carve([128, TG], F32)
    Pb = [C.carve([128, TG], BF16) for _ in range(2)]
    PT = C.carve([128, TG // 128, 4, 128], BF16)
    ctst = [C.carve([128, 4, 128], BF16) for _ in range(2)]
    st = C.carve([128, 32], F32)
    psS = [C.psum(F32) for _ in range(3)]
    psT = [C.psum(BF16) for _ in range(2)]
    psC = [C.psum(F32) for _ in range(2)]
    ldc, ldq, mm, dmx, aev, pml, stc = (C.s[n] for n in
                                        ("ld1", "ld2", "mm", "dve", "acs", "pml", "st1"))

    for h in range(HKV):
        for s_, Ksrc in ((0, Kown), (1, Kpart)):
            _dma(nc.sync,
                 K_sb[:, h, :].rearrange("p (u s q) -> p u s q", s=2, q=128)[:, :, s_, :],
                 Ksrc[h].rearrange("p (u q) -> p u q", q=128), ldc)
    for s_, Vsrc in ((0, Vown), (1, Vpart)):
        _dma(nc.sync,
             V_sb.rearrange("p (u s) d -> p u s d", s=2)[:, :, s_, :],
             Vsrc[:].rearrange("(u p) d -> p u d", p=128), ldc)
    _dma(nc.sync, idt[:], C.inline("identb", IDENT)[:], ldc)

    negmax = [st[:, 16 + i:17 + i] for i in range(2)]
    lsum = [st[:, 18 + i:19 + i] for i in range(2)]
    linv = [st[:, 20 + i:21 + i] for i in range(2)]
    linv2 = [st[:, 22 + i:23 + i] for i in range(2)]

    ctx_store = {}
    qtile_mm_done = {}
    bank_free = [None, None, None]
    ptbank_free = [None, None]
    ctx_evac = {}
    first = [True]
    prev_exp = [None]
    pb_free = {0: None, 1: None}
    pending = [None]

    def s_stage(ti, kvh, h4, nk5, hpar):
        qh = kvh * 4 + h4
        for kt in range(nk5):
            bk = kt % 3
            bank = psS[bk]
            if bank_free[bk] is not None:
                s_, v_ = bank_free[bk]
                nc.tensor.wait_ge(s_.h, v_)
            ins = nc.tensor.matmul(bank[:, :], qa[:, qh, :],
                                   K_sb[:, kvh, kt * 512:(kt + 1) * 512],
                                   start=True, stop=True)
            mm.inc(ins, 1)
            sl = slice(kt * 512, (kt + 1) * 512)
            if kt == nk5 - 1:
                nc.vector.wait_ge(mm.h, mm.v)
                if prev_exp[0] is not None:
                    nc.vector.wait_ge(aev.h, prev_exp[0])
                dmx.inc(nc.vector.scalar_tensor_tensor(
                    out=S_sb[:, sl], in0=bank[:, :], scalar=1.0,
                    in1=M_sb[ti % 2][:], op0=ALU.mult, op1=ALU.add), 1)
                bank_free[bk] = (dmx, dmx.v)
            else:
                nc.scalar.wait_ge(mm.h, mm.v)
                if prev_exp[0] is not None:
                    pass
                aev.inc(nc.scalar.copy(out=S_sb[:, sl], in_=bank[:, :]), 1)
                bank_free[bk] = (aev, aev.v)
        nc.vector.wait_ge(aev.h, aev.v)
        dmx.inc(nc.vector.tensor_reduce(negmax[hpar], S_sb[:, 0:nk5 * 512], AX,
                                        ALU.max, negate=True), 1)
        nc.scalar.wait_ge(dmx.h, dmx.v)
        if pb_free[hpar] is not None:
            nc.scalar.wait_ge(mm.h, pb_free[hpar])
        aev.inc(nc.scalar.activation(
            out=Pb[hpar][:, 0:nk5 * 512], in_=S_sb[:, 0:nk5 * 512], func=AF.Exp,
            bias=negmax[hpar], accum_out=lsum[hpar]), 1)
        prev_exp[0] = aev.v
        nc.vector.wait_ge(aev.h, aev.v)
        dmx.inc(nc.vector.reciprocal(linv[hpar], lsum[hpar]), 1)
        nc.gpsimd.wait_ge(dmx.h, dmx.v)
        pml.inc(nc.gpsimd.tensor_copy(linv2[hpar], linv[hpar]), 1)
        nc.vector.wait_ge(pml.h, pml.v)
        dmx.inc(nc.vector.tensor_scalar(
            out=Pb[hpar][:, 0:nk5 * 512], in0=Pb[hpar][:, 0:nk5 * 512],
            scalar1=linv2[hpar], scalar2=None, op0=ALU.mult), 1)
        return dmx.v

    def t_stage(h4, nk1, hpar, pscale_v):
        nc.tensor.wait_ge(dmx.h, pscale_v)
        for g in range((nk1 + 3) // 4):
            bk = g % 2
            tb = psT[bk]
            if ptbank_free[bk] is not None:
                nc.tensor.wait_ge(dmx.h, ptbank_free[bk])
            nkk = min(4, nk1 - g * 4)
            for j in range(nkk):
                kc = g * 4 + j
                ins = nc.tensor.transpose(
                    tb[:, j * 128:(j + 1) * 128],
                    Pb[hpar][:, kc * 128:(kc + 1) * 128], idt[:])
            mm.inc(ins, 1)
            nc.vector.wait_ge(mm.h, mm.v)
            dmx.inc(nc.vector.tensor_copy(
                PT[:, g * 4:g * 4 + nkk, h4, :],
                tb[:].rearrange("p (a q) -> p a q", a=4)[:, 0:nkk, :]), 1)
            ptbank_free[bk] = dmx.v
        pb_free[hpar] = mm.v

    for ti in range(NTT):
        lq0 = ti * 128
        qpar = ti % 2
        nk5 = ti // 2 + 1
        nk1 = 4 * nk5
        if ti >= 1:
            nc.sync.wait_ge(mm.h, qtile_mm_done[ti - 1])
        _dma(nc.sync, qa[:], Qi[:, :, lq0:lq0 + 128].rearrange("h p q -> p h q"), ldq)
        _dma(nc.sync, M_sb[qpar][:], Mi[ti], ldq)
        nc.tensor.wait_ge(ldq.h, ldq.v)
        if first[0]:
            nc.tensor.wait_ge(ldc.h, ldc.v)
            first[0] = False
        for kvh in range(HKV):
            cpar = (ti * HKV + kvh) % 2
            pv = {}
            pv[0] = s_stage(ti, kvh, 0, nk5, 0)
            pv[1] = s_stage(ti, kvh, 1, nk5, 1)
            if pending[0] is not None:
                pending[0]()
                pending[0] = None
            t_stage(0, nk1, 0, pv[0])
            pv[2] = s_stage(ti, kvh, 2, nk5, 0)
            t_stage(1, nk1, 1, pv[1])
            pv[3] = s_stage(ti, kvh, 3, nk5, 1)
            t_stage(2, nk1, 0, pv[2])
            t_stage(3, nk1, 1, pv[3])
            pt_ready = dmx.v

            def mk_pv(ti_=ti, kvh_=kvh, nk1_=nk1, cpar_=cpar, lq0_=lq0,
                      ptv_=pt_ready):
                def do():
                    nc.tensor.wait_ge(dmx.h, ptv_)
                    idx = ti_ * HKV + kvh_
                    if idx >= 2:
                        nc.tensor.wait_ge(aev.h, ctx_evac[idx - 2])
                    for kc in range(nk1_):
                        ins = nc.tensor.matmul(
                            psC[cpar_][:, :], V_sb[:, kc, kvh_ * 128:(kvh_ + 1) * 128],
                            PT[:, kc, :, :].rearrange("p a q -> p (a q)"),
                            start=(kc == 0), stop=(kc == nk1_ - 1))
                    mm.inc(ins, 1)
                    nc.scalar.wait_ge(mm.h, mm.v)
                    if cpar_ in ctx_store:
                        nc.scalar.wait_ge(stc.h, ctx_store[cpar_])
                    aev.inc(nc.scalar.copy(out=ctst[cpar_][:].rearrange("p a q -> p (a q)"),
                                           in_=psC[cpar_][:, :]), 1)
                    ctx_evac[idx] = aev.v
                    nc.gpsimd.wait_ge(aev.h, aev.v)
                    _dma(nc.gpsimd,
                         Co[kvh_ * 4:(kvh_ + 1) * 4, :, lq0_:lq0_ + 128]
                         .rearrange("h p q -> p h q"),
                         ctst[cpar_][:], stc)
                    ctx_store[cpar_] = stc.v
                return do
            pending[0] = mk_pv()
        qtile_mm_done[ti] = mm.v
    if pending[0] is not None:
        pending[0]()
        pending[0] = None
    nc.sync.wait_ge(stc.h, stc.v)
    nc.gpsimd.wait_ge(stc.h, stc.v)


def emit_combine(C, kind, A, W, xprev, xnew, xn2):
    nc = C.nc
    nK = CD if kind == "wo" else HB
    nKh = nK // 2
    A_sb = C.carve([128, nK, 512], BF16)
    wbuf = [C.carve([128, nKh, 128], BF16) for _ in range(2)]
    yf1 = C.carve([128, 512], F32)
    yf = [yf1, yf1]
    y4 = C.carve([128, 4, CD, 128], F32)
    xp = C.carve([128, D], F32)
    xnb = C.carve([128, D], BF16)
    st4 = C.carve([128, 64], F32)
    stage = C.carve([128, CD, 128], BF16)
    idt = C.carve([128, 128], BF16)
    idtf = C.carve([128, 128], F32)
    psY = [C.psum(F32) for _ in range(2)]
    psT = [C.psum(F32) for _ in range(2)]
    psB = [C.psum(BF16) for _ in range(2)]
    lda, wld, mm, aev, dev, acs, stx = (C.s[n] for n in
                                        ("ld1", "wld", "mm", "acs", "dve", "x1", "st1"))
    wld0 = wld.v
    _dma(nc.sync, idt[:], C.inline("identb", IDENT)[:], lda)
    _dma(nc.sync, idtf[:], C.inline("identf", IDENTF)[:], lda)

    nwld = 0
    mm_oc = {}
    yf_ev = {}
    trn_ev = {}
    dev_ev = {}
    xnb_ev = {}
    xnb_tr = {}
    stage_store = {}
    xp_store = {}
    for ts in range(4):
        tsl = slice(ts * 512, (ts + 1) * 512)
        if ts >= 1:
            nc.sync.wait_ge(mm.h, mm_oc[(ts - 1, CD - 1)])
        _dma(nc.sync, A_sb[:], A[:, :, tsl].rearrange("c p t -> p c t"), lda)
        for oc in range(CD):
            par = oc % 2
            ldq_eng = nc.gpsimd if kind == "wo" else nc.sync
            for half in range(2):
                buf = wbuf[half]
                if (ts, oc - 1) in mm_oc:
                    ldq_eng.wait_ge(mm.h, mm_oc[(ts, oc - 1)])
                elif ts >= 1 and oc == 0:
                    ldq_eng.wait_ge(mm.h, mm_oc[(ts - 1, CD - 1)])
                if kind == "wo":
                    src = W[:, oc * 128:(oc + 1) * 128] \
                        .rearrange("(c p) o -> p c o", p=128)[:, half * nKh:(half + 1) * nKh, :]
                    _dma(nc.gpsimd, buf[:], src, wld)
                else:
                    src = W[oc, half * nKh:(half + 1) * nKh, :, :].rearrange("c p o -> p c o")
                    _dma(nc.sync, buf[:], src, wld)
                nwld += 1
            nc.tensor.wait_ge(wld.h, wld0 + 16 * nwld)
            nc.tensor.wait_ge(lda.h, lda.v)
            if (ts, oc - 2) in yf_ev:
                nc.tensor.wait_ge(aev.h, yf_ev[(ts, oc - 2)])
            elif ts >= 1 and oc < 2:
                nc.tensor.wait_ge(aev.h, yf_ev[(ts - 1, CD - 2 + oc)])
            for c in range(nK):
                ins = nc.tensor.matmul(psY[par][:], wbuf[c // nKh][:, c % nKh, :],
                                       A_sb[:, c, :],
                                       start=(c == 0), stop=(c == nK - 1))
            mm.inc(ins, 1)
            mm_oc[(ts, oc)] = mm.v
            nc.scalar.wait_ge(mm.h, mm.v)
            if (ts, oc - 1) in trn_ev:
                nc.scalar.wait_ge(mm.h, trn_ev[(ts, oc - 1)])
            elif ts >= 1 and oc == 0:
                nc.scalar.wait_ge(mm.h, trn_ev[(ts - 1, CD - 1)])
            aev.inc(nc.scalar.copy(out=yf[par][:], in_=psY[par][:]), 1)
            yf_ev[(ts, oc)] = aev.v
            nc.tensor.wait_ge(aev.h, aev.v)
            if (ts, oc - 2) in dev_ev:
                nc.tensor.wait_ge(dev.h, dev_ev[(ts, oc - 2)])
            elif ts >= 1 and oc < 2:
                nc.tensor.wait_ge(dev.h, dev_ev[(ts - 1, CD - 2 + oc)])
            for a in range(4):
                ins = nc.tensor.transpose(psT[par][:, a * 128:(a + 1) * 128],
                                          yf[par][:, a * 128:(a + 1) * 128], idtf[:])
            mm.inc(ins, 1)
            trn_ev[(ts, oc)] = mm.v
            nc.vector.wait_ge(mm.h, mm.v)
            dev.inc(nc.vector.tensor_copy(y4[:, :, oc, :], psT[par][:]
                                          .rearrange("p (a q) -> p a q", a=4)), 1)
            dev_ev[(ts, oc)] = dev.v
        yav = [y4[:, a, :, :].rearrange("p c q -> p (c q)") for a in range(4)]
        nc.vector.wait_ge(dev.h, dev.v)
        parts1 = []
        for a in range(4):
            m_, r_, last_, fns = emit_ln_stats(nc, yav[a], xnb[:], st4[:, 16 * a:16 * a + 8])
            parts1.append((m_, r_, fns))
        dev.inc(last_, 1)
        nc.scalar.wait_ge(dev.h, dev.v)
        for a in range(4):
            ins_ = parts1[a][2][0]()
        acs.inc(ins_, 1)
        nc.vector.wait_ge(acs.h, acs.v)
        for a in range(4):
            ins_ = parts1[a][2][1]()
        dev.inc(ins_, 1)
        nc.scalar.wait_ge(dev.h, dev.v)
        for a in range(4):
            ins_ = parts1[a][2][2]()
        acs.inc(ins_, 1)
        nc.vector.wait_ge(acs.h, acs.v)
        for a in range(4):
            tt = ts * 4 + a
            m_, r_, _ = parts1[a]
            nc.vector.tensor_scalar(out=yav[a], in0=yav[a], scalar1=m_, scalar2=r_,
                                    op0=ALU.subtract, op1=ALU.mult)
            if tt >= 1:
                nc.sync.wait_ge(dev.h, dev.v)
                nc.sync.wait_ge(stx.h, xp_store[tt - 1])
            _dma(nc.sync, xp[:], xprev[tt * 128:(tt + 1) * 128, :], lda)
            nc.vector.wait_ge(lda.h, lda.v)
            dev.inc(nc.vector.tensor_tensor(out=yav[a], in0=xp[:], in1=yav[a],
                                            op=ALU.add), 1)
            xp_store[tt] = stx.v + 16
            nc.gpsimd.wait_ge(dev.h, dev.v)
            _dma(nc.gpsimd, xnew[tt * 128:(tt + 1) * 128, :], yav[a], stx)
        if xn2 is not None:
            parts2 = []
            for a in range(4):
                m_, r_, last_, fns = emit_ln_stats(nc, yav[a], xnb[:],
                                                   st4[:, 16 * a + 8:16 * a + 16])
                parts2.append((m_, r_, fns))
            dev.inc(last_, 1)
            nc.scalar.wait_ge(dev.h, dev.v)
            for a in range(4):
                ins_ = parts2[a][2][0]()
            acs.inc(ins_, 1)
            nc.vector.wait_ge(acs.h, acs.v)
            for a in range(4):
                ins_ = parts2[a][2][1]()
            dev.inc(ins_, 1)
            nc.scalar.wait_ge(dev.h, dev.v)
            for a in range(4):
                ins_ = parts2[a][2][2]()
            acs.inc(ins_, 1)
            for a in range(4):
                tt = ts * 4 + a
                m_, r_, _ = parts2[a]
                nc.vector.wait_ge(acs.h, acs.v)
                if tt >= 1 and (tt - 1) in xnb_tr:
                    nc.vector.wait_ge(mm.h, xnb_tr[tt - 1])
                dev.inc(nc.vector.tensor_scalar(out=xnb[:], in0=yav[a], scalar1=m_,
                                                scalar2=r_, op0=ALU.subtract,
                                                op1=ALU.mult), 1)
                nc.tensor.wait_ge(dev.h, dev.v)
                stv = stage_store.get(tt - 1) if tt >= 1 else None
                if stv is not None:
                    nc.vector.wait_ge(stx.h, stv)
                for g in range(8):
                    bank = psB[g % 2]
                    need = None
                    if g >= 2:
                        need = xnb_ev[(tt, g - 2)]
                    elif tt >= 1:
                        need = xnb_ev[(tt - 1, 6 + g)]
                    if need is not None:
                        nc.tensor.wait_ge(dev.h, need)
                    for j in range(4):
                        c = g * 4 + j
                        ins = nc.tensor.transpose(bank[:, j * 128:(j + 1) * 128],
                                                  xnb[:, c * 128:(c + 1) * 128], idt[:])
                    mm.inc(ins, 1)
                    nc.vector.wait_ge(mm.h, mm.v)
                    dev.inc(nc.vector.tensor_copy(
                        stage[:, g * 4:(g + 1) * 4, :].rearrange("p c q -> p (c q)"),
                        bank[:]), 1)
                    xnb_ev[(tt, g)] = dev.v
                xnb_tr[tt] = mm.v
                nc.gpsimd.wait_ge(dev.h, dev.v)
                _dma(nc.gpsimd, xn2[:, :, tt * 128:(tt + 1) * 128].rearrange("c p q -> p c q"),
                     stage[:], stx)
                stage_store[tt] = stx.v
    nc.sync.wait_ge(stx.h, stx.v)
    nc.gpsimd.wait_ge(stx.h, stx.v)


def emit_dequant(nc, eng, pk, s_sb, Wb, mi, ni, ngrp):
    se = s_sb.rearrange("p (q two) -> p q two", two=2)
    eng.tensor_scalar(out=mi, in0=pk, scalar1=4, scalar2=None,
                      op0=ALU.arith_shift_right)
    eng.tensor_scalar(out=ni, in0=pk, scalar1=15, scalar2=8,
                      op0=ALU.bitwise_and, op1=ALU.bitwise_xor)
    eng.tensor_scalar(out=ni, in0=ni, scalar1=-8, scalar2=None, op0=ALU.add)
    eng.tensor_tensor(out=Wb[:, :, 0, :],
                      in0=mi.rearrange("p (q j) -> p q j", j=64),
                      in1=se[:, :, 0:1].broadcast_to([128, ngrp, 64]), op=ALU.mult)
    return eng.tensor_tensor(out=Wb[:, :, 1, :],
                             in0=ni.rearrange("p (q j) -> p q j", j=64),
                             in1=se[:, :, 1:2].broadcast_to([128, ngrp, 64]),
                             op=ALU.mult)


def emit_ffn0up(C, xn, w1, s1, w3, s3, ha):
    nc = C.nc
    xn_sb = C.carve([128, CD, T], BF16)
    pk1 = C.carve([128, CD * 64], I32)
    pk3 = C.carve([128, CD * 64], I32)
    mi1 = C.carve([128, CD * 64], I32)
    s1t = C.carve([128, 64], F32)
    s3t = C.carve([128, 64], F32)
    W1b = C.carve([128, CD, 2, 64], BF16)
    W3b = C.carve([128, CD, 2, 64], BF16)
    W1T = [C.carve([128, CD, 128], BF16) for _ in range(2)]
    W3T = [C.carve([128, CD, 128], BF16) for _ in range(2)]
    sil = C.carve([128, 512], BF16)
    hst = [C.carve([128, 512], BF16) for _ in range(2)]
    idt = C.carve([128, 128], BF16)
    psF = [C.psum(F32) for _ in range(4)]
    psX = [C.psum(BF16) for _ in range(2)]
    ldc, ldp, dqq, trn, ev1, ev3, mmf, sth = (C.s[n] for n in
        ("ld1", "ld2", "x1", "mm", "acs", "dve", "x2", "st1"))

    _dma(nc.sync, xn_sb[:], xn[:].rearrange("c p t -> p c t"), ldc)
    _dma(nc.sync, idt[:], C.inline("identb", IDENT)[:], ldc)

    dq_done = {}
    mmfv = {}
    trn_done = {}
    bank_free = {0: None, 1: None}
    psf_free = {0: None, 1: None}
    sthv = {}
    last_mul = None
    for hb in range(HB):
        par = hb % 2
        if hb >= 1:
            nc.sync.wait_ge(dqq.h, dq_done[hb - 1])
        _dma(nc.sync, pk1[:].rearrange("p (a j) -> p a j", j=64),
             w1[hb * 4096:(hb + 1) * 4096, :].rearrange("(p a) j -> p a j", p=128), ldp)
        _dma(nc.sync, pk3[:].rearrange("p (a j) -> p a j", j=64),
             w3[hb * 4096:(hb + 1) * 4096, :].rearrange("(p a) j -> p a j", p=128), ldp)
        _dma(nc.sync, s1t[:], s1.rearrange("(h g) -> h g", g=64)[hb * 128:(hb + 1) * 128, :], ldp)
        _dma(nc.sync, s3t[:], s3.rearrange("(h g) -> h g", g=64)[hb * 128:(hb + 1) * 128, :], ldp)
        nc.vector.wait_ge(ldp.h, ldp.v)
        if hb >= 1:
            nc.vector.wait_ge(trn.h, trn_done[hb - 1])
        dqq.inc(emit_dequant(nc, nc.vector, pk1[:], s1t[:], W1b[:], mi1[:], pk1[:], CD), 1)
        dq1_v = dqq.v
        dqq.inc(emit_dequant(nc, nc.vector, pk3[:], s3t[:], W3b[:], mi1[:], pk3[:], CD), 1)
        dq_done[hb] = dqq.v
        if hb == 0:
            nc.tensor.wait_ge(ldc.h, ldc.v)
        if hb >= 2:
            nc.tensor.wait_ge(mmf.h, mmfv[hb - 2])
        for (W_b, W_T, on_act, dqv_) in ((W1b, W1T[par], True, dq1_v),
                                         (W3b, W3T[par], False, dq_done[hb])):
            nc.tensor.wait_ge(dqq.h, dqv_)
            for g in range(8):
                bk = g % 2
                bank = psX[bk]
                if bank_free[bk] is not None:
                    s_, v_ = bank_free[bk]
                    nc.tensor.wait_ge(s_.h, v_)
                for j in range(4):
                    c = g * 4 + j
                    ins = nc.tensor.transpose(
                        bank[:, j * 128:(j + 1) * 128],
                        W_b[:].rearrange("p a two j -> p (a two j)")[:, c * 128:(c + 1) * 128],
                        idt[:])
                trn.inc(ins, 1)
                dst = W_T[:, g * 4:(g + 1) * 4, :].rearrange("p c q -> p (c q)")
                if on_act:
                    nc.scalar.wait_ge(trn.h, trn.v)
                    ev1.inc(nc.scalar.copy(out=dst, in_=bank[:]), 1)
                    bank_free[bk] = (ev1, ev1.v)
                else:
                    nc.vector.wait_ge(trn.h, trn.v)
                    ev3.inc(nc.vector.tensor_copy(dst, bank[:]), 1)
                    bank_free[bk] = (ev3, ev3.v)
        trn_done[hb] = trn.v
        w_ready = (ev1.v, ev3.v)
        nc.tensor.wait_ge(ev1.h, w_ready[0])
        nc.tensor.wait_ge(ev3.h, w_ready[1])
        for t4 in range(4):
            tsl = slice(t4 * 512, (t4 + 1) * 512)
            pp = t4 % 2
            pa, pb = psF[2 * pp], psF[2 * pp + 1]
            if psf_free[pp] is not None:
                nc.tensor.wait_ge(ev1.h, psf_free[pp][0])
                nc.tensor.wait_ge(ev3.h, psf_free[pp][1])
            for c in range(CD):
                nc.tensor.matmul(pa[:], W1T[par][:, c, :], xn_sb[:, c, tsl],
                                 start=(c == 0), stop=(c == CD - 1))
            for c in range(CD):
                ins = nc.tensor.matmul(pb[:], W3T[par][:, c, :], xn_sb[:, c, tsl],
                                       start=(c == 0), stop=(c == CD - 1))
            mmf.inc(ins, 1)
            nc.scalar.wait_ge(mmf.h, mmf.v)
            if last_mul is not None:
                nc.scalar.wait_ge(ev3.h, last_mul)
            ev1.inc(nc.scalar.activation(out=sil[:], in_=pa[:], func=AF.Silu), 1)
            silu_v = ev1.v
            nc.vector.wait_ge(ev1.h, silu_v)
            it = hb * 4 + t4
            if it >= 2:
                nc.vector.wait_ge(sth.h, sthv[it - 2])
            ev3.inc(nc.vector.tensor_tensor(out=hst[pp][:], in0=sil[:],
                                            in1=pb[:], op=ALU.mult), 1)
            last_mul = ev3.v
            psf_free[pp] = (silu_v, last_mul)
            nc.gpsimd.wait_ge(ev3.h, last_mul)
            _dma(nc.gpsimd, ha[hb, :, tsl], hst[pp][:], sth)
            sthv[it] = sth.v
        mmfv[hb] = mmf.v
    nc.sync.wait_ge(sth.h, sth.v)
    nc.gpsimd.wait_ge(sth.h, sth.v)


def emit_w2t(C, w2, s2, W2T):
    nc = C.nc
    NG = HB
    pk = [C.carve([128, NG * 64], I32) for _ in range(2)]
    s2t = [C.carve([128, 2 * NG], F32) for _ in range(2)]
    mi = C.carve([128, NG * 64], I32)
    ni = C.carve([128, NG * 64], I32)
    Wb = C.carve([128, NG, 2, 64], BF16)
    stage = [C.carve([128, NG, 128], BF16) for _ in range(2)]
    idt = C.carve([128, 128], BF16)
    psX = [C.psum(BF16) for _ in range(2)]
    ldc, ldp, dqd, trn, evc, stw = (C.s[n] for n in
                                    ("ld1", "ld2", "x1", "mm", "acs", "st1"))
    _dma(nc.sync, idt[:], C.inline("identb", IDENT)[:], ldc)
    dqv = {}
    trv = {}
    stv = {}
    bank_free = {0: None, 1: None}
    for ob in range(CD):
        par = ob % 2
        if ob >= 2:
            nc.sync.wait_ge(dqd.h, dqv[ob - 2])
        _dma(nc.sync, pk[par][:].rearrange("p (a j) -> p a j", j=64),
             w2[ob * 11008:(ob + 1) * 11008, :].rearrange("(p a) j -> p a j", p=128), ldp)
        _dma(nc.sync, s2t[par][:],
             s2.rearrange("(h g) -> h g", g=2 * NG)[ob * 128:(ob + 1) * 128, :], ldp)
        nc.vector.wait_ge(ldp.h, ldp.v)
        if ob >= 1:
            nc.vector.wait_ge(trn.h, trv[ob - 1])
        dqd.inc(emit_dequant(nc, nc.vector, pk[par][:], s2t[par][:], Wb[:],
                             mi[:], ni[:], NG), 1)
        dqv[ob] = dqd.v
        nc.tensor.wait_ge(dqd.h, dqd.v)
        if ob == 0:
            nc.tensor.wait_ge(ldc.h, ldc.v)
        for g in range((NG + 3) // 4):
            bk = g % 2
            bank = psX[bk]
            if bank_free[bk] is not None:
                nc.tensor.wait_ge(evc.h, bank_free[bk])
            nkk = min(4, NG - g * 4)
            for j in range(nkk):
                c = g * 4 + j
                ins = nc.tensor.transpose(
                    bank[:, j * 128:(j + 1) * 128],
                    Wb[:].rearrange("p a two j -> p (a two j)")[:, c * 128:(c + 1) * 128],
                    idt[:])
            trn.inc(ins, 1)
            nc.scalar.wait_ge(trn.h, trn.v)
            if g == 0 and ob >= 2:
                nc.scalar.wait_ge(stw.h, stv[ob - 2])
            evc.inc(nc.scalar.copy(
                out=stage[par][:, g * 4:g * 4 + nkk, :].rearrange("p c q -> p (c q)"),
                in_=bank[:, 0:nkk * 128]), 1)
            bank_free[bk] = evc.v
        trv[ob] = trn.v
        nc.gpsimd.wait_ge(evc.h, evc.v)
        _dma(nc.gpsimd, W2T[ob].rearrange("c p o -> p c o"), stage[par][:], stw)
        stv[ob] = stw.v
    nc.sync.wait_ge(stw.h, stw.v)
    nc.gpsimd.wait_ge(stw.h, stw.v)


def emit_ffn1(C, xn, Wup, gateT, ebias, AT, Ball, xprev, xnew):
    nc = C.nc
    xn_ts = [C.carve([128, CD, 512], BF16) for _ in range(2)]
    gA = C.carve([128, CD, 36], BF16)
    ob1 = C.carve([1, 132], BF16)
    Ball_sb = C.carve([32, D], BF16)
    zsb = C.carve([128, 16, 32], F32)
    lgsb = C.carve([128, 16, 4], F32)
    lg2 = C.carve([128, 16, 4], F32)
    mx = C.carve([128, 16], F32)
    mx2 = C.carve([128, 16], F32)
    oh = C.carve([128, 16, 4], F32)
    zmb = C.carve([128, 16, 32], BF16)
    zmT = C.carve([32, T], BF16)
    wub = [C.carve([128, CD, 128], BF16) for _ in range(2)]
    wubT = C.carve([128, CD, 128], BF16)
    yf1 = C.carve([128, 512], F32)
    y4 = C.carve([128, 4, CD, 128], F32)
    xp = C.carve([128, D], F32)
    xnb = C.carve([128, D], BF16)
    st4 = C.carve([128, 64], F32)
    idtf = C.carve([128, 128], F32)
    idtb = C.carve([128, 128], BF16)
    psY = [C.psum(F32) for _ in range(2)]
    psT = [C.psum(F32) for _ in range(2)]
    psZ = C.psum(F32)
    psL = C.psum(F32)[:, 0:64]
    psB2 = [C.psum(BF16) for _ in range(2)]
    lda, wld, mm, dev, acs, pml, stx = (C.s[n] for n in
                                        ("ld1", "wld", "mm", "dve", "acs", "pml", "st1"))
    wld0 = wld.v

    _dma(nc.sync, idtf[:], C.inline("identf", IDENTF)[:], lda)
    _dma(nc.sync, idtb[:], C.inline("identb", IDENT)[:], lda)
    _dma(nc.gpsimd, ob1[:, 0:128], C.inline("onesb", np.ones((1, 128), dtype=BF))[:], lda)
    _dma(nc.gpsimd, ob1[:, 128:132], ebias[:], lda)
    _dma(nc.gpsimd, gA[:, :, 0:4], gateT[:].rearrange("(c p) e -> p c e", p=128), lda)
    _dma(nc.gpsimd, gA[:, :, 4:36], AT[:].rearrange("(c p) e -> p c e", p=128), lda)
    _dma(nc.gpsimd, Ball_sb[:], Ball[:], lda)

    pre_mm = {}
    for ts in range(4):
        tsl = slice(ts * 512, (ts + 1) * 512)
        buf = xn_ts[ts % 2]
        if ts >= 2:
            nc.sync.wait_ge(mm.h, pre_mm[ts - 2])
        _dma(nc.sync, buf[:], xn[:, :, tsl].rearrange("c p t -> p c t"), lda)
        nc.tensor.wait_ge(lda.h, lda.v)
        for tt4 in range(4):
            tt = ts * 4 + tt4
            sl = slice(tt4 * 128, (tt4 + 1) * 128)
            for c in range(CD):
                nc.tensor.matmul(psL[:, tt * 4:(tt + 1) * 4],
                                 buf[:, c, sl], gA[:, c, 0:4],
                                 start=(c == 0), stop=False)
            nc.tensor.matmul(psL[:, tt * 4:(tt + 1) * 4],
                             ob1[:, 0:128], ob1[:, 128:132],
                             start=False, stop=True)
            for c in range(CD):
                ins = nc.tensor.matmul(psZ[:, tt * 32:(tt + 1) * 32],
                                       buf[:, c, sl], gA[:, c, 4:36],
                                       start=(c == 0), stop=(c == CD - 1))
            mm.inc(ins, 1)
        pre_mm[ts] = mm.v
    nc.vector.wait_ge(mm.h, mm.v)
    nc.vector.tensor_copy(lgsb[:], psL[:].rearrange("p (t e) -> p t e", e=4))
    dev.inc(nc.vector.tensor_copy(zsb[:], psZ[:].rearrange("p (t e) -> p t e", e=32)), 1)
    nc.gpsimd.wait_ge(dev.h, dev.v)
    pml.inc(nc.gpsimd.tensor_copy(lg2[:], lgsb[:]), 1)
    nc.vector.wait_ge(pml.h, pml.v)
    dev.inc(nc.vector.tensor_reduce(mx[:].unsqueeze(2), lg2[:], AX, ALU.max), 1)
    nc.gpsimd.wait_ge(dev.h, dev.v)
    pml.inc(nc.gpsimd.tensor_copy(mx2[:], mx[:]), 1)
    nc.vector.wait_ge(pml.h, pml.v)
    dev.inc(nc.vector.tensor_tensor(
        out=oh[:], in0=lg2[:],
        in1=mx2[:].unsqueeze(2).broadcast_to([128, 16, 4]), op=ALU.is_ge), 1)
    nc.gpsimd.wait_ge(dev.h, dev.v)
    pml.inc(nc.gpsimd.tensor_tensor(
        out=zmb[:].rearrange("p t (e r) -> p t e r", e=4),
        in0=zsb[:].rearrange("p t (e r) -> p t e r", e=4),
        in1=oh[:].unsqueeze(3).broadcast_to([128, 16, 4, 8]), op=ALU.mult), 1)
    nc.tensor.wait_ge(pml.h, pml.v)
    zmt_ev = {}
    for g in range(4):
        bank = psB2[g % 2]
        if g >= 2:
            nc.tensor.wait_ge(dev.h, zmt_ev[g - 2])
        for j in range(4):
            tt = g * 4 + j
            ins = nc.tensor.transpose(bank[0:32, j * 128:(j + 1) * 128],
                                      zmb[:, tt, :], idtb[:])
        mm.inc(ins, 1)
        nc.vector.wait_ge(mm.h, mm.v)
        dev.inc(nc.vector.tensor_copy(zmT[:, g * 512:(g + 1) * 512],
                                      bank[0:32, 0:512]), 1)
        zmt_ev[g] = dev.v

    nwld = 0
    mm_oc = {}
    yf_ev = {}
    trn_ev = {}
    dev_ev = {}
    wup_tr = {}
    wupT_free = {0: zmt_ev.get(2), 1: zmt_ev.get(3)}
    xp_store1 = {}
    pre_done_mm = mm.v
    for ts in range(4):
        tsl = slice(ts * 512, (ts + 1) * 512)
        buf = xn_ts[ts % 2]
        if ts >= 2:
            nc.sync.wait_ge(mm.h, mm_oc[(ts - 1, CD - 1)])
        else:
            nc.sync.wait_ge(mm.h, pre_done_mm)
        _dma(nc.sync, buf[:], xn[:, :, tsl].rearrange("c p t -> p c t"), lda)
        for oc in range(CD):
            par = oc % 2
            wb = wub[par]
            if (ts, oc - 2) in wup_tr:
                nc.gpsimd.wait_ge(mm.h, wup_tr[(ts, oc - 2)])
            elif ts >= 1 and oc < 2:
                nc.gpsimd.wait_ge(mm.h, wup_tr[(ts - 1, CD - 2 + oc)])
            _dma(nc.gpsimd, wb[:],
                 Wup[oc * 128:(oc + 1) * 128, :].rearrange("o (c p) -> o c p", p=128),
                 wld)
            nwld += 1
            nc.tensor.wait_ge(wld.h, wld0 + 16 * nwld)
            nc.tensor.wait_ge(lda.h, lda.v)
            if (ts, oc - 1) in mm_oc:
                nc.tensor.wait_ge(mm.h, mm_oc[(ts, oc - 1)])
            elif ts >= 1 and oc == 0:
                nc.tensor.wait_ge(mm.h, mm_oc[(ts - 1, CD - 1)])
            for g in range(8):
                bk = g % 2
                bank = psB2[bk]
                if wupT_free.get(bk) is not None:
                    nc.tensor.wait_ge(dev.h, wupT_free[bk])
                for j in range(4):
                    c = g * 4 + j
                    ins = nc.tensor.transpose(bank[:, j * 128:(j + 1) * 128],
                                              wb[:, c, :], idtb[:])
                mm.inc(ins, 1)
                nc.vector.wait_ge(mm.h, mm.v)
                dev.inc(nc.vector.tensor_copy(
                    wubT[:, g * 4:(g + 1) * 4, :].rearrange("p c q -> p (c q)"),
                    bank[:]), 1)
                wupT_free[bk] = dev.v
            wup_tr[(ts, oc)] = mm.v
            nc.tensor.wait_ge(dev.h, dev.v)
            if (ts, oc - 2) in yf_ev:
                nc.tensor.wait_ge(acs.h, yf_ev[(ts, oc - 2)])
            elif ts >= 1 and oc < 2:
                nc.tensor.wait_ge(acs.h, yf_ev[(ts - 1, CD - 2 + oc)])
            for c in range(CD):
                nc.tensor.matmul(psY[par][:], wubT[:, c, :], buf[:, c, :],
                                 start=(c == 0), stop=False)
            ins = nc.tensor.matmul(psY[par][:], Ball_sb[0:32, oc * 128:(oc + 1) * 128],
                                   zmT[0:32, tsl], start=False, stop=True)
            mm.inc(ins, 1)
            mm_oc[(ts, oc)] = mm.v
            nc.scalar.wait_ge(mm.h, mm.v)
            if (ts, oc - 1) in trn_ev:
                nc.scalar.wait_ge(mm.h, trn_ev[(ts, oc - 1)])
            acs.inc(nc.scalar.copy(out=yf1[:], in_=psY[par][:]), 1)
            yf_ev[(ts, oc)] = acs.v
            nc.tensor.wait_ge(acs.h, acs.v)
            if (ts, oc - 2) in dev_ev:
                nc.tensor.wait_ge(dev.h, dev_ev[(ts, oc - 2)])
            for a in range(4):
                ins = nc.tensor.transpose(psT[par][:, a * 128:(a + 1) * 128],
                                          yf1[:, a * 128:(a + 1) * 128], idtf[:])
            mm.inc(ins, 1)
            trn_ev[(ts, oc)] = mm.v
            nc.vector.wait_ge(mm.h, mm.v)
            dev.inc(nc.vector.tensor_copy(y4[:, :, oc, :], psT[par][:]
                                          .rearrange("p (a q) -> p a q", a=4)), 1)
            dev_ev[(ts, oc)] = dev.v
        yav = [y4[:, a, :, :].rearrange("p c q -> p (c q)") for a in range(4)]
        nc.vector.wait_ge(dev.h, dev.v)
        parts1 = []
        for a in range(4):
            m_, r_, last_, fns = emit_ln_stats(nc, yav[a], xnb[:], st4[:, 16 * a:16 * a + 8])
            parts1.append((m_, r_, fns))
        dev.inc(last_, 1)
        nc.scalar.wait_ge(dev.h, dev.v)
        for a in range(4):
            ins_ = parts1[a][2][0]()
        acs.inc(ins_, 1)
        nc.vector.wait_ge(acs.h, acs.v)
        for a in range(4):
            ins_ = parts1[a][2][1]()
        dev.inc(ins_, 1)
        nc.scalar.wait_ge(dev.h, dev.v)
        for a in range(4):
            ins_ = parts1[a][2][2]()
        acs.inc(ins_, 1)
        nc.vector.wait_ge(acs.h, acs.v)
        for a in range(4):
            tt = ts * 4 + a
            m_, r_, _ = parts1[a]
            nc.vector.tensor_scalar(out=yav[a], in0=yav[a], scalar1=m_, scalar2=r_,
                                    op0=ALU.subtract, op1=ALU.mult)
            if tt >= 1:
                nc.sync.wait_ge(dev.h, dev.v)
                nc.sync.wait_ge(stx.h, xp_store1[tt - 1])
            _dma(nc.sync, xp[:], xprev[tt * 128:(tt + 1) * 128, :], lda)
            nc.vector.wait_ge(lda.h, lda.v)
            dev.inc(nc.vector.tensor_tensor(out=yav[a], in0=xp[:], in1=yav[a],
                                            op=ALU.add), 1)
            xp_store1[tt] = stx.v + 16
            nc.gpsimd.wait_ge(dev.h, dev.v)
            _dma(nc.gpsimd, xnew[tt * 128:(tt + 1) * 128, :], yav[a], stx)
    nc.sync.wait_ge(stx.h, stx.v)
    nc.gpsimd.wait_ge(stx.h, stx.v)



def _newC():
    C = Ctx()
    C.setup()
    C.section()
    return C


def build_ln0():
    C = _newC()
    nc = C.nc
    x = nc.dram_tensor("x", [T, D], F32, kind="ExternalInput").ap()
    xn = nc.dram_tensor("xn", [CD, 128, T], BF16, kind="ExternalOutput").ap()
    emit_ln0(C, x, xn)
    return C.finish()


def build_qkv(layer=0):
    C = _newC()
    nc = C.nc
    xn = nc.dram_tensor("xn", [CD, 128, T], BF16, kind="ExternalInput").ap()
    Wq = nc.dram_tensor("Wq", [D, D], F32, kind="ExternalInput").ap()
    Wk = nc.dram_tensor("Wk", [D, HKV * DH], F32, kind="ExternalInput").ap()
    Wv = nc.dram_tensor("Wv", [D, HKV * DH], F32, kind="ExternalInput").ap()
    cq = nc.dram_tensor("cq", [128, T], F32, kind="ExternalInput").ap()
    sq = nc.dram_tensor("sq", [128, T], F32, kind="ExternalInput").ap()
    ck = nc.dram_tensor("ck", [128, T], F32, kind="ExternalInput").ap()
    sk = nc.dram_tensor("sk", [128, T], F32, kind="ExternalInput").ap()
    Qo = nc.dram_tensor("Q", [HQ, 128, T], BF16, kind="ExternalOutput").ap()
    Ko = nc.dram_tensor("K", [HKV, 128, T], BF16, kind="ExternalOutput").ap()
    Vo = nc.dram_tensor("V", [T, HKV * DH], BF16, kind="ExternalOutput").ap()
    emit_qkv(C, xn, Wq, Wk, Wv, cq, sq, ck, sk, Qo, Ko, Vo)
    return C.finish()


def build_attn():
    C = _newC()
    nc = C.nc
    Qi = nc.dram_tensor("Q", [HQ, 128, T], BF16, kind="ExternalInput").ap()
    Ko = nc.dram_tensor("Kown", [HKV, 128, T], BF16, kind="ExternalInput").ap()
    Kp = nc.dram_tensor("Kpart", [HKV, 128, T], BF16, kind="ExternalInput").ap()
    Vo_ = nc.dram_tensor("Vown", [T, HKV * DH], BF16, kind="ExternalInput").ap()
    Vp = nc.dram_tensor("Vpart", [T, HKV * DH], BF16, kind="ExternalInput").ap()
    Mi = nc.dram_tensor("masks", [NTT, 128, 512], BF16, kind="ExternalInput").ap()
    Co = nc.dram_tensor("ctx", [HQ, 128, T], BF16, kind="ExternalOutput").ap()
    emit_attn(C, Qi, Ko, Kp, Vo_, Vp, Mi, Co)
    return C.finish()


def build_combine(kind):
    C = _newC()
    nc = C.nc
    nK = CD if kind == "wo" else HB
    A = nc.dram_tensor("A", [nK, 128, T], BF16, kind="ExternalInput").ap()
    if kind == "wo":
        W = nc.dram_tensor("W", [D, D], F32, kind="ExternalInput").ap()
    else:
        W = nc.dram_tensor("W", [CD, HB, 128, 128], BF16, kind="ExternalInput").ap()
    xprev = nc.dram_tensor("xprev", [T, D], F32, kind="ExternalInput").ap()
    xnew = nc.dram_tensor("xnew", [T, D], F32, kind="ExternalOutput").ap()
    xn2 = nc.dram_tensor("xn2", [CD, 128, T], BF16, kind="ExternalOutput").ap()
    emit_combine(C, kind, A, W, xprev, xnew, xn2)
    return C.finish()


def build_ffn0up():
    C = _newC()
    nc = C.nc
    xn = nc.dram_tensor("xn", [CD, 128, T], BF16, kind="ExternalInput").ap()
    w1 = nc.dram_tensor("w1", [352256, 64], I32, kind="ExternalInput").ap()
    s1 = nc.dram_tensor("s1", [704512], F32, kind="ExternalInput").ap()
    w3 = nc.dram_tensor("w3", [352256, 64], I32, kind="ExternalInput").ap()
    s3 = nc.dram_tensor("s3", [704512], F32, kind="ExternalInput").ap()
    ha = nc.dram_tensor("ha", [HB, 128, T], BF16, kind="ExternalOutput").ap()
    emit_ffn0up(C, xn, w1, s1, w3, s3, ha)
    return C.finish()


def build_w2t():
    C = _newC()
    nc = C.nc
    w2 = nc.dram_tensor("w2", [352256, 64], I32, kind="ExternalInput").ap()
    s2 = nc.dram_tensor("s2", [704512], F32, kind="ExternalInput").ap()
    W2T = nc.dram_tensor("W2T", [CD, HB, 128, 128], BF16, kind="ExternalOutput").ap()
    emit_w2t(C, w2, s2, W2T)
    return C.finish()


def build_ffn1():
    C = _newC()
    nc = C.nc
    xn = nc.dram_tensor("xn", [CD, 128, T], BF16, kind="ExternalInput").ap()
    Wup = nc.dram_tensor("Wup", [D, D], F32, kind="ExternalInput").ap()
    gateT = nc.dram_tensor("gateT", [D, 4], F32, kind="ExternalInput").ap()
    ebias = nc.dram_tensor("ebias", [1, 4], F32, kind="ExternalInput").ap()
    AT = nc.dram_tensor("AT", [D, 32], F32, kind="ExternalInput").ap()
    Ball = nc.dram_tensor("Ball", [32, D], F32, kind="ExternalInput").ap()
    xprev = nc.dram_tensor("xprev", [T, D], F32, kind="ExternalInput").ap()
    xnew = nc.dram_tensor("xnew", [T, D], F32, kind="ExternalOutput").ap()
    emit_ffn1(C, xn, Wup, gateT, ebias, AT, Ball, xprev, xnew)
    return C.finish()


def build_megaA():
    C = _newC()
    nc = C.nc
    EI = lambda n, s, d: nc.dram_tensor(n, s, d, kind="ExternalInput").ap()
    EO = lambda n, s, d: nc.dram_tensor(n, s, d, kind="ExternalOutput").ap()
    IN = lambda n, s, d: nc.dram_tensor(n, s, d, kind="Internal").ap()
    x = EI("x", [T, D], F32)
    xpart = EI("xpart", [T, D], F32)
    Wq0 = EI("Wq0", [D, D], F32)
    Wk0 = EI("Wk0", [D, HKV * DH], F32)
    Wv0 = EI("Wv0", [D, HKV * DH], F32)
    Wo0 = EI("Wo0", [D, D], F32)
    w1 = EI("w1", [352256, 64], I32)
    s1 = EI("s1", [704512], F32)
    w2 = EI("w2", [352256, 64], I32)
    s2 = EI("s2", [704512], F32)
    w3 = EI("w3", [352256, 64], I32)
    s3 = EI("s3", [704512], F32)
    Wq1 = EI("Wq1", [D, D], F32)
    Wk1 = EI("Wk1", [D, HKV * DH], F32)
    Wv1 = EI("Wv1", [D, HKV * DH], F32)
    cq0 = EI("cq0", [128, T], F32)
    sq0 = EI("sq0", [128, T], F32)
    ck0 = EI("ck0", [128, T], F32)
    sk0 = EI("sk0", [128, T], F32)
    ck0p = EI("ck0p", [128, T], F32)
    sk0p = EI("sk0p", [128, T], F32)
    cq1 = EI("cq1", [128, T], F32)
    sq1 = EI("sq1", [128, T], F32)
    ck1 = EI("ck1", [128, T], F32)
    sk1 = EI("sk1", [128, T], F32)
    masks = EI("masks", [NTT, 128, 512], BF16)
    x2o = EO("x2", [T, D], F32)
    Q1o = EO("Q1", [HQ, 128, T], BF16)
    K1o = EO("K1", [HKV, 128, T], BF16)
    V1o = EO("V1", [T, HKV * DH], BF16)

    xn0 = IN("xn0", [CD, 128, T], BF16)
    xn0p = IN("xn0p", [CD, 128, T], BF16)
    Q0 = IN("Q0", [HQ, 128, T], BF16)
    K0 = IN("K0", [HKV, 128, T], BF16)
    V0 = IN("V0", [T, HKV * DH], BF16)
    K0p = IN("K0p", [HKV, 128, T], BF16)
    V0p = IN("V0p", [T, HKV * DH], BF16)
    ctx0 = IN("ctx0", [HQ, 128, T], BF16)
    x1 = IN("x1", [T, D], F32)
    xnf = IN("xnf", [CD, 128, T], BF16)
    ha = IN("ha", [HB, 128, T], BF16)
    W2T = IN("W2T", [CD, HB, 128, 128], BF16)
    x2i = IN("x2i", [T, D], F32)
    xn3 = IN("xn3", [CD, 128, T], BF16)

    emit_ln0(C, x, xn0)
    C.join(); C.section()
    emit_ln0(C, xpart, xn0p)
    C.join(); C.section()
    emit_qkv(C, xn0, Wq0, Wk0, Wv0, cq0, sq0, ck0, sk0, Q0, K0, V0)
    C.join(); C.section()
    emit_qkv(C, xn0p, None, Wk0, Wv0, None, None, ck0p, sk0p, None, K0p, V0p)
    C.join(); C.section()
    emit_attn(C, Q0, K0, K0p, V0, V0p, masks, ctx0)
    C.join(); C.section()
    emit_combine(C, "wo", ctx0, Wo0, x, x1, xnf)
    C.join(); C.section()
    emit_w2t(C, w2, s2, W2T)
    C.join(); C.section()
    emit_ffn0up(C, xnf, w1, s1, w3, s3, ha)
    C.join(); C.section()
    emit_combine(C, "ffn0down", ha, W2T, x1, x2o, xn3)
    C.join(); C.section()
    emit_qkv(C, xn3, Wq1, Wk1, Wv1, cq1, sq1, ck1, sk1, Q1o, K1o, V1o)
    return C.finish()


def build_megaB():
    C = _newC()
    nc = C.nc
    EI = lambda n, s, d: nc.dram_tensor(n, s, d, kind="ExternalInput").ap()
    EO = lambda n, s, d: nc.dram_tensor(n, s, d, kind="ExternalOutput").ap()
    IN = lambda n, s, d: nc.dram_tensor(n, s, d, kind="Internal").ap()
    x2 = EI("x2", [T, D], F32)
    Q1 = EI("Q1", [HQ, 128, T], BF16)
    K1 = EI("K1", [HKV, 128, T], BF16)
    K1p = EI("K1p", [HKV, 128, T], BF16)
    V1 = EI("V1", [T, HKV * DH], BF16)
    V1p = EI("V1p", [T, HKV * DH], BF16)
    masks = EI("masks", [NTT, 128, 512], BF16)
    Wo1 = EI("Wo1", [D, D], F32)
    Wup = EI("Wup", [D, D], F32)
    gateT = EI("gateT", [D, 4], F32)
    ebias = EI("ebias", [1, 4], F32)
    AT = EI("AT", [D, 32], F32)
    Ball = EI("Ball", [32, D], F32)
    x4o = EO("x4", [T, D], F32)
    ctx1 = IN("ctx1", [HQ, 128, T], BF16)
    x3 = IN("x3", [T, D], F32)
    xn4 = IN("xn4", [CD, 128, T], BF16)

    emit_attn(C, Q1, K1, K1p, V1, V1p, masks, ctx1)
    C.join(); C.section()
    emit_combine(C, "wo", ctx1, Wo1, x2, x3, xn4)
    C.join(); C.section()
    emit_ffn1(C, xn4, Wup, gateT, ebias, AT, Ball, x3, x4o)
    return C.finish()



_PROGS = {}


def _prog(name, fn):
    if name not in _PROGS:
        _PROGS[name] = fn()
    return _PROGS[name]


def _run(name, fn, in_maps):
    nc = _prog(name, fn)
    return run_bass_kernel_spmd(nc, in_maps, core_ids=list(range(8))).results


def kernel(**inputs):
    ins = {k: np.asarray(v) for k, v in inputs.items()}
    x0 = ins["L0_attn_resid"]
    NCORE = 8
    roles = [c % 2 for c in range(NCORE)]
    batches = [c // 2 for c in range(NCORE)]
    pos = {r: tok_positions(r) for r in (0, 1)}
    masks = {r: diag_masks(r) for r in (0, 1)}
    sc = 1.0 / np.sqrt(128.0)
    tabs = {}
    for layer in (0, 1):
        for r in (0, 1):
            Ct, St = rope_tables(layer, r)
            tabs[(layer, r)] = (Ct, St)

    gateT = np.ascontiguousarray(ins["L1_ffn_gate_w"].T)
    AT = np.ascontiguousarray(ins["L1_ffn_lora_A"].transpose(2, 0, 1).reshape(D, 32))
    Ball = np.ascontiguousarray(ins["L1_ffn_lora_B"].transpose(0, 2, 1).reshape(32, D))
    ebias = np.ascontiguousarray(ins["L1_ffn_expert_bias"].reshape(1, 4))

    in_maps_A = []
    for c in range(NCORE):
        r, b = roles[c], batches[c]
        C0, S0 = tabs[(0, r)]
        C0p, S0p = tabs[(0, 1 - r)]
        C1, S1 = tabs[(1, r)]
        in_maps_A.append({
            "x": np.ascontiguousarray(x0[b][pos[r]]),
            "xpart": np.ascontiguousarray(x0[b][pos[1 - r]]),
            "Wq0": ins["L0_attn_Wq"], "Wk0": ins["L0_attn_Wk"],
            "Wv0": ins["L0_attn_Wv"], "Wo0": ins["L0_attn_Wo"],
            "w1": ins["L0_ffn_w1"], "s1": ins["L0_ffn_s1"],
            "w2": ins["L0_ffn_w2"], "s2": ins["L0_ffn_s2"],
            "w3": ins["L0_ffn_w3"], "s3": ins["L0_ffn_s3"],
            "Wq1": ins["L1_attn_Wq"], "Wk1": ins["L1_attn_Wk"],
            "Wv1": ins["L1_attn_Wv"],
            "cq0": C0 * sc, "sq0": S0 * sc, "ck0": C0, "sk0": S0,
            "ck0p": C0p, "sk0p": S0p,
            "cq1": C1 * sc, "sq1": S1 * sc, "ck1": C1, "sk1": S1,
            "masks": masks[r],
        })
    resA = _run("megaA", build_megaA, in_maps_A)

    in_maps_B = []
    for c in range(NCORE):
        r, b = roles[c], batches[c]
        partner = 2 * b + (1 - r)
        in_maps_B.append({
            "x2": resA[c]["x2"],
            "Q1": resA[c]["Q1"],
            "K1": resA[c]["K1"], "K1p": resA[partner]["K1"],
            "V1": resA[c]["V1"], "V1p": resA[partner]["V1"],
            "masks": masks[r],
            "Wo1": ins["L1_attn_Wo"],
            "Wup": ins["L1_ffn_W_up"], "gateT": gateT, "ebias": ebias,
            "AT": AT, "Ball": Ball,
        })
    resB = _run("megaB", build_megaB, in_maps_B)

    out = np.empty((4, TG, D), np.float32)
    for c in range(NCORE):
        out[batches[c]][pos[roles[c]]] = resB[c]["x4"]
    return out


# version 06
# speedup vs baseline: 1.1859x; 1.1859x over previous
import math
import contextlib
import numpy as np
import ml_dtypes
import concourse.bass as bass
import concourse.mybir as mybir
from concourse.bass_utils import run_bass_kernel_spmd

F32 = mybir.dt.float32
BF16 = mybir.dt.bfloat16
I32 = mybir.dt.int32
U8 = mybir.dt.uint8
AF = mybir.ActivationFunctionType
ALU = mybir.AluOpType
AX = mybir.AxisListType.X
BF = ml_dtypes.bfloat16

D = 4096
T = 2048
TG = 4096
NTT = T // 128
CD = D // 128
HQ, HKV, DH = 32, 8, 128
HID = 11008
HB = HID // 128
EPS = 1e-5
NEG = -1e30
DTS = {F32: 4, BF16: 2, I32: 4, U8: 1}
POOLB = 212736

SEMNAMES = ["ld1", "ld2", "wld", "mm", "dve", "acs", "pml", "st1", "st2",
            "x1", "x2", "x3", "bar"]

IDENT = np.eye(128).astype(BF)
IDENTF = np.eye(128, dtype=np.float32)


class Sem:
    def __init__(self, es, nc, name):
        self.h = es.enter_context(nc.semaphore(name))
        self.v = 0

    def inc(self, instr, n=1):
        instr.then_inc(self.h, n)
        self.v += n
        return instr


def _dma(q, out, in_, sem):
    sem.inc(q.dma_start(out=out, in_=in_), 16)


class Ctx:

    def __init__(self):
        self.nc = bass.Bass("TRN2", target_bir_lowering=False)
        self.es = contextlib.ExitStack()
        self._off = 0
        self._psum_i = 0
        self._inline = {}

    def setup(self):
        nc = self.nc
        self.pool = self.es.enter_context(nc.sbuf_tensor("pool", [128, POOLB], U8))
        self.banks = [self.es.enter_context(nc.psum_tensor(f"pp{i}", [128, 512], F32))
                      for i in range(8)]
        self.s = {n: Sem(self.es, nc, n) for n in SEMNAMES}
        self.es.enter_context(nc.Block())

    def inline(self, name, arr):
        if name not in self._inline:
            self._inline[name] = self.nc.inline_tensor(arr, name).ap()
        return self._inline[name]

    def section(self):
        self._off = 0
        self._psum_i = 0

    def carve(self, shape, dtype):
        nb = int(np.prod(shape[1:])) * DTS[dtype]
        off = (self._off + 63) & ~63
        self._off = off + nb
        assert self._off <= POOLB, f"pool overflow: {self._off}"
        v = self.pool[0:shape[0], off:off + nb].bitcast(dtype)
        if len(shape) > 2:
            dims = " ".join(f"d{i}" for i in range(1, len(shape)))
            kw = {f"d{i}": shape[i] for i in range(1, len(shape))}
            v = v.rearrange(f"p ({dims}) -> p {dims}", **kw)
        return v

    def psum(self, dtype=F32):
        t = self.banks[self._psum_i]
        self._psum_i += 1
        if dtype == F32:
            return t[:]
        return t[:].bitcast(dtype)[:, 0:512]

    def psumb(self):
        t = self.banks[self._psum_i]
        self._psum_i += 1
        return t[:].bitcast(BF16)

    def join(self):
        nc = self.nc
        b = self.s["bar"]
        engs = [nc.tensor, nc.vector, nc.scalar, nc.gpsimd, nc.sync]
        for e in engs:
            e.nop()
            b.inc(e.nop(), 1)
        for e in engs:
            e.wait_ge(b.h, b.v)

    def finish(self):
        self.es.close()
        return self.nc


def tok_positions(role):
    return np.concatenate([np.arange(256 * i + 128 * role, 256 * i + 128 * (role + 1))
                           for i in range(NTT)])


def key_globals(role):
    g = np.empty(TG, np.int64)
    for u in range(16):
        g[256 * u:256 * u + 128] = 256 * u + 128 * role + np.arange(128)
        g[256 * u + 128:256 * u + 256] = 256 * u + 128 * (1 - role) + np.arange(128)
    return g


def diag_masks(role):
    kg = key_globals(role)
    m = np.zeros((NTT, 128, 512), np.float32)
    for ti in range(NTT):
        g0 = 256 * ti + 128 * role
        nk5 = ti // 2 + 1
        k0 = 512 * (nk5 - 1)
        kj = kg[k0:k0 + 512][None, :]
        qi = np.arange(128)[:, None]
        m[ti] = np.where(kj <= g0 + qi, 0.0, NEG)
    return m.astype(BF)


def rope_tables(layer, role):
    pos = tok_positions(role).astype(np.float64)
    return rope_tables_pos(layer, pos)


def rope_tables_pos(layer, pos):
    if layer == 0:
        theta = 500000.0
        idx = np.arange(0, DH, 2, dtype=np.float64) / DH
        freq = 1.0 / (2.0 * math.pi) * theta ** (-idx)
        factor, lo, hi, L0 = 32.0, 1.0, 4.0, 8192.0
        freq_low, freq_high = lo / L0, hi / L0
        freq_scaled = np.where(freq < freq_low, freq / factor, freq)
        smooth = np.clip((L0 * freq - lo) / (hi - lo), 0.0, 1.0)
        freq_smooth = (1.0 - smooth) * (freq / factor) + smooth * freq
        is_mid = (freq >= freq_low) & (freq <= freq_high)
        freq = np.where(is_mid, freq_smooth, freq_scaled)
        phase = 2.0 * math.pi * pos[:, None] * freq[None, :]
    else:
        theta = 10000.0
        inv = 1.0 / theta ** (np.arange(0, DH, 2, dtype=np.float64)[: DH // 2] / DH)
        phase = pos[:, None] * inv[None, :]
    cos = np.cos(phase)
    sin = np.sin(phase)
    Ct = np.empty((128, T), np.float64)
    St = np.empty((128, T), np.float64)
    Ct[:64] = cos.T
    Ct[64:] = cos.T
    St[:64] = -sin.T
    St[64:] = sin.T
    return Ct.astype(np.float32), St.astype(np.float32)


def emit_ln_stats(nc, xt, xsq, st):
    sx, sxx, m, r, m2, sxxd, ve = (st[:, i:i + 1] for i in range(7))
    nc.vector.tensor_scalar(out=xsq, in0=xt, scalar1=1.0, scalar2=0.0,
                            op0=ALU.mult, op1=ALU.add, accum_out=sx)
    last = nc.vector.scalar_tensor_tensor(out=xsq, in0=xt, scalar=1.0, in1=xt,
                                          op0=ALU.mult, op1=ALU.mult, accum_out=sxx)

    def act1(d=D):
        nc.scalar.activation(out=m2, in_=sx, func=AF.Square, scale=1.0 / d)
        nc.scalar.activation(out=sxxd, in_=sxx, func=AF.Copy, scale=1.0 / d, bias=EPS)
        return nc.scalar.activation(out=m, in_=sx, func=AF.Copy, scale=1.0 / d)

    def vec2():
        return nc.vector.tensor_tensor(out=ve, in0=sxxd, in1=m2, op=ALU.subtract)

    def act2():
        nc.scalar.activation(out=ve, in_=ve, func=AF.Ln)
        return nc.scalar.activation(out=r, in_=ve, func=AF.Exp, scale=-0.5)

    return m, r, last, (act1, vec2, act2)



def emit_ln0(C, x, xn):
    nc = C.nc
    xt = [C.carve([128, D], F32) for _ in range(2)]
    xb = C.carve([128, D], BF16)
    xsq = C.carve([128, D], F32)
    st = C.carve([128, 8], F32)
    idt = C.carve([128, 128], BF16)
    stage = [C.carve([128, CD, 128], BF16) for _ in range(2)]
    pst = [C.psumb() for _ in range(2)]
    ldx, cmp, act, trn, evc, stx = (C.s[n] for n in
                                    ("ld1", "x1", "acs", "mm", "x2", "st1"))
    trn0, evc0, stx0 = trn.v, evc.v, stx.v
    _dma(nc.sync, idt[:], C.inline("identb", IDENT)[:], ldx)
    applied_v = {}
    for tt in range(NTT):
        b = tt % 2
        if tt >= 2:
            nc.sync.wait_ge(cmp.h, applied_v[tt - 2])
        _dma(nc.sync, xt[b][:], x[tt * 128:(tt + 1) * 128, :], ldx)
        nc.vector.wait_ge(ldx.h, ldx.v)
        if tt >= 1:
            nc.vector.wait_ge(trn.h, trn0 + 32 * tt)
        m, r, last, (a1, v2, a3) = emit_ln_stats(nc, xt[b][:], xsq[:], st[:])
        cmp.inc(last, 1)
        nc.scalar.wait_ge(cmp.h, cmp.v)
        act.inc(a1(), 1)
        nc.vector.wait_ge(act.h, act.v)
        cmp.inc(v2(), 1)
        nc.scalar.wait_ge(cmp.h, cmp.v)
        act.inc(a3(), 1)
        nc.vector.wait_ge(act.h, act.v)
        cmp.inc(nc.vector.tensor_scalar(out=xb[:], in0=xt[b][:], scalar1=m, scalar2=r,
                                        op0=ALU.subtract, op1=ALU.mult), 1)
        applied_v[tt] = cmp.v
        nc.tensor.wait_ge(cmp.h, cmp.v)
        if tt >= 2:
            nc.scalar.wait_ge(stx.h, stx0 + 16 * (tt - 1))
        for g in range(4):
            bank = pst[g % 2]
            need = 4 * tt + g - 1
            if need > 0:
                nc.tensor.wait_ge(evc.h, evc0 + need)
            for j in range(8):
                c = g * 8 + j
                trn.inc(nc.tensor.transpose(bank[:, j * 128:(j + 1) * 128],
                                            xb[:, c * 128:(c + 1) * 128], idt[:]), 1)
            nc.scalar.wait_ge(trn.h, trn.v)
            evc.inc(nc.scalar.copy(
                out=stage[b][:, g * 8:(g + 1) * 8, :].rearrange("p c q -> p (c q)"),
                in_=bank[:]), 1)
        nc.gpsimd.wait_ge(evc.h, evc.v)
        _dma(nc.gpsimd,
             xn[:, :, tt * 128:(tt + 1) * 128].rearrange("c p q -> p c q"),
             stage[b][:], stx)
    nc.sync.wait_ge(stx.h, stx.v)
    nc.gpsimd.wait_ge(stx.h, stx.v)


def emit_qkv(C, xn, Wq, Wk, Wv, cq, sq, ck, sk, Qo, Ko, Vo):
    nc = C.nc
    xn_sb = C.carve([128, CD, T], BF16)
    tabs = {n: C.carve([128, T], F32) for n in ("cq", "sq", "ck", "sk")}
    wbuf = [C.carve([128, CD, 128], BF16) for _ in range(2)]
    rp1 = [C.carve([128, 512], F32) for _ in range(2)]
    prt = [C.carve([128, 512], F32) for _ in range(2)]
    rp2 = [C.carve([128, 512], F32) for _ in range(2)]
    qstage = [C.carve([128, 512], BF16) for _ in range(2)]
    vf = C.carve([128, T], F32)
    idtf = C.carve([128, 128], F32)
    vts = [C.carve([128, 4, 128], BF16) for _ in range(2)]
    ps = [C.psum(F32) for _ in range(8)]
    ldc, wld, mm, evq, pse, stq = (C.s[n] for n in
                                   ("ld1", "wld", "mm", "x1", "x2", "st1"))
    wld0, evq0, stq0 = wld.v, evq.v, stq.v

    _dma(nc.sync, xn_sb[:], xn[:].rearrange("c p t -> p c t"), ldc)
    for nm, src in (("cq", cq), ("sq", sq), ("ck", ck), ("sk", sk)):
        if src is not None:
            _dma(nc.sync, tabs[nm][:], src[:], ldc)
    _dma(nc.sync, idtf[:], C.inline("identf", IDENTF)[:], ldc)

    mmv = {}
    rope_i = 0
    nload = 0

    def qk_part(W, nOC, ctab, stab, OUT, base_oc, wslice):
        nonlocal rope_i, nload
        for j in range(nOC):
            oc = base_oc + j
            par = oc % 2
            banks = ps[par * 4:par * 4 + 4]
            if oc - 2 in mmv:
                nc.gpsimd.wait_ge(mm.h, mmv[oc - 2])
            _dma(nc.gpsimd, wbuf[par][:],
                 W[:, wslice + j * 128:wslice + (j + 1) * 128]
                 .rearrange("(c p) o -> p c o", p=128), wld)
            nload += 1
            nc.tensor.wait_ge(wld.h, wld0 + 16 * nload)
            if oc == 0:
                nc.tensor.wait_ge(ldc.h, ldc.v)
            if oc >= 2:
                nc.tensor.wait_ge(evq.h, evq0 + 4 * (oc - 1))
            for c in range(CD):
                for t4 in range(4):
                    ins = nc.tensor.matmul(banks[t4][:], wbuf[par][:, c, :],
                                           xn_sb[:, c, t4 * 512:(t4 + 1) * 512],
                                           start=(c == 0), stop=(c == CD - 1))
            mm.inc(ins, 1)
            mmv[oc] = mm.v
            for t4 in range(4):
                rpar = rope_i % 2
                sl = slice(t4 * 512, (t4 + 1) * 512)
                nc.vector.wait_ge(mm.h, mmv[oc])
                if rope_i >= 2:
                    nc.vector.wait_ge(stq.h, stq0 + 16 * (rope_i - 1))
                nc.vector.tensor_copy(prt[rpar][0:64, :], banks[t4][64:128, :])
                nc.vector.tensor_copy(prt[rpar][64:128, :], banks[t4][0:64, :])
                nc.vector.tensor_tensor(out=rp1[rpar][:], in0=banks[t4][:],
                                        in1=ctab[:, sl], op=ALU.mult)
                evq.inc(nc.vector.tensor_tensor(out=rp2[rpar][:], in0=prt[rpar][:],
                                                in1=stab[:, sl], op=ALU.mult), 1)
                pse.inc(nc.vector.tensor_tensor(out=qstage[rpar][:], in0=rp1[rpar][:],
                                                in1=rp2[rpar][:], op=ALU.add), 1)
                nc.gpsimd.wait_ge(pse.h, pse.v)
                _dma(nc.gpsimd, OUT[j, :, sl], qstage[rpar][:], stq)
                rope_i += 1

    nq = 0
    if Qo is not None:
        qk_part(Wq, HQ, tabs["cq"], tabs["sq"], Qo, 0, 0)
        nq = HQ
    qk_part(Wk, HKV, tabs["ck"], tabs["sk"], Ko, nq, 0)

    vstore = {}
    vevac = {}
    for j in range(HKV):
        oc = nq + HKV + j
        par = oc % 2
        banks = ps[par * 4:par * 4 + 4]
        if oc - 2 in mmv:
            nc.gpsimd.wait_ge(mm.h, mmv[oc - 2])
        _dma(nc.gpsimd, wbuf[par][:],
             Wv[:, j * 128:(j + 1) * 128].rearrange("(c p) o -> p c o", p=128), wld)
        nload += 1
        nc.tensor.wait_ge(wld.h, wld0 + 16 * nload)
        if j < 2:
            nc.tensor.wait_ge(evq.h, evq0 + 4 * (nq + HKV))
        if j >= 2:
            nc.tensor.wait_ge(evq.h, vevac[(j - 2, 1)])
        for c in range(CD):
            for t4 in range(4):
                ins = nc.tensor.matmul(banks[t4][:], wbuf[par][:, c, :],
                                       xn_sb[:, c, t4 * 512:(t4 + 1) * 512],
                                       start=(c == 0), stop=(c == CD - 1))
        mm.inc(ins, 1)
        mmv[oc] = mm.v
        nc.scalar.wait_ge(mm.h, mmv[oc])
        for t4 in range(4):
            evq.inc(nc.scalar.copy(out=vf[:, t4 * 512:(t4 + 1) * 512],
                                   in_=banks[t4][:]), 1)
        nc.tensor.wait_ge(evq.h, evq.v)
        for g in range(4):
            slot = g % 2
            tb = banks[slot]
            if g >= 2:
                nc.tensor.wait_ge(evq.h, vevac[(j, slot)])
            for t4 in range(4):
                tt = g * 4 + t4
                ins = nc.tensor.transpose(tb[:, t4 * 128:(t4 + 1) * 128],
                                          vf[:, tt * 128:(tt + 1) * 128], idtf[:])
            mm.inc(ins, 1)
            nc.vector.wait_ge(mm.h, mm.v)
            if slot in vstore:
                nc.vector.wait_ge(stq.h, vstore[slot])
            evq.inc(nc.vector.tensor_copy(
                vts[slot][:].rearrange("p a d -> p (a d)"), tb[:]), 1)
            vevac[(j, slot)] = evq.v
            nc.gpsimd.wait_ge(evq.h, evq.v)
            _dma(nc.gpsimd,
                 Vo[g * 512:(g + 1) * 512, j * 128:(j + 1) * 128]
                 .rearrange("(a p) d -> p a d", p=128),
                 vts[slot][:], stq)
            vstore[slot] = stq.v
    nc.sync.wait_ge(stq.h, stq.v)
    nc.gpsimd.wait_ge(stq.h, stq.v)


def emit_attn(C, Qi, Kown, Kpart, Vown, Vpart, Mi, Co):
    nc = C.nc
    K_sb = C.carve([128, HKV, TG], BF16)
    V_sb = C.carve([128, TG // 128, HKV * DH], BF16)
    M_sb = [C.carve([128, 512], BF16) for _ in range(2)]
    idt = C.carve([128, 128], BF16)
    qa = C.carve([128, HQ, 128], BF16)
    S_sb = C.carve([128, TG], F32)
    Pb = [C.carve([128, TG], BF16) for _ in range(2)]
    PT = C.carve([128, TG // 128, 4, 128], BF16)
    ctst = [C.carve([128, 4, 128], BF16) for _ in range(2)]
    st = C.carve([128, 32], F32)
    psS = [C.psum(F32) for _ in range(3)]
    psT = [C.psumb() for _ in range(2)]
    psC = [C.psum(F32) for _ in range(2)]
    ldc, ldq, mm, dmx, aev, pml, stc = (C.s[n] for n in
                                        ("ld1", "ld2", "mm", "dve", "acs", "pml", "st1"))

    for h in range(HKV):
        for s_, Ksrc in ((0, Kown), (1, Kpart)):
            _dma(nc.sync,
                 K_sb[:, h, :].rearrange("p (u s q) -> p u s q", s=2, q=128)[:, :, s_, :],
                 Ksrc[h].rearrange("p (u q) -> p u q", q=128), ldc)
    for s_, Vsrc in ((0, Vown), (1, Vpart)):
        _dma(nc.sync,
             V_sb.rearrange("p (u s) d -> p u s d", s=2)[:, :, s_, :],
             Vsrc[:].rearrange("(u p) d -> p u d", p=128), ldc)
    _dma(nc.sync, idt[:], C.inline("identb", IDENT)[:], ldc)

    negmax = [st[:, 16 + i:17 + i] for i in range(2)]
    lsum = [st[:, 18 + i:19 + i] for i in range(2)]
    linv = [st[:, 20 + i:21 + i] for i in range(2)]
    linv2 = [st[:, 22 + i:23 + i] for i in range(2)]

    ctx_store = {}
    qtile_mm_done = {}
    bank_free = [None, None, None]
    ptbank_free = [None, None]
    ctx_evac = {}
    first = [True]
    prev_exp = [None]
    pb_free = {0: None, 1: None}
    pending = [None]

    def s_stage(ti, kvh, h4, nk5, hpar):
        qh = kvh * 4 + h4
        for kt in range(nk5):
            bk = kt % 3
            bank = psS[bk]
            if bank_free[bk] is not None:
                s_, v_ = bank_free[bk]
                nc.tensor.wait_ge(s_.h, v_)
            ins = nc.tensor.matmul(bank[:, :], qa[:, qh, :],
                                   K_sb[:, kvh, kt * 512:(kt + 1) * 512],
                                   start=True, stop=True)
            mm.inc(ins, 1)
            sl = slice(kt * 512, (kt + 1) * 512)
            if kt == nk5 - 1:
                nc.vector.wait_ge(mm.h, mm.v)
                if prev_exp[0] is not None:
                    nc.vector.wait_ge(aev.h, prev_exp[0])
                dmx.inc(nc.vector.scalar_tensor_tensor(
                    out=S_sb[:, sl], in0=bank[:, :], scalar=1.0,
                    in1=M_sb[ti % 2][:], op0=ALU.mult, op1=ALU.add), 1)
                bank_free[bk] = (dmx, dmx.v)
            else:
                nc.scalar.wait_ge(mm.h, mm.v)
                if prev_exp[0] is not None:
                    pass
                aev.inc(nc.scalar.copy(out=S_sb[:, sl], in_=bank[:, :]), 1)
                bank_free[bk] = (aev, aev.v)
        nc.vector.wait_ge(aev.h, aev.v)
        dmx.inc(nc.vector.tensor_reduce(negmax[hpar], S_sb[:, 0:nk5 * 512], AX,
                                        ALU.max, negate=True), 1)
        nc.scalar.wait_ge(dmx.h, dmx.v)
        if pb_free[hpar] is not None:
            nc.scalar.wait_ge(mm.h, pb_free[hpar])
        aev.inc(nc.scalar.activation(
            out=Pb[hpar][:, 0:nk5 * 512], in_=S_sb[:, 0:nk5 * 512], func=AF.Exp,
            bias=negmax[hpar], accum_out=lsum[hpar]), 1)
        prev_exp[0] = aev.v
        nc.vector.wait_ge(aev.h, aev.v)
        dmx.inc(nc.vector.reciprocal(linv[hpar], lsum[hpar]), 1)
        nc.gpsimd.wait_ge(dmx.h, dmx.v)
        pml.inc(nc.gpsimd.tensor_copy(linv2[hpar], linv[hpar]), 1)
        nc.vector.wait_ge(pml.h, pml.v)
        dmx.inc(nc.vector.tensor_scalar(
            out=Pb[hpar][:, 0:nk5 * 512], in0=Pb[hpar][:, 0:nk5 * 512],
            scalar1=linv2[hpar], scalar2=None, op0=ALU.mult), 1)
        return dmx.v

    def t_stage(h4, nk1, hpar, pscale_v):
        nc.tensor.wait_ge(dmx.h, pscale_v)
        for g in range((nk1 + 7) // 8):
            bk = g % 2
            tb = psT[bk]
            if ptbank_free[bk] is not None:
                nc.tensor.wait_ge(dmx.h, ptbank_free[bk])
            nkk = min(8, nk1 - g * 8)
            for j in range(nkk):
                kc = g * 8 + j
                ins = nc.tensor.transpose(
                    tb[:, j * 128:(j + 1) * 128],
                    Pb[hpar][:, kc * 128:(kc + 1) * 128], idt[:])
            mm.inc(ins, 1)
            nc.vector.wait_ge(mm.h, mm.v)
            dmx.inc(nc.vector.tensor_copy(
                PT[:, g * 8:g * 8 + nkk, h4, :],
                tb[:].rearrange("p (a q) -> p a q", a=8)[:, 0:nkk, :]), 1)
            ptbank_free[bk] = dmx.v
        pb_free[hpar] = mm.v

    for ti in range(NTT):
        lq0 = ti * 128
        qpar = ti % 2
        nk5 = ti // 2 + 1
        nk1 = 4 * nk5
        if ti >= 1:
            nc.sync.wait_ge(mm.h, qtile_mm_done[ti - 1])
        _dma(nc.sync, qa[:], Qi[:, :, lq0:lq0 + 128].rearrange("h p q -> p h q"), ldq)
        _dma(nc.sync, M_sb[qpar][:], Mi[ti], ldq)
        nc.tensor.wait_ge(ldq.h, ldq.v)
        if first[0]:
            nc.tensor.wait_ge(ldc.h, ldc.v)
            first[0] = False
        for kvh in range(HKV):
            cpar = (ti * HKV + kvh) % 2
            pv = {}
            pv[0] = s_stage(ti, kvh, 0, nk5, 0)
            pv[1] = s_stage(ti, kvh, 1, nk5, 1)
            if pending[0] is not None:
                pending[0]()
                pending[0] = None
            t_stage(0, nk1, 0, pv[0])
            pv[2] = s_stage(ti, kvh, 2, nk5, 0)
            t_stage(1, nk1, 1, pv[1])
            pv[3] = s_stage(ti, kvh, 3, nk5, 1)
            t_stage(2, nk1, 0, pv[2])
            t_stage(3, nk1, 1, pv[3])
            pt_ready = dmx.v

            def mk_pv(ti_=ti, kvh_=kvh, nk1_=nk1, cpar_=cpar, lq0_=lq0,
                      ptv_=pt_ready):
                def do():
                    nc.tensor.wait_ge(dmx.h, ptv_)
                    idx = ti_ * HKV + kvh_
                    if idx >= 2:
                        nc.tensor.wait_ge(aev.h, ctx_evac[idx - 2])
                    for kc in range(nk1_):
                        ins = nc.tensor.matmul(
                            psC[cpar_][:, :], V_sb[:, kc, kvh_ * 128:(kvh_ + 1) * 128],
                            PT[:, kc, :, :].rearrange("p a q -> p (a q)"),
                            start=(kc == 0), stop=(kc == nk1_ - 1))
                    mm.inc(ins, 1)
                    nc.scalar.wait_ge(mm.h, mm.v)
                    if cpar_ in ctx_store:
                        nc.scalar.wait_ge(stc.h, ctx_store[cpar_])
                    aev.inc(nc.scalar.copy(out=ctst[cpar_][:].rearrange("p a q -> p (a q)"),
                                           in_=psC[cpar_][:, :]), 1)
                    ctx_evac[idx] = aev.v
                    nc.gpsimd.wait_ge(aev.h, aev.v)
                    _dma(nc.gpsimd,
                         Co[kvh_ * 4:(kvh_ + 1) * 4, :, lq0_:lq0_ + 128]
                         .rearrange("h p q -> p h q"),
                         ctst[cpar_][:], stc)
                    ctx_store[cpar_] = stc.v
                return do
            pending[0] = mk_pv()
        qtile_mm_done[ti] = mm.v
    if pending[0] is not None:
        pending[0]()
        pending[0] = None
    nc.sync.wait_ge(stc.h, stc.v)
    nc.gpsimd.wait_ge(stc.h, stc.v)


def emit_combine(C, kind, A, W, xprev, xnew, xn2):
    nc = C.nc
    nK = CD if kind == "wo" else HB
    nKh = nK // 2
    A_sb = C.carve([128, nK, 512], BF16)
    wbuf = [C.carve([128, nKh, 128], BF16) for _ in range(2)]
    yf1 = C.carve([128, 512], F32)
    yf = [yf1, yf1]
    y4 = C.carve([128, 4, CD, 128], F32)
    xp = C.carve([128, D], F32)
    xnb = C.carve([128, D], BF16)
    st4 = C.carve([128, 64], F32)
    stage = C.carve([128, CD, 128], BF16)
    idt = C.carve([128, 128], BF16)
    idtf = C.carve([128, 128], F32)
    psY = [C.psum(F32) for _ in range(2)]
    psT = [C.psum(F32) for _ in range(2)]
    psB = [C.psumb() for _ in range(2)]
    lda, wld, mm, aev, dev, acs, stx = (C.s[n] for n in
                                        ("ld1", "wld", "mm", "acs", "dve", "x1", "st1"))
    wld0 = wld.v
    _dma(nc.sync, idt[:], C.inline("identb", IDENT)[:], lda)
    _dma(nc.sync, idtf[:], C.inline("identf", IDENTF)[:], lda)

    nwld = 0
    mm_oc = {}
    yf_ev = {}
    trn_ev = {}
    dev_ev = {}
    xnb_ev = {}
    xnb_tr = {}
    stage_store = {}
    xp_store = {}
    for ts in range(4):
        tsl = slice(ts * 512, (ts + 1) * 512)
        if ts >= 1:
            nc.sync.wait_ge(mm.h, mm_oc[(ts - 1, CD - 1)])
        _dma(nc.sync, A_sb[:], A[:, :, tsl].rearrange("c p t -> p c t"), lda)
        for oc in range(CD):
            par = oc % 2
            ldq_eng = nc.gpsimd if kind == "wo" else nc.sync
            for half in range(2):
                buf = wbuf[half]
                if (ts, oc - 1) in mm_oc:
                    ldq_eng.wait_ge(mm.h, mm_oc[(ts, oc - 1)])
                elif ts >= 1 and oc == 0:
                    ldq_eng.wait_ge(mm.h, mm_oc[(ts - 1, CD - 1)])
                if kind == "wo":
                    src = W[:, oc * 128:(oc + 1) * 128] \
                        .rearrange("(c p) o -> p c o", p=128)[:, half * nKh:(half + 1) * nKh, :]
                    _dma(nc.gpsimd, buf[:], src, wld)
                else:
                    src = W[oc, half * nKh:(half + 1) * nKh, :, :].rearrange("c p o -> p c o")
                    _dma(nc.sync, buf[:], src, wld)
                nwld += 1
            nc.tensor.wait_ge(wld.h, wld0 + 16 * nwld)
            nc.tensor.wait_ge(lda.h, lda.v)
            if (ts, oc - 2) in yf_ev:
                nc.tensor.wait_ge(aev.h, yf_ev[(ts, oc - 2)])
            elif ts >= 1 and oc < 2:
                nc.tensor.wait_ge(aev.h, yf_ev[(ts - 1, CD - 2 + oc)])
            for c in range(nK):
                ins = nc.tensor.matmul(psY[par][:], wbuf[c // nKh][:, c % nKh, :],
                                       A_sb[:, c, :],
                                       start=(c == 0), stop=(c == nK - 1))
            mm.inc(ins, 1)
            mm_oc[(ts, oc)] = mm.v
            nc.scalar.wait_ge(mm.h, mm.v)
            if (ts, oc - 1) in trn_ev:
                nc.scalar.wait_ge(mm.h, trn_ev[(ts, oc - 1)])
            elif ts >= 1 and oc == 0:
                nc.scalar.wait_ge(mm.h, trn_ev[(ts - 1, CD - 1)])
            aev.inc(nc.scalar.copy(out=yf[par][:], in_=psY[par][:]), 1)
            yf_ev[(ts, oc)] = aev.v
            nc.tensor.wait_ge(aev.h, aev.v)
            if (ts, oc - 2) in dev_ev:
                nc.tensor.wait_ge(dev.h, dev_ev[(ts, oc - 2)])
            elif ts >= 1 and oc < 2:
                nc.tensor.wait_ge(dev.h, dev_ev[(ts - 1, CD - 2 + oc)])
            for a in range(4):
                ins = nc.tensor.transpose(psT[par][:, a * 128:(a + 1) * 128],
                                          yf[par][:, a * 128:(a + 1) * 128], idtf[:])
            mm.inc(ins, 1)
            trn_ev[(ts, oc)] = mm.v
            nc.vector.wait_ge(mm.h, mm.v)
            dev.inc(nc.vector.tensor_copy(y4[:, :, oc, :], psT[par][:]
                                          .rearrange("p (a q) -> p a q", a=4)), 1)
            dev_ev[(ts, oc)] = dev.v
        yav = [y4[:, a, :, :].rearrange("p c q -> p (c q)") for a in range(4)]
        nc.vector.wait_ge(dev.h, dev.v)
        parts1 = []
        for a in range(4):
            m_, r_, last_, fns = emit_ln_stats(nc, yav[a], xnb[:], st4[:, 16 * a:16 * a + 8])
            parts1.append((m_, r_, fns))
        dev.inc(last_, 1)
        nc.scalar.wait_ge(dev.h, dev.v)
        for a in range(4):
            ins_ = parts1[a][2][0]()
        acs.inc(ins_, 1)
        nc.vector.wait_ge(acs.h, acs.v)
        for a in range(4):
            ins_ = parts1[a][2][1]()
        dev.inc(ins_, 1)
        nc.scalar.wait_ge(dev.h, dev.v)
        for a in range(4):
            ins_ = parts1[a][2][2]()
        acs.inc(ins_, 1)
        nc.vector.wait_ge(acs.h, acs.v)
        for a in range(4):
            tt = ts * 4 + a
            m_, r_, _ = parts1[a]
            nc.vector.tensor_scalar(out=yav[a], in0=yav[a], scalar1=m_, scalar2=r_,
                                    op0=ALU.subtract, op1=ALU.mult)
            if tt >= 1:
                nc.sync.wait_ge(dev.h, dev.v)
                nc.sync.wait_ge(stx.h, xp_store[tt - 1])
            _dma(nc.sync, xp[:], xprev[tt * 128:(tt + 1) * 128, :], lda)
            nc.vector.wait_ge(lda.h, lda.v)
            dev.inc(nc.vector.tensor_tensor(out=yav[a], in0=xp[:], in1=yav[a],
                                            op=ALU.add), 1)
            xp_store[tt] = stx.v + 16
            nc.gpsimd.wait_ge(dev.h, dev.v)
            _dma(nc.gpsimd, xnew[tt * 128:(tt + 1) * 128, :], yav[a], stx)
        if xn2 is not None:
            parts2 = []
            for a in range(4):
                m_, r_, last_, fns = emit_ln_stats(nc, yav[a], xnb[:],
                                                   st4[:, 16 * a + 8:16 * a + 16])
                parts2.append((m_, r_, fns))
            dev.inc(last_, 1)
            nc.scalar.wait_ge(dev.h, dev.v)
            for a in range(4):
                ins_ = parts2[a][2][0]()
            acs.inc(ins_, 1)
            nc.vector.wait_ge(acs.h, acs.v)
            for a in range(4):
                ins_ = parts2[a][2][1]()
            dev.inc(ins_, 1)
            nc.scalar.wait_ge(dev.h, dev.v)
            for a in range(4):
                ins_ = parts2[a][2][2]()
            acs.inc(ins_, 1)
            for a in range(4):
                tt = ts * 4 + a
                m_, r_, _ = parts2[a]
                nc.vector.wait_ge(acs.h, acs.v)
                if tt >= 1 and (tt - 1) in xnb_tr:
                    nc.vector.wait_ge(mm.h, xnb_tr[tt - 1])
                dev.inc(nc.vector.tensor_scalar(out=xnb[:], in0=yav[a], scalar1=m_,
                                                scalar2=r_, op0=ALU.subtract,
                                                op1=ALU.mult), 1)
                nc.tensor.wait_ge(dev.h, dev.v)
                stv = stage_store.get(tt - 1) if tt >= 1 else None
                if stv is not None:
                    nc.vector.wait_ge(stx.h, stv)
                for g in range(4):
                    bank = psB[g % 2]
                    need = None
                    if g >= 2:
                        need = xnb_ev[(tt, g - 2)]
                    elif tt >= 1:
                        need = xnb_ev[(tt - 1, 2 + g)]
                    if need is not None:
                        nc.tensor.wait_ge(dev.h, need)
                    for j in range(8):
                        c = g * 8 + j
                        ins = nc.tensor.transpose(bank[:, j * 128:(j + 1) * 128],
                                                  xnb[:, c * 128:(c + 1) * 128], idt[:])
                    mm.inc(ins, 1)
                    nc.vector.wait_ge(mm.h, mm.v)
                    dev.inc(nc.vector.tensor_copy(
                        stage[:, g * 8:(g + 1) * 8, :].rearrange("p c q -> p (c q)"),
                        bank[:]), 1)
                    xnb_ev[(tt, g)] = dev.v
                xnb_tr[tt] = mm.v
                nc.gpsimd.wait_ge(dev.h, dev.v)
                _dma(nc.gpsimd, xn2[:, :, tt * 128:(tt + 1) * 128].rearrange("c p q -> p c q"),
                     stage[:], stx)
                stage_store[tt] = stx.v
    nc.sync.wait_ge(stx.h, stx.v)
    nc.gpsimd.wait_ge(stx.h, stx.v)


def emit_dequant(nc, eng, pk, s_sb, Wb, mi, ni, ngrp):
    se = s_sb.rearrange("p (q two) -> p q two", two=2)
    eng.tensor_scalar(out=mi, in0=pk, scalar1=4, scalar2=None,
                      op0=ALU.arith_shift_right)
    eng.tensor_scalar(out=ni, in0=pk, scalar1=15, scalar2=8,
                      op0=ALU.bitwise_and, op1=ALU.bitwise_xor)
    eng.tensor_scalar(out=ni, in0=ni, scalar1=-8, scalar2=None, op0=ALU.add)
    eng.tensor_tensor(out=Wb[:, :, 0, :],
                      in0=mi.rearrange("p (q j) -> p q j", j=64),
                      in1=se[:, :, 0:1].broadcast_to([128, ngrp, 64]), op=ALU.mult)
    return eng.tensor_tensor(out=Wb[:, :, 1, :],
                             in0=ni.rearrange("p (q j) -> p q j", j=64),
                             in1=se[:, :, 1:2].broadcast_to([128, ngrp, 64]),
                             op=ALU.mult)


def emit_ffn0up(C, xn, w1, s1, w3, s3, ha):
    nc = C.nc
    xn_sb = C.carve([128, CD, T], BF16)
    pk1 = C.carve([128, CD * 64], I32)
    pk3 = C.carve([128, CD * 64], I32)
    mi1 = C.carve([128, CD * 64], I32)
    s1t = C.carve([128, 64], F32)
    s3t = C.carve([128, 64], F32)
    W1b = C.carve([128, CD, 2, 64], BF16)
    W3b = C.carve([128, CD, 2, 64], BF16)
    W1T = [C.carve([128, CD, 128], BF16) for _ in range(2)]
    W3T = [C.carve([128, CD, 128], BF16) for _ in range(2)]
    sil = C.carve([128, 512], BF16)
    hst = [C.carve([128, 512], BF16) for _ in range(2)]
    idt = C.carve([128, 128], BF16)
    psF = [C.psum(F32) for _ in range(4)]
    psX = [C.psumb() for _ in range(2)]
    ldc, ldp, dqq, trn, ev1, ev3, mmf, sth = (C.s[n] for n in
        ("ld1", "ld2", "x1", "mm", "acs", "dve", "x2", "st1"))

    _dma(nc.sync, xn_sb[:], xn[:].rearrange("c p t -> p c t"), ldc)
    _dma(nc.sync, idt[:], C.inline("identb", IDENT)[:], ldc)

    dq_done = {}
    mmfv = {}
    trn_done = {}
    bank_free = {0: None, 1: None}
    psf_free = {0: None, 1: None}
    sthv = {}
    last_mul = None
    for hb in range(HB):
        par = hb % 2
        if hb >= 1:
            nc.sync.wait_ge(dqq.h, dq_done[hb - 1])
        _dma(nc.sync, pk1[:].rearrange("p (a j) -> p a j", j=64),
             w1[hb * 4096:(hb + 1) * 4096, :].rearrange("(p a) j -> p a j", p=128), ldp)
        _dma(nc.sync, pk3[:].rearrange("p (a j) -> p a j", j=64),
             w3[hb * 4096:(hb + 1) * 4096, :].rearrange("(p a) j -> p a j", p=128), ldp)
        _dma(nc.sync, s1t[:], s1.rearrange("(h g) -> h g", g=64)[hb * 128:(hb + 1) * 128, :], ldp)
        _dma(nc.sync, s3t[:], s3.rearrange("(h g) -> h g", g=64)[hb * 128:(hb + 1) * 128, :], ldp)
        nc.vector.wait_ge(ldp.h, ldp.v)
        if hb >= 1:
            nc.vector.wait_ge(trn.h, trn_done[hb - 1])
        dqq.inc(emit_dequant(nc, nc.vector, pk1[:], s1t[:], W1b[:], mi1[:], pk1[:], CD), 1)
        dq1_v = dqq.v
        dqq.inc(emit_dequant(nc, nc.vector, pk3[:], s3t[:], W3b[:], mi1[:], pk3[:], CD), 1)
        dq_done[hb] = dqq.v
        if hb == 0:
            nc.tensor.wait_ge(ldc.h, ldc.v)
        if hb >= 2:
            nc.tensor.wait_ge(mmf.h, mmfv[hb - 2])
        for (W_b, W_T, on_act, dqv_) in ((W1b, W1T[par], True, dq1_v),
                                         (W3b, W3T[par], False, dq_done[hb])):
            nc.tensor.wait_ge(dqq.h, dqv_)
            for g in range(4):
                bk = g % 2
                bank = psX[bk]
                if bank_free[bk] is not None:
                    s_, v_ = bank_free[bk]
                    nc.tensor.wait_ge(s_.h, v_)
                for j in range(8):
                    c = g * 8 + j
                    ins = nc.tensor.transpose(
                        bank[:, j * 128:(j + 1) * 128],
                        W_b[:].rearrange("p a two j -> p (a two j)")[:, c * 128:(c + 1) * 128],
                        idt[:])
                trn.inc(ins, 1)
                dst = W_T[:, g * 8:(g + 1) * 8, :].rearrange("p c q -> p (c q)")
                if on_act:
                    nc.scalar.wait_ge(trn.h, trn.v)
                    ev1.inc(nc.scalar.copy(out=dst, in_=bank[:]), 1)
                    bank_free[bk] = (ev1, ev1.v)
                else:
                    nc.vector.wait_ge(trn.h, trn.v)
                    ev3.inc(nc.vector.tensor_copy(dst, bank[:]), 1)
                    bank_free[bk] = (ev3, ev3.v)
        trn_done[hb] = trn.v
        w_ready = (ev1.v, ev3.v)
        nc.tensor.wait_ge(ev1.h, w_ready[0])
        nc.tensor.wait_ge(ev3.h, w_ready[1])
        for t4 in range(4):
            tsl = slice(t4 * 512, (t4 + 1) * 512)
            pp = t4 % 2
            pa, pb = psF[2 * pp], psF[2 * pp + 1]
            if psf_free[pp] is not None:
                nc.tensor.wait_ge(ev1.h, psf_free[pp][0])
                nc.tensor.wait_ge(ev3.h, psf_free[pp][1])
            for c in range(CD):
                nc.tensor.matmul(pa[:], W1T[par][:, c, :], xn_sb[:, c, tsl],
                                 start=(c == 0), stop=(c == CD - 1))
            for c in range(CD):
                ins = nc.tensor.matmul(pb[:], W3T[par][:, c, :], xn_sb[:, c, tsl],
                                       start=(c == 0), stop=(c == CD - 1))
            mmf.inc(ins, 1)
            nc.scalar.wait_ge(mmf.h, mmf.v)
            if last_mul is not None:
                nc.scalar.wait_ge(ev3.h, last_mul)
            ev1.inc(nc.scalar.activation(out=sil[:], in_=pa[:], func=AF.Silu), 1)
            silu_v = ev1.v
            nc.vector.wait_ge(ev1.h, silu_v)
            it = hb * 4 + t4
            if it >= 2:
                nc.vector.wait_ge(sth.h, sthv[it - 2])
            ev3.inc(nc.vector.tensor_tensor(out=hst[pp][:], in0=sil[:],
                                            in1=pb[:], op=ALU.mult), 1)
            last_mul = ev3.v
            psf_free[pp] = (silu_v, last_mul)
            nc.gpsimd.wait_ge(ev3.h, last_mul)
            _dma(nc.gpsimd, ha[hb, :, tsl], hst[pp][:], sth)
            sthv[it] = sth.v
        mmfv[hb] = mmf.v
    nc.sync.wait_ge(sth.h, sth.v)
    nc.gpsimd.wait_ge(sth.h, sth.v)


def emit_w2t(C, w2, s2, W2T):
    nc = C.nc
    NG = HB
    pk = [C.carve([128, NG * 64], I32) for _ in range(2)]
    s2t = [C.carve([128, 2 * NG], F32) for _ in range(2)]
    mi = C.carve([128, NG * 64], I32)
    ni = C.carve([128, NG * 64], I32)
    Wb = C.carve([128, NG, 2, 64], BF16)
    stage = [C.carve([128, NG, 128], BF16) for _ in range(2)]
    idt = C.carve([128, 128], BF16)
    psX = [C.psumb() for _ in range(2)]
    ldc, ldp, dqd, trn, evc, stw = (C.s[n] for n in
                                    ("ld1", "ld2", "x1", "mm", "acs", "st1"))
    _dma(nc.sync, idt[:], C.inline("identb", IDENT)[:], ldc)
    dqv = {}
    trv = {}
    stv = {}
    bank_free = {0: None, 1: None}
    for ob in range(CD):
        par = ob % 2
        if ob >= 2:
            nc.sync.wait_ge(dqd.h, dqv[ob - 2])
        _dma(nc.sync, pk[par][:].rearrange("p (a j) -> p a j", j=64),
             w2[ob * 11008:(ob + 1) * 11008, :].rearrange("(p a) j -> p a j", p=128), ldp)
        _dma(nc.sync, s2t[par][:],
             s2.rearrange("(h g) -> h g", g=2 * NG)[ob * 128:(ob + 1) * 128, :], ldp)
        nc.vector.wait_ge(ldp.h, ldp.v)
        if ob >= 1:
            nc.vector.wait_ge(trn.h, trv[ob - 1])
        dqd.inc(emit_dequant(nc, nc.vector, pk[par][:], s2t[par][:], Wb[:],
                             mi[:], ni[:], NG), 1)
        dqv[ob] = dqd.v
        nc.tensor.wait_ge(dqd.h, dqd.v)
        if ob == 0:
            nc.tensor.wait_ge(ldc.h, ldc.v)
        for g in range((NG + 7) // 8):
            bk = g % 2
            bank = psX[bk]
            if bank_free[bk] is not None:
                nc.tensor.wait_ge(evc.h, bank_free[bk])
            nkk = min(8, NG - g * 8)
            for j in range(nkk):
                c = g * 8 + j
                ins = nc.tensor.transpose(
                    bank[:, j * 128:(j + 1) * 128],
                    Wb[:].rearrange("p a two j -> p (a two j)")[:, c * 128:(c + 1) * 128],
                    idt[:])
            trn.inc(ins, 1)
            nc.scalar.wait_ge(trn.h, trn.v)
            if g == 0 and ob >= 2:
                nc.scalar.wait_ge(stw.h, stv[ob - 2])
            evc.inc(nc.scalar.copy(
                out=stage[par][:, g * 8:g * 8 + nkk, :].rearrange("p c q -> p (c q)"),
                in_=bank[:, 0:nkk * 128]), 1)
            bank_free[bk] = evc.v
        trv[ob] = trn.v
        nc.gpsimd.wait_ge(evc.h, evc.v)
        _dma(nc.gpsimd, W2T[ob].rearrange("c p o -> p c o"), stage[par][:], stw)
        stv[ob] = stw.v
    nc.sync.wait_ge(stw.h, stw.v)
    nc.gpsimd.wait_ge(stw.h, stw.v)


def emit_ffn1(C, xn, Wup, gateT, ebias, AT, Ball, xprev, xnew):
    nc = C.nc
    xn_ts = [C.carve([128, CD, 512], BF16) for _ in range(2)]
    gA = C.carve([128, CD, 36], BF16)
    ob1 = C.carve([1, 132], BF16)
    Ball_sb = C.carve([32, D], BF16)
    zsb = C.carve([128, 16, 32], F32)
    lgsb = C.carve([128, 16, 4], F32)
    lg2 = C.carve([128, 16, 4], F32)
    mx = C.carve([128, 16], F32)
    mx2 = C.carve([128, 16], F32)
    oh = C.carve([128, 16, 4], F32)
    zmb = C.carve([128, 16, 32], BF16)
    zmT = C.carve([32, T], BF16)
    wub = [C.carve([128, CD, 128], BF16) for _ in range(2)]
    wubT = C.carve([128, CD, 128], BF16)
    yf1 = C.carve([128, 512], F32)
    y4 = C.carve([128, 4, CD, 128], F32)
    xp = C.carve([128, D], F32)
    xnb = C.carve([128, D], BF16)
    st4 = C.carve([128, 64], F32)
    idtf = C.carve([128, 128], F32)
    idtb = C.carve([128, 128], BF16)
    psY = [C.psum(F32) for _ in range(2)]
    psT = [C.psum(F32) for _ in range(2)]
    psZ = C.psum(F32)
    psL = C.psum(F32)[:, 0:64]
    psB2 = [C.psumb() for _ in range(2)]
    lda, wld, mm, dev, acs, pml, stx = (C.s[n] for n in
                                        ("ld1", "wld", "mm", "dve", "acs", "pml", "st1"))
    wld0 = wld.v

    _dma(nc.sync, idtf[:], C.inline("identf", IDENTF)[:], lda)
    _dma(nc.sync, idtb[:], C.inline("identb", IDENT)[:], lda)
    _dma(nc.gpsimd, ob1[:, 0:128], C.inline("onesb", np.ones((1, 128), dtype=BF))[:], lda)
    _dma(nc.gpsimd, ob1[:, 128:132], ebias[:], lda)
    _dma(nc.gpsimd, gA[:, :, 0:4], gateT[:].rearrange("(c p) e -> p c e", p=128), lda)
    _dma(nc.gpsimd, gA[:, :, 4:36], AT[:].rearrange("(c p) e -> p c e", p=128), lda)
    _dma(nc.gpsimd, Ball_sb[:], Ball[:], lda)

    pre_mm = {}
    for ts in range(4):
        tsl = slice(ts * 512, (ts + 1) * 512)
        buf = xn_ts[ts % 2]
        if ts >= 2:
            nc.sync.wait_ge(mm.h, pre_mm[ts - 2])
        _dma(nc.sync, buf[:], xn[:, :, tsl].rearrange("c p t -> p c t"), lda)
        nc.tensor.wait_ge(lda.h, lda.v)
        for tt4 in range(4):
            tt = ts * 4 + tt4
            sl = slice(tt4 * 128, (tt4 + 1) * 128)
            for c in range(CD):
                nc.tensor.matmul(psL[:, tt * 4:(tt + 1) * 4],
                                 buf[:, c, sl], gA[:, c, 0:4],
                                 start=(c == 0), stop=False)
            nc.tensor.matmul(psL[:, tt * 4:(tt + 1) * 4],
                             ob1[:, 0:128], ob1[:, 128:132],
                             start=False, stop=True)
            for c in range(CD):
                ins = nc.tensor.matmul(psZ[:, tt * 32:(tt + 1) * 32],
                                       buf[:, c, sl], gA[:, c, 4:36],
                                       start=(c == 0), stop=(c == CD - 1))
            mm.inc(ins, 1)
        pre_mm[ts] = mm.v
    nc.vector.wait_ge(mm.h, mm.v)
    nc.vector.tensor_copy(lgsb[:], psL[:].rearrange("p (t e) -> p t e", e=4))
    dev.inc(nc.vector.tensor_copy(zsb[:], psZ[:].rearrange("p (t e) -> p t e", e=32)), 1)
    nc.gpsimd.wait_ge(dev.h, dev.v)
    pml.inc(nc.gpsimd.tensor_copy(lg2[:], lgsb[:]), 1)
    nc.vector.wait_ge(pml.h, pml.v)
    dev.inc(nc.vector.tensor_reduce(mx[:].unsqueeze(2), lg2[:], AX, ALU.max), 1)
    nc.gpsimd.wait_ge(dev.h, dev.v)
    pml.inc(nc.gpsimd.tensor_copy(mx2[:], mx[:]), 1)
    nc.vector.wait_ge(pml.h, pml.v)
    dev.inc(nc.vector.tensor_tensor(
        out=oh[:], in0=lg2[:],
        in1=mx2[:].unsqueeze(2).broadcast_to([128, 16, 4]), op=ALU.is_ge), 1)
    nc.gpsimd.wait_ge(dev.h, dev.v)
    pml.inc(nc.gpsimd.tensor_tensor(
        out=zmb[:].rearrange("p t (e r) -> p t e r", e=4),
        in0=zsb[:].rearrange("p t (e r) -> p t e r", e=4),
        in1=oh[:].unsqueeze(3).broadcast_to([128, 16, 4, 8]), op=ALU.mult), 1)
    nc.tensor.wait_ge(pml.h, pml.v)
    zmt_ev = {}
    for g in range(4):
        bank = psB2[g % 2]
        if g >= 2:
            nc.tensor.wait_ge(dev.h, zmt_ev[g - 2])
        for j in range(4):
            tt = g * 4 + j
            ins = nc.tensor.transpose(bank[0:32, j * 128:(j + 1) * 128],
                                      zmb[:, tt, :], idtb[:])
        mm.inc(ins, 1)
        nc.vector.wait_ge(mm.h, mm.v)
        dev.inc(nc.vector.tensor_copy(zmT[:, g * 512:(g + 1) * 512],
                                      bank[0:32, 0:512]), 1)
        zmt_ev[g] = dev.v

    nwld = 0
    mm_oc = {}
    yf_ev = {}
    trn_ev = {}
    dev_ev = {}
    wup_tr = {}
    wupT_free = {0: zmt_ev.get(2), 1: zmt_ev.get(3)}
    xp_store1 = {}
    pre_done_mm = mm.v
    for ts in range(4):
        tsl = slice(ts * 512, (ts + 1) * 512)
        buf = xn_ts[ts % 2]
        if ts >= 2:
            nc.sync.wait_ge(mm.h, mm_oc[(ts - 1, CD - 1)])
        else:
            nc.sync.wait_ge(mm.h, pre_done_mm)
        _dma(nc.sync, buf[:], xn[:, :, tsl].rearrange("c p t -> p c t"), lda)
        for oc in range(CD):
            par = oc % 2
            wb = wub[par]
            if (ts, oc - 2) in wup_tr:
                nc.gpsimd.wait_ge(mm.h, wup_tr[(ts, oc - 2)])
            elif ts >= 1 and oc < 2:
                nc.gpsimd.wait_ge(mm.h, wup_tr[(ts - 1, CD - 2 + oc)])
            _dma(nc.gpsimd, wb[:],
                 Wup[oc * 128:(oc + 1) * 128, :].rearrange("o (c p) -> o c p", p=128),
                 wld)
            nwld += 1
            nc.tensor.wait_ge(wld.h, wld0 + 16 * nwld)
            nc.tensor.wait_ge(lda.h, lda.v)
            if (ts, oc - 1) in mm_oc:
                nc.tensor.wait_ge(mm.h, mm_oc[(ts, oc - 1)])
            elif ts >= 1 and oc == 0:
                nc.tensor.wait_ge(mm.h, mm_oc[(ts - 1, CD - 1)])
            for g in range(4):
                bk = g % 2
                bank = psB2[bk]
                if wupT_free.get(bk) is not None:
                    nc.tensor.wait_ge(dev.h, wupT_free[bk])
                for j in range(8):
                    c = g * 8 + j
                    ins = nc.tensor.transpose(bank[:, j * 128:(j + 1) * 128],
                                              wb[:, c, :], idtb[:])
                mm.inc(ins, 1)
                nc.vector.wait_ge(mm.h, mm.v)
                dev.inc(nc.vector.tensor_copy(
                    wubT[:, g * 8:(g + 1) * 8, :].rearrange("p c q -> p (c q)"),
                    bank[:]), 1)
                wupT_free[bk] = dev.v
            wup_tr[(ts, oc)] = mm.v
            nc.tensor.wait_ge(dev.h, dev.v)
            if (ts, oc - 2) in yf_ev:
                nc.tensor.wait_ge(acs.h, yf_ev[(ts, oc - 2)])
            elif ts >= 1 and oc < 2:
                nc.tensor.wait_ge(acs.h, yf_ev[(ts - 1, CD - 2 + oc)])
            for c in range(CD):
                nc.tensor.matmul(psY[par][:], wubT[:, c, :], buf[:, c, :],
                                 start=(c == 0), stop=False)
            ins = nc.tensor.matmul(psY[par][:], Ball_sb[0:32, oc * 128:(oc + 1) * 128],
                                   zmT[0:32, tsl], start=False, stop=True)
            mm.inc(ins, 1)
            mm_oc[(ts, oc)] = mm.v
            nc.scalar.wait_ge(mm.h, mm.v)
            if (ts, oc - 1) in trn_ev:
                nc.scalar.wait_ge(mm.h, trn_ev[(ts, oc - 1)])
            acs.inc(nc.scalar.copy(out=yf1[:], in_=psY[par][:]), 1)
            yf_ev[(ts, oc)] = acs.v
            nc.tensor.wait_ge(acs.h, acs.v)
            if (ts, oc - 2) in dev_ev:
                nc.tensor.wait_ge(dev.h, dev_ev[(ts, oc - 2)])
            for a in range(4):
                ins = nc.tensor.transpose(psT[par][:, a * 128:(a + 1) * 128],
                                          yf1[:, a * 128:(a + 1) * 128], idtf[:])
            mm.inc(ins, 1)
            trn_ev[(ts, oc)] = mm.v
            nc.vector.wait_ge(mm.h, mm.v)
            dev.inc(nc.vector.tensor_copy(y4[:, :, oc, :], psT[par][:]
                                          .rearrange("p (a q) -> p a q", a=4)), 1)
            dev_ev[(ts, oc)] = dev.v
        yav = [y4[:, a, :, :].rearrange("p c q -> p (c q)") for a in range(4)]
        nc.vector.wait_ge(dev.h, dev.v)
        parts1 = []
        for a in range(4):
            m_, r_, last_, fns = emit_ln_stats(nc, yav[a], xnb[:], st4[:, 16 * a:16 * a + 8])
            parts1.append((m_, r_, fns))
        dev.inc(last_, 1)
        nc.scalar.wait_ge(dev.h, dev.v)
        for a in range(4):
            ins_ = parts1[a][2][0]()
        acs.inc(ins_, 1)
        nc.vector.wait_ge(acs.h, acs.v)
        for a in range(4):
            ins_ = parts1[a][2][1]()
        dev.inc(ins_, 1)
        nc.scalar.wait_ge(dev.h, dev.v)
        for a in range(4):
            ins_ = parts1[a][2][2]()
        acs.inc(ins_, 1)
        nc.vector.wait_ge(acs.h, acs.v)
        for a in range(4):
            tt = ts * 4 + a
            m_, r_, _ = parts1[a]
            nc.vector.tensor_scalar(out=yav[a], in0=yav[a], scalar1=m_, scalar2=r_,
                                    op0=ALU.subtract, op1=ALU.mult)
            if tt >= 1:
                nc.sync.wait_ge(dev.h, dev.v)
                nc.sync.wait_ge(stx.h, xp_store1[tt - 1])
            _dma(nc.sync, xp[:], xprev[tt * 128:(tt + 1) * 128, :], lda)
            nc.vector.wait_ge(lda.h, lda.v)
            dev.inc(nc.vector.tensor_tensor(out=yav[a], in0=xp[:], in1=yav[a],
                                            op=ALU.add), 1)
            xp_store1[tt] = stx.v + 16
            nc.gpsimd.wait_ge(dev.h, dev.v)
            _dma(nc.gpsimd, xnew[tt * 128:(tt + 1) * 128, :], yav[a], stx)
    nc.sync.wait_ge(stx.h, stx.v)
    nc.gpsimd.wait_ge(stx.h, stx.v)



def _newC():
    C = Ctx()
    C.setup()
    C.section()
    return C


def build_ln0():
    C = _newC()
    nc = C.nc
    x = nc.dram_tensor("x", [T, D], F32, kind="ExternalInput").ap()
    xn = nc.dram_tensor("xn", [CD, 128, T], BF16, kind="ExternalOutput").ap()
    emit_ln0(C, x, xn)
    return C.finish()


def build_qkv(layer=0):
    C = _newC()
    nc = C.nc
    xn = nc.dram_tensor("xn", [CD, 128, T], BF16, kind="ExternalInput").ap()
    Wq = nc.dram_tensor("Wq", [D, D], F32, kind="ExternalInput").ap()
    Wk = nc.dram_tensor("Wk", [D, HKV * DH], F32, kind="ExternalInput").ap()
    Wv = nc.dram_tensor("Wv", [D, HKV * DH], F32, kind="ExternalInput").ap()
    cq = nc.dram_tensor("cq", [128, T], F32, kind="ExternalInput").ap()
    sq = nc.dram_tensor("sq", [128, T], F32, kind="ExternalInput").ap()
    ck = nc.dram_tensor("ck", [128, T], F32, kind="ExternalInput").ap()
    sk = nc.dram_tensor("sk", [128, T], F32, kind="ExternalInput").ap()
    Qo = nc.dram_tensor("Q", [HQ, 128, T], BF16, kind="ExternalOutput").ap()
    Ko = nc.dram_tensor("K", [HKV, 128, T], BF16, kind="ExternalOutput").ap()
    Vo = nc.dram_tensor("V", [T, HKV * DH], BF16, kind="ExternalOutput").ap()
    emit_qkv(C, xn, Wq, Wk, Wv, cq, sq, ck, sk, Qo, Ko, Vo)
    return C.finish()


def build_attn():
    C = _newC()
    nc = C.nc
    Qi = nc.dram_tensor("Q", [HQ, 128, T], BF16, kind="ExternalInput").ap()
    Ko = nc.dram_tensor("Kown", [HKV, 128, T], BF16, kind="ExternalInput").ap()
    Kp = nc.dram_tensor("Kpart", [HKV, 128, T], BF16, kind="ExternalInput").ap()
    Vo_ = nc.dram_tensor("Vown", [T, HKV * DH], BF16, kind="ExternalInput").ap()
    Vp = nc.dram_tensor("Vpart", [T, HKV * DH], BF16, kind="ExternalInput").ap()
    Mi = nc.dram_tensor("masks", [NTT, 128, 512], BF16, kind="ExternalInput").ap()
    Co = nc.dram_tensor("ctx", [HQ, 128, T], BF16, kind="ExternalOutput").ap()
    emit_attn(C, Qi, Ko, Kp, Vo_, Vp, Mi, Co)
    return C.finish()


def build_combine(kind):
    C = _newC()
    nc = C.nc
    nK = CD if kind == "wo" else HB
    A = nc.dram_tensor("A", [nK, 128, T], BF16, kind="ExternalInput").ap()
    if kind == "wo":
        W = nc.dram_tensor("W", [D, D], F32, kind="ExternalInput").ap()
    else:
        W = nc.dram_tensor("W", [CD, HB, 128, 128], BF16, kind="ExternalInput").ap()
    xprev = nc.dram_tensor("xprev", [T, D], F32, kind="ExternalInput").ap()
    xnew = nc.dram_tensor("xnew", [T, D], F32, kind="ExternalOutput").ap()
    xn2 = nc.dram_tensor("xn2", [CD, 128, T], BF16, kind="ExternalOutput").ap()
    emit_combine(C, kind, A, W, xprev, xnew, xn2)
    return C.finish()


def build_ffn0up():
    C = _newC()
    nc = C.nc
    xn = nc.dram_tensor("xn", [CD, 128, T], BF16, kind="ExternalInput").ap()
    w1 = nc.dram_tensor("w1", [352256, 64], I32, kind="ExternalInput").ap()
    s1 = nc.dram_tensor("s1", [704512], F32, kind="ExternalInput").ap()
    w3 = nc.dram_tensor("w3", [352256, 64], I32, kind="ExternalInput").ap()
    s3 = nc.dram_tensor("s3", [704512], F32, kind="ExternalInput").ap()
    ha = nc.dram_tensor("ha", [HB, 128, T], BF16, kind="ExternalOutput").ap()
    emit_ffn0up(C, xn, w1, s1, w3, s3, ha)
    return C.finish()


def build_w2t():
    C = _newC()
    nc = C.nc
    w2 = nc.dram_tensor("w2", [352256, 64], I32, kind="ExternalInput").ap()
    s2 = nc.dram_tensor("s2", [704512], F32, kind="ExternalInput").ap()
    W2T = nc.dram_tensor("W2T", [CD, HB, 128, 128], BF16, kind="ExternalOutput").ap()
    emit_w2t(C, w2, s2, W2T)
    return C.finish()


def build_ffn1():
    C = _newC()
    nc = C.nc
    xn = nc.dram_tensor("xn", [CD, 128, T], BF16, kind="ExternalInput").ap()
    Wup = nc.dram_tensor("Wup", [D, D], F32, kind="ExternalInput").ap()
    gateT = nc.dram_tensor("gateT", [D, 4], F32, kind="ExternalInput").ap()
    ebias = nc.dram_tensor("ebias", [1, 4], F32, kind="ExternalInput").ap()
    AT = nc.dram_tensor("AT", [D, 32], F32, kind="ExternalInput").ap()
    Ball = nc.dram_tensor("Ball", [32, D], F32, kind="ExternalInput").ap()
    xprev = nc.dram_tensor("xprev", [T, D], F32, kind="ExternalInput").ap()
    xnew = nc.dram_tensor("xnew", [T, D], F32, kind="ExternalOutput").ap()
    emit_ffn1(C, xn, Wup, gateT, ebias, AT, Ball, xprev, xnew)
    return C.finish()


def build_megaA():
    C = _newC()
    nc = C.nc
    EI = lambda n, s, d: nc.dram_tensor(n, s, d, kind="ExternalInput").ap()
    EO = lambda n, s, d: nc.dram_tensor(n, s, d, kind="ExternalOutput").ap()
    IN = lambda n, s, d: nc.dram_tensor(n, s, d, kind="Internal").ap()
    x = EI("x", [T, D], F32)
    xpart = EI("xpart", [T, D], F32)
    Wq0 = EI("Wq0", [D, D], F32)
    Wk0 = EI("Wk0", [D, HKV * DH], F32)
    Wv0 = EI("Wv0", [D, HKV * DH], F32)
    Wo0 = EI("Wo0", [D, D], F32)
    w1 = EI("w1", [352256, 64], I32)
    s1 = EI("s1", [704512], F32)
    w2 = EI("w2", [352256, 64], I32)
    s2 = EI("s2", [704512], F32)
    w3 = EI("w3", [352256, 64], I32)
    s3 = EI("s3", [704512], F32)
    Wq1 = EI("Wq1", [D, D], F32)
    Wk1 = EI("Wk1", [D, HKV * DH], F32)
    Wv1 = EI("Wv1", [D, HKV * DH], F32)
    cq0 = EI("cq0", [128, T], F32)
    sq0 = EI("sq0", [128, T], F32)
    ck0 = EI("ck0", [128, T], F32)
    sk0 = EI("sk0", [128, T], F32)
    ck0p = EI("ck0p", [128, T], F32)
    sk0p = EI("sk0p", [128, T], F32)
    cq1 = EI("cq1", [128, T], F32)
    sq1 = EI("sq1", [128, T], F32)
    ck1 = EI("ck1", [128, T], F32)
    sk1 = EI("sk1", [128, T], F32)
    masks = EI("masks", [NTT, 128, 512], BF16)
    x2o = EO("x2", [T, D], F32)
    Q1o = EO("Q1", [HQ, 128, T], BF16)
    K1o = EO("K1", [HKV, 128, T], BF16)
    V1o = EO("V1", [T, HKV * DH], BF16)

    xn0 = IN("xn0", [CD, 128, T], BF16)
    xn0p = IN("xn0p", [CD, 128, T], BF16)
    Q0 = IN("Q0", [HQ, 128, T], BF16)
    K0 = IN("K0", [HKV, 128, T], BF16)
    V0 = IN("V0", [T, HKV * DH], BF16)
    K0p = IN("K0p", [HKV, 128, T], BF16)
    V0p = IN("V0p", [T, HKV * DH], BF16)
    ctx0 = IN("ctx0", [HQ, 128, T], BF16)
    x1 = IN("x1", [T, D], F32)
    xnf = IN("xnf", [CD, 128, T], BF16)
    ha = IN("ha", [HB, 128, T], BF16)
    W2T = IN("W2T", [CD, HB, 128, 128], BF16)
    x2i = IN("x2i", [T, D], F32)
    xn3 = IN("xn3", [CD, 128, T], BF16)

    emit_ln0(C, x, xn0)
    C.join(); C.section()
    emit_ln0(C, xpart, xn0p)
    C.join(); C.section()
    emit_qkv(C, xn0, Wq0, Wk0, Wv0, cq0, sq0, ck0, sk0, Q0, K0, V0)
    C.join(); C.section()
    emit_qkv(C, xn0p, None, Wk0, Wv0, None, None, ck0p, sk0p, None, K0p, V0p)
    C.join(); C.section()
    emit_attn(C, Q0, K0, K0p, V0, V0p, masks, ctx0)
    C.join(); C.section()
    emit_combine(C, "wo", ctx0, Wo0, x, x1, xnf)
    C.join(); C.section()
    emit_w2t(C, w2, s2, W2T)
    C.join(); C.section()
    emit_ffn0up(C, xnf, w1, s1, w3, s3, ha)
    C.join(); C.section()
    emit_combine(C, "ffn0down", ha, W2T, x1, x2o, xn3)
    C.join(); C.section()
    emit_qkv(C, xn3, Wq1, Wk1, Wv1, cq1, sq1, ck1, sk1, Q1o, K1o, V1o)
    return C.finish()


def build_megaB():
    C = _newC()
    nc = C.nc
    EI = lambda n, s, d: nc.dram_tensor(n, s, d, kind="ExternalInput").ap()
    EO = lambda n, s, d: nc.dram_tensor(n, s, d, kind="ExternalOutput").ap()
    IN = lambda n, s, d: nc.dram_tensor(n, s, d, kind="Internal").ap()
    x2 = EI("x2", [T, D], F32)
    Q1 = EI("Q1", [HQ, 128, T], BF16)
    K1 = EI("K1", [HKV, 128, T], BF16)
    K1p = EI("K1p", [HKV, 128, T], BF16)
    V1 = EI("V1", [T, HKV * DH], BF16)
    V1p = EI("V1p", [T, HKV * DH], BF16)
    masks = EI("masks", [NTT, 128, 512], BF16)
    Wo1 = EI("Wo1", [D, D], F32)
    Wup = EI("Wup", [D, D], F32)
    gateT = EI("gateT", [D, 4], F32)
    ebias = EI("ebias", [1, 4], F32)
    AT = EI("AT", [D, 32], F32)
    Ball = EI("Ball", [32, D], F32)
    x4o = EO("x4", [T, D], F32)
    ctx1 = IN("ctx1", [HQ, 128, T], BF16)
    x3 = IN("x3", [T, D], F32)
    xn4 = IN("xn4", [CD, 128, T], BF16)

    emit_attn(C, Q1, K1, K1p, V1, V1p, masks, ctx1)
    C.join(); C.section()
    emit_combine(C, "wo", ctx1, Wo1, x2, x3, xn4)
    C.join(); C.section()
    emit_ffn1(C, xn4, Wup, gateT, ebias, AT, Ball, x3, x4o)
    return C.finish()



_PROGS = {}


def _prog(name, fn):
    if name not in _PROGS:
        _PROGS[name] = fn()
    return _PROGS[name]


def _run(name, fn, in_maps):
    nc = _prog(name, fn)
    return run_bass_kernel_spmd(nc, in_maps, core_ids=list(range(8))).results


def kernel(**inputs):
    ins = {k: np.asarray(v) for k, v in inputs.items()}
    x0 = ins["L0_attn_resid"]
    NCORE = 8
    roles = [c % 2 for c in range(NCORE)]
    batches = [c // 2 for c in range(NCORE)]
    pos = {r: tok_positions(r) for r in (0, 1)}
    masks = {r: diag_masks(r) for r in (0, 1)}
    sc = 1.0 / np.sqrt(128.0)
    tabs = {}
    for layer in (0, 1):
        for r in (0, 1):
            Ct, St = rope_tables(layer, r)
            tabs[(layer, r)] = (Ct, St)

    gateT = np.ascontiguousarray(ins["L1_ffn_gate_w"].T)
    AT = np.ascontiguousarray(ins["L1_ffn_lora_A"].transpose(2, 0, 1).reshape(D, 32))
    Ball = np.ascontiguousarray(ins["L1_ffn_lora_B"].transpose(0, 2, 1).reshape(32, D))
    ebias = np.ascontiguousarray(ins["L1_ffn_expert_bias"].reshape(1, 4))

    in_maps_A = []
    for c in range(NCORE):
        r, b = roles[c], batches[c]
        C0, S0 = tabs[(0, r)]
        C0p, S0p = tabs[(0, 1 - r)]
        C1, S1 = tabs[(1, r)]
        in_maps_A.append({
            "x": np.ascontiguousarray(x0[b][pos[r]]),
            "xpart": np.ascontiguousarray(x0[b][pos[1 - r]]),
            "Wq0": ins["L0_attn_Wq"], "Wk0": ins["L0_attn_Wk"],
            "Wv0": ins["L0_attn_Wv"], "Wo0": ins["L0_attn_Wo"],
            "w1": ins["L0_ffn_w1"], "s1": ins["L0_ffn_s1"],
            "w2": ins["L0_ffn_w2"], "s2": ins["L0_ffn_s2"],
            "w3": ins["L0_ffn_w3"], "s3": ins["L0_ffn_s3"],
            "Wq1": ins["L1_attn_Wq"], "Wk1": ins["L1_attn_Wk"],
            "Wv1": ins["L1_attn_Wv"],
            "cq0": C0 * sc, "sq0": S0 * sc, "ck0": C0, "sk0": S0,
            "ck0p": C0p, "sk0p": S0p,
            "cq1": C1 * sc, "sq1": S1 * sc, "ck1": C1, "sk1": S1,
            "masks": masks[r],
        })
    resA = _run("megaA", build_megaA, in_maps_A)

    in_maps_B = []
    for c in range(NCORE):
        r, b = roles[c], batches[c]
        partner = 2 * b + (1 - r)
        in_maps_B.append({
            "x2": resA[c]["x2"],
            "Q1": resA[c]["Q1"],
            "K1": resA[c]["K1"], "K1p": resA[partner]["K1"],
            "V1": resA[c]["V1"], "V1p": resA[partner]["V1"],
            "masks": masks[r],
            "Wo1": ins["L1_attn_Wo"],
            "Wup": ins["L1_ffn_W_up"], "gateT": gateT, "ebias": ebias,
            "AT": AT, "Ball": Ball,
        })
    resB = _run("megaB", build_megaB, in_maps_B)

    out = np.empty((4, TG, D), np.float32)
    for c in range(NCORE):
        out[batches[c]][pos[roles[c]]] = resB[c]["x4"]
    return out


# version 07
# speedup vs baseline: 1.2722x; 1.0728x over previous
import math
import contextlib
import numpy as np
import ml_dtypes
import concourse.bass as bass
import concourse.mybir as mybir
from concourse.bass_utils import run_bass_kernel_spmd

F32 = mybir.dt.float32
BF16 = mybir.dt.bfloat16
I32 = mybir.dt.int32
U8 = mybir.dt.uint8
AF = mybir.ActivationFunctionType
ALU = mybir.AluOpType
AX = mybir.AxisListType.X
BF = ml_dtypes.bfloat16

D = 4096
T = 2048
TG = 4096
NTT = T // 128
CD = D // 128
HQ, HKV, DH = 32, 8, 128
HID = 11008
HB = HID // 128
EPS = 1e-5
NEG = -1e30
DTS = {F32: 4, BF16: 2, I32: 4, U8: 1}
POOLB = 212736

SEMNAMES = ["ld1", "ld2", "wld", "mm", "dve", "acs", "pml", "st1", "st2",
            "x1", "x2", "x3", "bar"]

IDENT = np.eye(128).astype(BF)
IDENTF = np.eye(128, dtype=np.float32)


class Sem:
    def __init__(self, es, nc, name):
        self.h = es.enter_context(nc.semaphore(name))
        self.v = 0

    def inc(self, instr, n=1):
        instr.then_inc(self.h, n)
        self.v += n
        return instr


def _dma(q, out, in_, sem):
    sem.inc(q.dma_start(out=out, in_=in_), 16)


class Ctx:

    def __init__(self):
        self.nc = bass.Bass("TRN2", target_bir_lowering=False)
        self.es = contextlib.ExitStack()
        self._off = 0
        self._psum_i = 0
        self._inline = {}

    def setup(self):
        nc = self.nc
        self.pool = self.es.enter_context(nc.sbuf_tensor("pool", [128, POOLB], U8))
        self.banks = [self.es.enter_context(nc.psum_tensor(f"pp{i}", [128, 512], F32))
                      for i in range(8)]
        self.s = {n: Sem(self.es, nc, n) for n in SEMNAMES}
        self.es.enter_context(nc.Block())

    def inline(self, name, arr):
        if name not in self._inline:
            self._inline[name] = self.nc.inline_tensor(arr, name).ap()
        return self._inline[name]

    def section(self):
        self._off = 0
        self._psum_i = 0

    def carve(self, shape, dtype):
        nb = int(np.prod(shape[1:])) * DTS[dtype]
        off = (self._off + 63) & ~63
        self._off = off + nb
        assert self._off <= POOLB, f"pool overflow: {self._off}"
        v = self.pool[0:shape[0], off:off + nb].bitcast(dtype)
        if len(shape) > 2:
            dims = " ".join(f"d{i}" for i in range(1, len(shape)))
            kw = {f"d{i}": shape[i] for i in range(1, len(shape))}
            v = v.rearrange(f"p ({dims}) -> p {dims}", **kw)
        return v

    def psum(self, dtype=F32):
        t = self.banks[self._psum_i]
        self._psum_i += 1
        if dtype == F32:
            return t[:]
        return t[:].bitcast(dtype)[:, 0:512]

    def psumb(self):
        t = self.banks[self._psum_i]
        self._psum_i += 1
        return t[:].bitcast(BF16)

    def join(self):
        nc = self.nc
        b = self.s["bar"]
        engs = [nc.tensor, nc.vector, nc.scalar, nc.gpsimd, nc.sync]
        for e in engs:
            e.nop()
            b.inc(e.nop(), 1)
        for e in engs:
            e.wait_ge(b.h, b.v)

    def finish(self):
        self.es.close()
        return self.nc


def tok_positions(role):
    return np.concatenate([np.arange(256 * i + 128 * role, 256 * i + 128 * (role + 1))
                           for i in range(NTT)])


def key_globals(role):
    g = np.empty(TG, np.int64)
    for u in range(16):
        g[256 * u:256 * u + 128] = 256 * u + 128 * role + np.arange(128)
        g[256 * u + 128:256 * u + 256] = 256 * u + 128 * (1 - role) + np.arange(128)
    return g


def diag_masks(role):
    kg = key_globals(role)
    m = np.zeros((NTT, 128, 512), np.float32)
    for ti in range(NTT):
        g0 = 256 * ti + 128 * role
        nk5 = ti // 2 + 1
        k0 = 512 * (nk5 - 1)
        kj = kg[k0:k0 + 512][None, :]
        qi = np.arange(128)[:, None]
        m[ti] = np.where(kj <= g0 + qi, 0.0, NEG)
    return m.astype(BF)


def rope_tables(layer, role):
    pos = tok_positions(role).astype(np.float64)
    return rope_tables_pos(layer, pos)


def rope_tables_pos(layer, pos):
    if layer == 0:
        theta = 500000.0
        idx = np.arange(0, DH, 2, dtype=np.float64) / DH
        freq = 1.0 / (2.0 * math.pi) * theta ** (-idx)
        factor, lo, hi, L0 = 32.0, 1.0, 4.0, 8192.0
        freq_low, freq_high = lo / L0, hi / L0
        freq_scaled = np.where(freq < freq_low, freq / factor, freq)
        smooth = np.clip((L0 * freq - lo) / (hi - lo), 0.0, 1.0)
        freq_smooth = (1.0 - smooth) * (freq / factor) + smooth * freq
        is_mid = (freq >= freq_low) & (freq <= freq_high)
        freq = np.where(is_mid, freq_smooth, freq_scaled)
        phase = 2.0 * math.pi * pos[:, None] * freq[None, :]
    else:
        theta = 10000.0
        inv = 1.0 / theta ** (np.arange(0, DH, 2, dtype=np.float64)[: DH // 2] / DH)
        phase = pos[:, None] * inv[None, :]
    cos = np.cos(phase)
    sin = np.sin(phase)
    Ct = np.empty((128, T), np.float64)
    St = np.empty((128, T), np.float64)
    Ct[:64] = cos.T
    Ct[64:] = cos.T
    St[:64] = -sin.T
    St[64:] = sin.T
    return Ct.astype(np.float32), St.astype(np.float32)


def emit_ln_stats(nc, xt, xsq, st):
    sx, sxx, m, r, m2, sxxd, ve = (st[:, i:i + 1] for i in range(7))
    nc.vector.tensor_scalar(out=xsq, in0=xt, scalar1=1.0, scalar2=0.0,
                            op0=ALU.mult, op1=ALU.add, accum_out=sx)
    last = nc.vector.scalar_tensor_tensor(out=xsq, in0=xt, scalar=1.0, in1=xt,
                                          op0=ALU.mult, op1=ALU.mult, accum_out=sxx)

    def act1(d=D):
        nc.scalar.activation(out=m2, in_=sx, func=AF.Square, scale=1.0 / d)
        nc.scalar.activation(out=sxxd, in_=sxx, func=AF.Copy, scale=1.0 / d, bias=EPS)
        return nc.scalar.activation(out=m, in_=sx, func=AF.Copy, scale=1.0 / d)

    def vec2():
        return nc.vector.tensor_tensor(out=ve, in0=sxxd, in1=m2, op=ALU.subtract)

    def act2():
        nc.scalar.activation(out=ve, in_=ve, func=AF.Ln)
        return nc.scalar.activation(out=r, in_=ve, func=AF.Exp, scale=-0.5)

    return m, r, last, (act1, vec2, act2)



def emit_ln0(C, x, xn):
    nc = C.nc
    xt = [C.carve([128, D], F32) for _ in range(2)]
    xb = C.carve([128, D], BF16)
    xsq = C.carve([128, D], F32)
    st = C.carve([128, 8], F32)
    idt = C.carve([128, 128], BF16)
    stage = [C.carve([128, CD, 128], BF16) for _ in range(2)]
    pst = [C.psumb() for _ in range(2)]
    ldx, cmp, act, trn, evc, stx = (C.s[n] for n in
                                    ("ld1", "x1", "acs", "mm", "x2", "st1"))
    trn0, evc0, stx0 = trn.v, evc.v, stx.v
    _dma(nc.sync, idt[:], C.inline("identb", IDENT)[:], ldx)
    applied_v = {}
    for tt in range(NTT):
        b = tt % 2
        if tt >= 2:
            nc.sync.wait_ge(cmp.h, applied_v[tt - 2])
        _dma(nc.sync, xt[b][:], x[tt * 128:(tt + 1) * 128, :], ldx)
        nc.vector.wait_ge(ldx.h, ldx.v)
        if tt >= 1:
            nc.vector.wait_ge(trn.h, trn0 + 32 * tt)
        m, r, last, (a1, v2, a3) = emit_ln_stats(nc, xt[b][:], xsq[:], st[:])
        cmp.inc(last, 1)
        nc.scalar.wait_ge(cmp.h, cmp.v)
        act.inc(a1(), 1)
        nc.vector.wait_ge(act.h, act.v)
        cmp.inc(v2(), 1)
        nc.scalar.wait_ge(cmp.h, cmp.v)
        act.inc(a3(), 1)
        nc.vector.wait_ge(act.h, act.v)
        cmp.inc(nc.vector.tensor_scalar(out=xb[:], in0=xt[b][:], scalar1=m, scalar2=r,
                                        op0=ALU.subtract, op1=ALU.mult), 1)
        applied_v[tt] = cmp.v
        nc.tensor.wait_ge(cmp.h, cmp.v)
        if tt >= 2:
            nc.scalar.wait_ge(stx.h, stx0 + 16 * (tt - 1))
        for g in range(4):
            bank = pst[g % 2]
            need = 4 * tt + g - 1
            if need > 0:
                nc.tensor.wait_ge(evc.h, evc0 + need)
            for j in range(8):
                c = g * 8 + j
                trn.inc(nc.tensor.transpose(bank[:, j * 128:(j + 1) * 128],
                                            xb[:, c * 128:(c + 1) * 128], idt[:]), 1)
            nc.scalar.wait_ge(trn.h, trn.v)
            evc.inc(nc.scalar.copy(
                out=stage[b][:, g * 8:(g + 1) * 8, :].rearrange("p c q -> p (c q)"),
                in_=bank[:]), 1)
        nc.gpsimd.wait_ge(evc.h, evc.v)
        _dma(nc.gpsimd,
             xn[:, :, tt * 128:(tt + 1) * 128].rearrange("c p q -> p c q"),
             stage[b][:], stx)
    nc.sync.wait_ge(stx.h, stx.v)
    nc.gpsimd.wait_ge(stx.h, stx.v)


def emit_qkv(C, xn, Wq, Wk, Wv, cq, sq, ck, sk, Qo, Ko, Vo):
    nc = C.nc
    xn_sb = C.carve([128, CD, T], BF16)
    tabs = {n: C.carve([128, T], F32) for n in ("cq", "sq", "ck", "sk")}
    wbuf = [C.carve([128, CD, 128], BF16) for _ in range(2)]
    rp1 = [C.carve([128, 512], F32) for _ in range(2)]
    prt = [C.carve([128, 512], F32) for _ in range(2)]
    rp2 = [C.carve([128, 512], F32) for _ in range(2)]
    qstage = [C.carve([128, 512], BF16) for _ in range(2)]
    vf = C.carve([128, T], F32)
    idtf = C.carve([128, 128], F32)
    vts = [C.carve([128, 4, 128], BF16) for _ in range(2)]
    ps = [C.psum(F32) for _ in range(8)]
    ldc, wld, mm, evq, pse, stq = (C.s[n] for n in
                                   ("ld1", "wld", "mm", "x1", "x2", "st1"))
    wld0, evq0, stq0 = wld.v, evq.v, stq.v

    _dma(nc.sync, xn_sb[:], xn[:].rearrange("c p t -> p c t"), ldc)
    for nm, src in (("cq", cq), ("sq", sq), ("ck", ck), ("sk", sk)):
        if src is not None:
            _dma(nc.sync, tabs[nm][:], src[:], ldc)
    _dma(nc.sync, idtf[:], C.inline("identf", IDENTF)[:], ldc)

    mmv = {}
    rope_i = 0
    nload = 0

    def qk_part(W, nOC, ctab, stab, OUT, base_oc, wslice):
        nonlocal rope_i, nload
        for j in range(nOC):
            oc = base_oc + j
            par = oc % 2
            banks = ps[par * 4:par * 4 + 4]
            if oc - 2 in mmv:
                nc.gpsimd.wait_ge(mm.h, mmv[oc - 2])
            _dma(nc.gpsimd, wbuf[par][:],
                 W[:, wslice + j * 128:wslice + (j + 1) * 128]
                 .rearrange("(c p) o -> p c o", p=128), wld)
            nload += 1
            nc.tensor.wait_ge(wld.h, wld0 + 16 * nload)
            if oc == 0:
                nc.tensor.wait_ge(ldc.h, ldc.v)
            if oc >= 2:
                nc.tensor.wait_ge(evq.h, evq0 + 4 * (oc - 1))
            for c in range(CD):
                for t4 in range(4):
                    ins = nc.tensor.matmul(banks[t4][:], wbuf[par][:, c, :],
                                           xn_sb[:, c, t4 * 512:(t4 + 1) * 512],
                                           start=(c == 0), stop=(c == CD - 1))
            mm.inc(ins, 1)
            mmv[oc] = mm.v
            for t4 in range(4):
                rpar = rope_i % 2
                sl = slice(t4 * 512, (t4 + 1) * 512)
                nc.vector.wait_ge(mm.h, mmv[oc])
                if rope_i >= 2:
                    nc.vector.wait_ge(stq.h, stq0 + 16 * (rope_i - 1))
                nc.vector.tensor_copy(prt[rpar][0:64, :], banks[t4][64:128, :])
                nc.vector.tensor_copy(prt[rpar][64:128, :], banks[t4][0:64, :])
                nc.vector.tensor_tensor(out=rp1[rpar][:], in0=banks[t4][:],
                                        in1=ctab[:, sl], op=ALU.mult)
                evq.inc(nc.vector.tensor_tensor(out=rp2[rpar][:], in0=prt[rpar][:],
                                                in1=stab[:, sl], op=ALU.mult), 1)
                pse.inc(nc.vector.tensor_tensor(out=qstage[rpar][:], in0=rp1[rpar][:],
                                                in1=rp2[rpar][:], op=ALU.add), 1)
                nc.gpsimd.wait_ge(pse.h, pse.v)
                _dma(nc.gpsimd, OUT[j, :, sl], qstage[rpar][:], stq)
                rope_i += 1

    nq = 0
    if Qo is not None:
        qk_part(Wq, HQ, tabs["cq"], tabs["sq"], Qo, 0, 0)
        nq = HQ
    qk_part(Wk, HKV, tabs["ck"], tabs["sk"], Ko, nq, 0)

    vstore = {}
    vevac = {}
    for j in range(HKV):
        oc = nq + HKV + j
        par = oc % 2
        banks = ps[par * 4:par * 4 + 4]
        if oc - 2 in mmv:
            nc.gpsimd.wait_ge(mm.h, mmv[oc - 2])
        _dma(nc.gpsimd, wbuf[par][:],
             Wv[:, j * 128:(j + 1) * 128].rearrange("(c p) o -> p c o", p=128), wld)
        nload += 1
        nc.tensor.wait_ge(wld.h, wld0 + 16 * nload)
        if j < 2:
            nc.tensor.wait_ge(evq.h, evq0 + 4 * (nq + HKV))
        if j >= 2:
            nc.tensor.wait_ge(evq.h, vevac[(j - 2, 1)])
        for c in range(CD):
            for t4 in range(4):
                ins = nc.tensor.matmul(banks[t4][:], wbuf[par][:, c, :],
                                       xn_sb[:, c, t4 * 512:(t4 + 1) * 512],
                                       start=(c == 0), stop=(c == CD - 1))
        mm.inc(ins, 1)
        mmv[oc] = mm.v
        nc.scalar.wait_ge(mm.h, mmv[oc])
        for t4 in range(4):
            evq.inc(nc.scalar.copy(out=vf[:, t4 * 512:(t4 + 1) * 512],
                                   in_=banks[t4][:]), 1)
        nc.tensor.wait_ge(evq.h, evq.v)
        for g in range(4):
            slot = g % 2
            tb = banks[slot]
            if g >= 2:
                nc.tensor.wait_ge(evq.h, vevac[(j, slot)])
            for t4 in range(4):
                tt = g * 4 + t4
                ins = nc.tensor.transpose(tb[:, t4 * 128:(t4 + 1) * 128],
                                          vf[:, tt * 128:(tt + 1) * 128], idtf[:])
            mm.inc(ins, 1)
            nc.vector.wait_ge(mm.h, mm.v)
            if slot in vstore:
                nc.vector.wait_ge(stq.h, vstore[slot])
            evq.inc(nc.vector.tensor_copy(
                vts[slot][:].rearrange("p a d -> p (a d)"), tb[:]), 1)
            vevac[(j, slot)] = evq.v
            nc.gpsimd.wait_ge(evq.h, evq.v)
            _dma(nc.gpsimd,
                 Vo[g * 512:(g + 1) * 512, j * 128:(j + 1) * 128]
                 .rearrange("(a p) d -> p a d", p=128),
                 vts[slot][:], stq)
            vstore[slot] = stq.v
    nc.sync.wait_ge(stq.h, stq.v)
    nc.gpsimd.wait_ge(stq.h, stq.v)


def emit_attn(C, Qi, Kown, Kpart, Vown, Vpart, Mi, Co):
    nc = C.nc
    K_sb = C.carve([128, HKV, TG], BF16)
    V_sb = C.carve([128, TG // 128, HKV * DH], BF16)
    M_sb = [C.carve([128, 512], BF16) for _ in range(2)]
    idt = C.carve([128, 128], BF16)
    qa = C.carve([128, HQ, 128], BF16)
    S_sb = C.carve([128, TG], F32)
    Pb = [C.carve([128, TG], BF16) for _ in range(2)]
    PT = C.carve([128, TG // 128, 4, 128], BF16)
    ctst = [C.carve([128, 4, 128], BF16) for _ in range(2)]
    st = C.carve([128, 32], F32)
    psS = [C.psum(F32) for _ in range(3)]
    psT = [C.psumb() for _ in range(2)]
    psC = [C.psum(F32) for _ in range(2)]
    ldc, ldq, mm, dmx, aev, pml, stc = (C.s[n] for n in
                                        ("ld1", "ld2", "mm", "dve", "acs", "pml", "st1"))

    for h in range(HKV):
        for s_, Ksrc in ((0, Kown), (1, Kpart)):
            _dma(nc.sync,
                 K_sb[:, h, :].rearrange("p (u s q) -> p u s q", s=2, q=128)[:, :, s_, :],
                 Ksrc[h].rearrange("p (u q) -> p u q", q=128), ldc)
    for s_, Vsrc in ((0, Vown), (1, Vpart)):
        _dma(nc.sync,
             V_sb.rearrange("p (u s) d -> p u s d", s=2)[:, :, s_, :],
             Vsrc[:].rearrange("(u p) d -> p u d", p=128), ldc)
    _dma(nc.sync, idt[:], C.inline("identb", IDENT)[:], ldc)

    negmax = [st[:, 16 + i:17 + i] for i in range(2)]
    lsum = [st[:, 18 + i:19 + i] for i in range(2)]
    linv = [st[:, 20 + i:21 + i] for i in range(2)]
    linv2 = [st[:, 22 + i:23 + i] for i in range(2)]

    ctx_store = {}
    qtile_mm_done = {}
    bank_free = [None, None, None]
    ptbank_free = [None, None]
    ctx_evac = {}
    first = [True]
    prev_exp = [None]
    pb_free = {0: None, 1: None}
    pending = [None]

    def s_stage(ti, kvh, h4, nk5, hpar):
        qh = kvh * 4 + h4
        for kt in range(nk5):
            bk = kt % 3
            bank = psS[bk]
            if bank_free[bk] is not None:
                s_, v_ = bank_free[bk]
                nc.tensor.wait_ge(s_.h, v_)
            ins = nc.tensor.matmul(bank[:, :], qa[:, qh, :],
                                   K_sb[:, kvh, kt * 512:(kt + 1) * 512],
                                   start=True, stop=True)
            mm.inc(ins, 1)
            sl = slice(kt * 512, (kt + 1) * 512)
            if kt == nk5 - 1:
                nc.vector.wait_ge(mm.h, mm.v)
                if prev_exp[0] is not None:
                    nc.vector.wait_ge(aev.h, prev_exp[0])
                dmx.inc(nc.vector.scalar_tensor_tensor(
                    out=S_sb[:, sl], in0=bank[:, :], scalar=1.0,
                    in1=M_sb[ti % 2][:], op0=ALU.mult, op1=ALU.add), 1)
                bank_free[bk] = (dmx, dmx.v)
            else:
                nc.scalar.wait_ge(mm.h, mm.v)
                if prev_exp[0] is not None:
                    pass
                aev.inc(nc.scalar.copy(out=S_sb[:, sl], in_=bank[:, :]), 1)
                bank_free[bk] = (aev, aev.v)
        nc.vector.wait_ge(aev.h, aev.v)
        dmx.inc(nc.vector.tensor_reduce(negmax[hpar], S_sb[:, 0:nk5 * 512], AX,
                                        ALU.max, negate=True), 1)
        nc.scalar.wait_ge(dmx.h, dmx.v)
        if pb_free[hpar] is not None:
            nc.scalar.wait_ge(mm.h, pb_free[hpar])
        aev.inc(nc.scalar.activation(
            out=Pb[hpar][:, 0:nk5 * 512], in_=S_sb[:, 0:nk5 * 512], func=AF.Exp,
            bias=negmax[hpar], accum_out=lsum[hpar]), 1)
        prev_exp[0] = aev.v
        nc.vector.wait_ge(aev.h, aev.v)
        dmx.inc(nc.vector.reciprocal(linv[hpar], lsum[hpar]), 1)
        nc.gpsimd.wait_ge(dmx.h, dmx.v)
        pml.inc(nc.gpsimd.tensor_copy(linv2[hpar], linv[hpar]), 1)
        nc.vector.wait_ge(pml.h, pml.v)
        dmx.inc(nc.vector.tensor_scalar(
            out=Pb[hpar][:, 0:nk5 * 512], in0=Pb[hpar][:, 0:nk5 * 512],
            scalar1=linv2[hpar], scalar2=None, op0=ALU.mult), 1)
        return dmx.v

    def t_stage(h4, nk1, hpar, pscale_v):
        nc.tensor.wait_ge(dmx.h, pscale_v)
        for g in range((nk1 + 7) // 8):
            bk = g % 2
            tb = psT[bk]
            if ptbank_free[bk] is not None:
                nc.tensor.wait_ge(dmx.h, ptbank_free[bk])
            nkk = min(8, nk1 - g * 8)
            for j in range(nkk):
                kc = g * 8 + j
                ins = nc.tensor.transpose(
                    tb[:, j * 128:(j + 1) * 128],
                    Pb[hpar][:, kc * 128:(kc + 1) * 128], idt[:])
            mm.inc(ins, 1)
            nc.vector.wait_ge(mm.h, mm.v)
            dmx.inc(nc.vector.tensor_copy(
                PT[:, g * 8:g * 8 + nkk, h4, :],
                tb[:].rearrange("p (a q) -> p a q", a=8)[:, 0:nkk, :]), 1)
            ptbank_free[bk] = dmx.v
        pb_free[hpar] = mm.v

    for ti in range(NTT):
        lq0 = ti * 128
        qpar = ti % 2
        nk5 = ti // 2 + 1
        nk1 = 4 * nk5
        if ti >= 1:
            nc.sync.wait_ge(mm.h, qtile_mm_done[ti - 1])
        _dma(nc.sync, qa[:], Qi[:, :, lq0:lq0 + 128].rearrange("h p q -> p h q"), ldq)
        _dma(nc.sync, M_sb[qpar][:], Mi[ti], ldq)
        nc.tensor.wait_ge(ldq.h, ldq.v)
        if first[0]:
            nc.tensor.wait_ge(ldc.h, ldc.v)
            first[0] = False
        for kvh in range(HKV):
            cpar = (ti * HKV + kvh) % 2
            pv = {}
            pv[0] = s_stage(ti, kvh, 0, nk5, 0)
            pv[1] = s_stage(ti, kvh, 1, nk5, 1)
            if pending[0] is not None:
                pending[0]()
                pending[0] = None
            t_stage(0, nk1, 0, pv[0])
            pv[2] = s_stage(ti, kvh, 2, nk5, 0)
            t_stage(1, nk1, 1, pv[1])
            pv[3] = s_stage(ti, kvh, 3, nk5, 1)
            t_stage(2, nk1, 0, pv[2])
            t_stage(3, nk1, 1, pv[3])
            pt_ready = dmx.v

            def mk_pv(ti_=ti, kvh_=kvh, nk1_=nk1, cpar_=cpar, lq0_=lq0,
                      ptv_=pt_ready):
                def do():
                    nc.tensor.wait_ge(dmx.h, ptv_)
                    idx = ti_ * HKV + kvh_
                    if idx >= 2:
                        nc.tensor.wait_ge(aev.h, ctx_evac[idx - 2])
                    for kc in range(nk1_):
                        ins = nc.tensor.matmul(
                            psC[cpar_][:, :], V_sb[:, kc, kvh_ * 128:(kvh_ + 1) * 128],
                            PT[:, kc, :, :].rearrange("p a q -> p (a q)"),
                            start=(kc == 0), stop=(kc == nk1_ - 1))
                    mm.inc(ins, 1)
                    nc.scalar.wait_ge(mm.h, mm.v)
                    if cpar_ in ctx_store:
                        nc.scalar.wait_ge(stc.h, ctx_store[cpar_])
                    aev.inc(nc.scalar.copy(out=ctst[cpar_][:].rearrange("p a q -> p (a q)"),
                                           in_=psC[cpar_][:, :]), 1)
                    ctx_evac[idx] = aev.v
                    nc.gpsimd.wait_ge(aev.h, aev.v)
                    _dma(nc.gpsimd,
                         Co[kvh_ * 4:(kvh_ + 1) * 4, :, lq0_:lq0_ + 128]
                         .rearrange("h p q -> p h q"),
                         ctst[cpar_][:], stc)
                    ctx_store[cpar_] = stc.v
                return do
            pending[0] = mk_pv()
        qtile_mm_done[ti] = mm.v
    if pending[0] is not None:
        pending[0]()
        pending[0] = None
    nc.sync.wait_ge(stc.h, stc.v)
    nc.gpsimd.wait_ge(stc.h, stc.v)


def emit_combine(C, kind, A, W, xprev, xnew, xn2):
    nc = C.nc
    nK = CD if kind == "wo" else HB
    nKh = nK // 2
    nwb = 4 if kind == "wo" else 2
    A_sb = C.carve([128, nK, 512], BF16)
    wbuf = [C.carve([128, nKh, 128], BF16) for _ in range(nwb)]
    yf1 = C.carve([128, 512], F32)
    yf = [yf1, yf1]
    y4 = C.carve([128, 4, CD, 128], F32)
    xp = C.carve([128, D], F32)
    xnb = C.carve([128, D], BF16)
    st4 = C.carve([128, 64], F32)
    stage = C.carve([128, CD, 128], BF16)
    idt = C.carve([128, 128], BF16)
    idtf = C.carve([128, 128], F32)
    psY = [C.psum(F32) for _ in range(2)]
    psT = [C.psum(F32) for _ in range(2)]
    psB = [C.psumb() for _ in range(2)]
    lda, wld, mm, aev, dev, acs, stx = (C.s[n] for n in
                                        ("ld1", "wld", "mm", "acs", "dve", "x1", "st1"))
    wld0 = wld.v
    _dma(nc.sync, idt[:], C.inline("identb", IDENT)[:], lda)
    _dma(nc.sync, idtf[:], C.inline("identf", IDENTF)[:], lda)

    nwld = 0
    mm_oc = {}
    yf_ev = {}
    trn_ev = {}
    dev_ev = {}
    xnb_ev = {}
    xnb_tr = {}
    stage_store = {}
    xp_store = {}
    pend_tr = [None]
    for ts in range(4):
        tsl = slice(ts * 512, (ts + 1) * 512)
        if ts >= 1:
            nc.sync.wait_ge(mm.h, mm_oc[(ts - 1, CD - 1)])
        _dma(nc.sync, A_sb[:], A[:, :, tsl].rearrange("c p t -> p c t"), lda)
        for oc in range(CD):
            par = oc % 2
            obuf = (oc % 2) * 2 if kind == "wo" else 0
            for half in range(2):
                buf = wbuf[obuf + half]
                wprev = oc - 2 if kind == "wo" else oc - 1
                if (ts, wprev) in mm_oc:
                    nc.sync.wait_ge(mm.h, mm_oc[(ts, wprev)])
                elif ts >= 1 and oc == 0:
                    nc.sync.wait_ge(mm.h, mm_oc[(ts - 1, CD - 1)])
                if kind == "wo":
                    src = W[:, oc * 128:(oc + 1) * 128] \
                        .rearrange("(c p) o -> p c o", p=128)[:, half * nKh:(half + 1) * nKh, :]
                    _dma(nc.sync, buf[:], src, wld)
                else:
                    src = W[oc, half * nKh:(half + 1) * nKh, :, :].rearrange("c p o -> p c o")
                    _dma(nc.sync, buf[:], src, wld)
                nwld += 1
            nc.tensor.wait_ge(wld.h, wld0 + 16 * nwld)
            nc.tensor.wait_ge(lda.h, lda.v)
            if (ts, oc - 2) in yf_ev:
                nc.tensor.wait_ge(aev.h, yf_ev[(ts, oc - 2)])
            elif ts >= 1 and oc < 2:
                nc.tensor.wait_ge(aev.h, yf_ev[(ts - 1, CD - 2 + oc)])
            for c in range(nK):
                ins = nc.tensor.matmul(psY[par][:], wbuf[obuf + c // nKh][:, c % nKh, :],
                                       A_sb[:, c, :],
                                       start=(c == 0), stop=(c == nK - 1))
            mm.inc(ins, 1)
            mm_oc[(ts, oc)] = mm.v
            if pend_tr[0] is not None:
                pend_tr[0]()
            nc.scalar.wait_ge(mm.h, mm.v)
            if (ts, oc - 1) in trn_ev:
                nc.scalar.wait_ge(mm.h, trn_ev[(ts, oc - 1)])
            elif ts >= 1 and oc == 0:
                nc.scalar.wait_ge(mm.h, trn_ev[(ts - 1, CD - 1)])
            aev.inc(nc.scalar.copy(out=yf[par][:], in_=psY[par][:]), 1)
            yf_ev[(ts, oc)] = aev.v
            def mk_tr(ts_=ts, oc_=oc, par_=par, aevv=aev.v):
                def do():
                    nc.tensor.wait_ge(aev.h, aevv)
                    if (ts_, oc_ - 2) in dev_ev:
                        nc.tensor.wait_ge(dev.h, dev_ev[(ts_, oc_ - 2)])
                    elif ts_ >= 1 and oc_ < 2:
                        nc.tensor.wait_ge(dev.h, dev_ev[(ts_ - 1, CD - 2 + oc_)])
                    for a in range(4):
                        ins = nc.tensor.transpose(psT[par_][:, a * 128:(a + 1) * 128],
                                                  yf[par_][:, a * 128:(a + 1) * 128], idtf[:])
                    mm.inc(ins, 1)
                    trn_ev[(ts_, oc_)] = mm.v
                    nc.vector.wait_ge(mm.h, mm.v)
                    dev.inc(nc.vector.tensor_copy(y4[:, :, oc_, :], psT[par_][:]
                                                  .rearrange("p (a q) -> p a q", a=4)), 1)
                    dev_ev[(ts_, oc_)] = dev.v
                return do
            pend_tr[0] = mk_tr()
        if pend_tr[0] is not None:
            pend_tr[0]()
            pend_tr[0] = None
        yav = [y4[:, a, :, :].rearrange("p c q -> p (c q)") for a in range(4)]
        nc.vector.wait_ge(dev.h, dev.v)
        parts1 = []
        for a in range(4):
            m_, r_, last_, fns = emit_ln_stats(nc, yav[a], xnb[:], st4[:, 16 * a:16 * a + 8])
            parts1.append((m_, r_, fns))
        dev.inc(last_, 1)
        nc.scalar.wait_ge(dev.h, dev.v)
        for a in range(4):
            ins_ = parts1[a][2][0]()
        acs.inc(ins_, 1)
        nc.vector.wait_ge(acs.h, acs.v)
        for a in range(4):
            ins_ = parts1[a][2][1]()
        dev.inc(ins_, 1)
        nc.scalar.wait_ge(dev.h, dev.v)
        for a in range(4):
            ins_ = parts1[a][2][2]()
        acs.inc(ins_, 1)
        nc.vector.wait_ge(acs.h, acs.v)
        for a in range(4):
            tt = ts * 4 + a
            m_, r_, _ = parts1[a]
            nc.vector.tensor_scalar(out=yav[a], in0=yav[a], scalar1=m_, scalar2=r_,
                                    op0=ALU.subtract, op1=ALU.mult)
            if tt >= 1:
                nc.sync.wait_ge(dev.h, dev.v)
                nc.sync.wait_ge(stx.h, xp_store[tt - 1])
            _dma(nc.sync, xp[:], xprev[tt * 128:(tt + 1) * 128, :], lda)
            nc.vector.wait_ge(lda.h, lda.v)
            dev.inc(nc.vector.tensor_tensor(out=yav[a], in0=xp[:], in1=yav[a],
                                            op=ALU.add), 1)
            xp_store[tt] = stx.v + 16
            nc.gpsimd.wait_ge(dev.h, dev.v)
            _dma(nc.gpsimd, xnew[tt * 128:(tt + 1) * 128, :], yav[a], stx)
        if xn2 is not None:
            parts2 = []
            for a in range(4):
                m_, r_, last_, fns = emit_ln_stats(nc, yav[a], xnb[:],
                                                   st4[:, 16 * a + 8:16 * a + 16])
                parts2.append((m_, r_, fns))
            dev.inc(last_, 1)
            nc.scalar.wait_ge(dev.h, dev.v)
            for a in range(4):
                ins_ = parts2[a][2][0]()
            acs.inc(ins_, 1)
            nc.vector.wait_ge(acs.h, acs.v)
            for a in range(4):
                ins_ = parts2[a][2][1]()
            dev.inc(ins_, 1)
            nc.scalar.wait_ge(dev.h, dev.v)
            for a in range(4):
                ins_ = parts2[a][2][2]()
            acs.inc(ins_, 1)
            for a in range(4):
                tt = ts * 4 + a
                m_, r_, _ = parts2[a]
                nc.vector.wait_ge(acs.h, acs.v)
                if tt >= 1 and (tt - 1) in xnb_tr:
                    nc.vector.wait_ge(mm.h, xnb_tr[tt - 1])
                dev.inc(nc.vector.tensor_scalar(out=xnb[:], in0=yav[a], scalar1=m_,
                                                scalar2=r_, op0=ALU.subtract,
                                                op1=ALU.mult), 1)
                nc.tensor.wait_ge(dev.h, dev.v)
                stv = stage_store.get(tt - 1) if tt >= 1 else None
                if stv is not None:
                    nc.vector.wait_ge(stx.h, stv)
                for g in range(4):
                    bank = psB[g % 2]
                    need = None
                    if g >= 2:
                        need = xnb_ev[(tt, g - 2)]
                    elif tt >= 1:
                        need = xnb_ev[(tt - 1, 2 + g)]
                    if need is not None:
                        nc.tensor.wait_ge(dev.h, need)
                    for j in range(8):
                        c = g * 8 + j
                        ins = nc.tensor.transpose(bank[:, j * 128:(j + 1) * 128],
                                                  xnb[:, c * 128:(c + 1) * 128], idt[:])
                    mm.inc(ins, 1)
                    nc.vector.wait_ge(mm.h, mm.v)
                    dev.inc(nc.vector.tensor_copy(
                        stage[:, g * 8:(g + 1) * 8, :].rearrange("p c q -> p (c q)"),
                        bank[:]), 1)
                    xnb_ev[(tt, g)] = dev.v
                xnb_tr[tt] = mm.v
                nc.gpsimd.wait_ge(dev.h, dev.v)
                _dma(nc.gpsimd, xn2[:, :, tt * 128:(tt + 1) * 128].rearrange("c p q -> p c q"),
                     stage[:], stx)
                stage_store[tt] = stx.v
    nc.sync.wait_ge(stx.h, stx.v)
    nc.gpsimd.wait_ge(stx.h, stx.v)


def emit_dequant(nc, eng, pk, s_sb, Wb, mi, ni, ngrp):
    se = s_sb.rearrange("p (q two) -> p q two", two=2)
    eng.tensor_scalar(out=mi, in0=pk, scalar1=4, scalar2=None,
                      op0=ALU.arith_shift_right)
    eng.tensor_scalar(out=ni, in0=pk, scalar1=15, scalar2=8,
                      op0=ALU.bitwise_and, op1=ALU.bitwise_xor)
    eng.tensor_scalar(out=ni, in0=ni, scalar1=-8, scalar2=None, op0=ALU.add)
    eng.tensor_tensor(out=Wb[:, :, 0, :],
                      in0=mi.rearrange("p (q j) -> p q j", j=64),
                      in1=se[:, :, 0:1].broadcast_to([128, ngrp, 64]), op=ALU.mult)
    return eng.tensor_tensor(out=Wb[:, :, 1, :],
                             in0=ni.rearrange("p (q j) -> p q j", j=64),
                             in1=se[:, :, 1:2].broadcast_to([128, ngrp, 64]),
                             op=ALU.mult)


def emit_ffn0up(C, xn, w1, s1, w3, s3, ha):
    nc = C.nc
    xn_sb = C.carve([128, CD, T], BF16)
    pk1 = C.carve([128, CD * 64], I32)
    pk3 = C.carve([128, CD * 64], I32)
    mi1 = C.carve([128, CD * 64], I32)
    s1t = C.carve([128, 64], F32)
    s3t = C.carve([128, 64], F32)
    W1b = C.carve([128, CD, 2, 64], BF16)
    W3b = C.carve([128, CD, 2, 64], BF16)
    W1T = [C.carve([128, CD, 128], BF16) for _ in range(2)]
    W3T = [C.carve([128, CD, 128], BF16) for _ in range(2)]
    sil = C.carve([128, 512], BF16)
    hst = [C.carve([128, 512], BF16) for _ in range(2)]
    idt = C.carve([128, 128], BF16)
    psF = [C.psum(F32) for _ in range(4)]
    psX = [C.psumb() for _ in range(2)]
    ldc, ldp, dqq, trn, ev1, ev3, mmf, sth = (C.s[n] for n in
        ("ld1", "ld2", "x1", "mm", "acs", "dve", "x2", "st1"))

    _dma(nc.sync, xn_sb[:], xn[:].rearrange("c p t -> p c t"), ldc)
    _dma(nc.sync, idt[:], C.inline("identb", IDENT)[:], ldc)

    dq_done = {}
    mmfv = {}
    trn_done = {}
    bank_free = {0: None, 1: None}
    psf_free = {0: None, 1: None}
    sthv = {}
    last_mul = None
    for hb in range(HB):
        par = hb % 2
        if hb >= 1:
            nc.sync.wait_ge(dqq.h, dq_done[hb - 1])
        _dma(nc.sync, pk1[:].rearrange("p (a j) -> p a j", j=64),
             w1[hb * 4096:(hb + 1) * 4096, :].rearrange("(p a) j -> p a j", p=128), ldp)
        _dma(nc.sync, pk3[:].rearrange("p (a j) -> p a j", j=64),
             w3[hb * 4096:(hb + 1) * 4096, :].rearrange("(p a) j -> p a j", p=128), ldp)
        _dma(nc.sync, s1t[:], s1.rearrange("(h g) -> h g", g=64)[hb * 128:(hb + 1) * 128, :], ldp)
        _dma(nc.sync, s3t[:], s3.rearrange("(h g) -> h g", g=64)[hb * 128:(hb + 1) * 128, :], ldp)
        nc.vector.wait_ge(ldp.h, ldp.v)
        if hb >= 1:
            nc.vector.wait_ge(trn.h, trn_done[hb - 1])
        dqq.inc(emit_dequant(nc, nc.vector, pk1[:], s1t[:], W1b[:], mi1[:], pk1[:], CD), 1)
        dq1_v = dqq.v
        dqq.inc(emit_dequant(nc, nc.vector, pk3[:], s3t[:], W3b[:], mi1[:], pk3[:], CD), 1)
        dq_done[hb] = dqq.v
        if hb == 0:
            nc.tensor.wait_ge(ldc.h, ldc.v)
        if hb >= 2:
            nc.tensor.wait_ge(mmf.h, mmfv[hb - 2])
        for (W_b, W_T, on_act, dqv_) in ((W1b, W1T[par], True, dq1_v),
                                         (W3b, W3T[par], False, dq_done[hb])):
            nc.tensor.wait_ge(dqq.h, dqv_)
            for g in range(4):
                bk = g % 2
                bank = psX[bk]
                if bank_free[bk] is not None:
                    s_, v_ = bank_free[bk]
                    nc.tensor.wait_ge(s_.h, v_)
                for j in range(8):
                    c = g * 8 + j
                    ins = nc.tensor.transpose(
                        bank[:, j * 128:(j + 1) * 128],
                        W_b[:].rearrange("p a two j -> p (a two j)")[:, c * 128:(c + 1) * 128],
                        idt[:])
                trn.inc(ins, 1)
                dst = W_T[:, g * 8:(g + 1) * 8, :].rearrange("p c q -> p (c q)")
                if on_act:
                    nc.scalar.wait_ge(trn.h, trn.v)
                    ev1.inc(nc.scalar.copy(out=dst, in_=bank[:]), 1)
                    bank_free[bk] = (ev1, ev1.v)
                else:
                    nc.vector.wait_ge(trn.h, trn.v)
                    ev3.inc(nc.vector.tensor_copy(dst, bank[:]), 1)
                    bank_free[bk] = (ev3, ev3.v)
        trn_done[hb] = trn.v
        w_ready = (ev1.v, ev3.v)
        nc.tensor.wait_ge(ev1.h, w_ready[0])
        nc.tensor.wait_ge(ev3.h, w_ready[1])
        for t4 in range(4):
            tsl = slice(t4 * 512, (t4 + 1) * 512)
            pp = t4 % 2
            pa, pb = psF[2 * pp], psF[2 * pp + 1]
            if psf_free[pp] is not None:
                nc.tensor.wait_ge(ev1.h, psf_free[pp][0])
                nc.tensor.wait_ge(ev3.h, psf_free[pp][1])
            for c in range(CD):
                nc.tensor.matmul(pa[:], W1T[par][:, c, :], xn_sb[:, c, tsl],
                                 start=(c == 0), stop=(c == CD - 1))
            for c in range(CD):
                ins = nc.tensor.matmul(pb[:], W3T[par][:, c, :], xn_sb[:, c, tsl],
                                       start=(c == 0), stop=(c == CD - 1))
            mmf.inc(ins, 1)
            nc.scalar.wait_ge(mmf.h, mmf.v)
            if last_mul is not None:
                nc.scalar.wait_ge(ev3.h, last_mul)
            ev1.inc(nc.scalar.activation(out=sil[:], in_=pa[:], func=AF.Silu), 1)
            silu_v = ev1.v
            nc.vector.wait_ge(ev1.h, silu_v)
            it = hb * 4 + t4
            if it >= 2:
                nc.vector.wait_ge(sth.h, sthv[it - 2])
            ev3.inc(nc.vector.tensor_tensor(out=hst[pp][:], in0=sil[:],
                                            in1=pb[:], op=ALU.mult), 1)
            last_mul = ev3.v
            psf_free[pp] = (silu_v, last_mul)
            nc.gpsimd.wait_ge(ev3.h, last_mul)
            _dma(nc.gpsimd, ha[hb, :, tsl], hst[pp][:], sth)
            sthv[it] = sth.v
        mmfv[hb] = mmf.v
    nc.sync.wait_ge(sth.h, sth.v)
    nc.gpsimd.wait_ge(sth.h, sth.v)


def emit_w2t(C, w2, s2, W2T):
    nc = C.nc
    NG = HB
    pk = [C.carve([128, NG * 64], I32) for _ in range(2)]
    s2t = [C.carve([128, 2 * NG], F32) for _ in range(2)]
    mi = C.carve([128, NG * 64], I32)
    ni = C.carve([128, NG * 64], I32)
    Wb = C.carve([128, NG, 2, 64], BF16)
    stage = [C.carve([128, NG, 128], BF16) for _ in range(2)]
    idt = C.carve([128, 128], BF16)
    psX = [C.psumb() for _ in range(2)]
    ldc, ldp, dqd, trn, evc, stw = (C.s[n] for n in
                                    ("ld1", "ld2", "x1", "mm", "acs", "st1"))
    _dma(nc.sync, idt[:], C.inline("identb", IDENT)[:], ldc)
    dqv = {}
    trv = {}
    stv = {}
    bank_free = {0: None, 1: None}
    for ob in range(CD):
        par = ob % 2
        if ob >= 2:
            nc.sync.wait_ge(dqd.h, dqv[ob - 2])
        _dma(nc.sync, pk[par][:].rearrange("p (a j) -> p a j", j=64),
             w2[ob * 11008:(ob + 1) * 11008, :].rearrange("(p a) j -> p a j", p=128), ldp)
        _dma(nc.sync, s2t[par][:],
             s2.rearrange("(h g) -> h g", g=2 * NG)[ob * 128:(ob + 1) * 128, :], ldp)
        nc.vector.wait_ge(ldp.h, ldp.v)
        if ob >= 1:
            nc.vector.wait_ge(trn.h, trv[ob - 1])
        dqd.inc(emit_dequant(nc, nc.vector, pk[par][:], s2t[par][:], Wb[:],
                             mi[:], ni[:], NG), 1)
        dqv[ob] = dqd.v
        nc.tensor.wait_ge(dqd.h, dqd.v)
        if ob == 0:
            nc.tensor.wait_ge(ldc.h, ldc.v)
        for g in range((NG + 7) // 8):
            bk = g % 2
            bank = psX[bk]
            if bank_free[bk] is not None:
                nc.tensor.wait_ge(evc.h, bank_free[bk])
            nkk = min(8, NG - g * 8)
            for j in range(nkk):
                c = g * 8 + j
                ins = nc.tensor.transpose(
                    bank[:, j * 128:(j + 1) * 128],
                    Wb[:].rearrange("p a two j -> p (a two j)")[:, c * 128:(c + 1) * 128],
                    idt[:])
            trn.inc(ins, 1)
            nc.scalar.wait_ge(trn.h, trn.v)
            if g == 0 and ob >= 2:
                nc.scalar.wait_ge(stw.h, stv[ob - 2])
            evc.inc(nc.scalar.copy(
                out=stage[par][:, g * 8:g * 8 + nkk, :].rearrange("p c q -> p (c q)"),
                in_=bank[:, 0:nkk * 128]), 1)
            bank_free[bk] = evc.v
        trv[ob] = trn.v
        nc.gpsimd.wait_ge(evc.h, evc.v)
        _dma(nc.gpsimd, W2T[ob].rearrange("c p o -> p c o"), stage[par][:], stw)
        stv[ob] = stw.v
    nc.sync.wait_ge(stw.h, stw.v)
    nc.gpsimd.wait_ge(stw.h, stw.v)


def emit_ffn1(C, xn, Wup, gateT, ebias, AT, Ball, xprev, xnew):
    nc = C.nc
    xn_ts = [C.carve([128, CD, 512], BF16) for _ in range(2)]
    gA = C.carve([128, CD, 36], BF16)
    ob1 = C.carve([1, 132], BF16)
    Ball_sb = C.carve([32, D], BF16)
    zsb = C.carve([128, 16, 32], F32)
    lgsb = C.carve([128, 16, 4], F32)
    lg2 = C.carve([128, 16, 4], F32)
    mx = C.carve([128, 16], F32)
    mx2 = C.carve([128, 16], F32)
    oh = C.carve([128, 16, 4], F32)
    zmb = C.carve([128, 16, 32], BF16)
    zmT = C.carve([32, T], BF16)
    wub = [C.carve([128, CD, 128], BF16) for _ in range(2)]
    wubT = C.carve([128, CD, 128], BF16)
    yf1 = C.carve([128, 512], F32)
    y4 = C.carve([128, 4, CD, 128], F32)
    xp = C.carve([128, D], F32)
    xnb = C.carve([128, D], BF16)
    st4 = C.carve([128, 64], F32)
    idtf = C.carve([128, 128], F32)
    idtb = C.carve([128, 128], BF16)
    psY = [C.psum(F32) for _ in range(2)]
    psT = [C.psum(F32) for _ in range(2)]
    psZ = C.psum(F32)
    psL = C.psum(F32)[:, 0:64]
    psB2 = [C.psumb() for _ in range(2)]
    lda, wld, mm, dev, acs, pml, stx = (C.s[n] for n in
                                        ("ld1", "wld", "mm", "dve", "acs", "pml", "st1"))
    wld0 = wld.v

    _dma(nc.sync, idtf[:], C.inline("identf", IDENTF)[:], lda)
    _dma(nc.sync, idtb[:], C.inline("identb", IDENT)[:], lda)
    _dma(nc.gpsimd, ob1[:, 0:128], C.inline("onesb", np.ones((1, 128), dtype=BF))[:], lda)
    _dma(nc.gpsimd, ob1[:, 128:132], ebias[:], lda)
    _dma(nc.gpsimd, gA[:, :, 0:4], gateT[:].rearrange("(c p) e -> p c e", p=128), lda)
    _dma(nc.gpsimd, gA[:, :, 4:36], AT[:].rearrange("(c p) e -> p c e", p=128), lda)
    _dma(nc.gpsimd, Ball_sb[:], Ball[:], lda)

    pre_mm = {}
    for ts in range(4):
        tsl = slice(ts * 512, (ts + 1) * 512)
        buf = xn_ts[ts % 2]
        if ts >= 2:
            nc.sync.wait_ge(mm.h, pre_mm[ts - 2])
        _dma(nc.sync, buf[:], xn[:, :, tsl].rearrange("c p t -> p c t"), lda)
        nc.tensor.wait_ge(lda.h, lda.v)
        for tt4 in range(4):
            tt = ts * 4 + tt4
            sl = slice(tt4 * 128, (tt4 + 1) * 128)
            for c in range(CD):
                nc.tensor.matmul(psL[:, tt * 4:(tt + 1) * 4],
                                 buf[:, c, sl], gA[:, c, 0:4],
                                 start=(c == 0), stop=False)
            nc.tensor.matmul(psL[:, tt * 4:(tt + 1) * 4],
                             ob1[:, 0:128], ob1[:, 128:132],
                             start=False, stop=True)
            for c in range(CD):
                ins = nc.tensor.matmul(psZ[:, tt * 32:(tt + 1) * 32],
                                       buf[:, c, sl], gA[:, c, 4:36],
                                       start=(c == 0), stop=(c == CD - 1))
            mm.inc(ins, 1)
        pre_mm[ts] = mm.v
    nc.vector.wait_ge(mm.h, mm.v)
    nc.vector.tensor_copy(lgsb[:], psL[:].rearrange("p (t e) -> p t e", e=4))
    dev.inc(nc.vector.tensor_copy(zsb[:], psZ[:].rearrange("p (t e) -> p t e", e=32)), 1)
    nc.gpsimd.wait_ge(dev.h, dev.v)
    pml.inc(nc.gpsimd.tensor_copy(lg2[:], lgsb[:]), 1)
    nc.vector.wait_ge(pml.h, pml.v)
    dev.inc(nc.vector.tensor_reduce(mx[:].unsqueeze(2), lg2[:], AX, ALU.max), 1)
    nc.gpsimd.wait_ge(dev.h, dev.v)
    pml.inc(nc.gpsimd.tensor_copy(mx2[:], mx[:]), 1)
    nc.vector.wait_ge(pml.h, pml.v)
    dev.inc(nc.vector.tensor_tensor(
        out=oh[:], in0=lg2[:],
        in1=mx2[:].unsqueeze(2).broadcast_to([128, 16, 4]), op=ALU.is_ge), 1)
    nc.gpsimd.wait_ge(dev.h, dev.v)
    pml.inc(nc.gpsimd.tensor_tensor(
        out=zmb[:].rearrange("p t (e r) -> p t e r", e=4),
        in0=zsb[:].rearrange("p t (e r) -> p t e r", e=4),
        in1=oh[:].unsqueeze(3).broadcast_to([128, 16, 4, 8]), op=ALU.mult), 1)
    nc.tensor.wait_ge(pml.h, pml.v)
    zmt_ev = {}
    for g in range(4):
        bank = psB2[g % 2]
        if g >= 2:
            nc.tensor.wait_ge(dev.h, zmt_ev[g - 2])
        for j in range(4):
            tt = g * 4 + j
            ins = nc.tensor.transpose(bank[0:32, j * 128:(j + 1) * 128],
                                      zmb[:, tt, :], idtb[:])
        mm.inc(ins, 1)
        nc.vector.wait_ge(mm.h, mm.v)
        dev.inc(nc.vector.tensor_copy(zmT[:, g * 512:(g + 1) * 512],
                                      bank[0:32, 0:512]), 1)
        zmt_ev[g] = dev.v

    nwld = 0
    mm_oc = {}
    yf_ev = {}
    trn_ev = {}
    dev_ev = {}
    wup_tr = {}
    wupT_free = {0: zmt_ev.get(2), 1: zmt_ev.get(3)}
    xp_store1 = {}
    pre_done_mm = mm.v
    for ts in range(4):
        tsl = slice(ts * 512, (ts + 1) * 512)
        buf = xn_ts[ts % 2]
        if ts >= 2:
            nc.sync.wait_ge(mm.h, mm_oc[(ts - 1, CD - 1)])
        else:
            nc.sync.wait_ge(mm.h, pre_done_mm)
        _dma(nc.sync, buf[:], xn[:, :, tsl].rearrange("c p t -> p c t"), lda)
        for oc in range(CD):
            par = oc % 2
            wb = wub[par]
            if (ts, oc - 2) in wup_tr:
                nc.sync.wait_ge(mm.h, wup_tr[(ts, oc - 2)])
            elif ts >= 1 and oc < 2:
                nc.sync.wait_ge(mm.h, wup_tr[(ts - 1, CD - 2 + oc)])
            _dma(nc.sync, wb[:],
                 Wup[oc * 128:(oc + 1) * 128, :].rearrange("o (c p) -> o c p", p=128),
                 wld)
            nwld += 1
            nc.tensor.wait_ge(wld.h, wld0 + 16 * nwld)
            nc.tensor.wait_ge(lda.h, lda.v)
            if (ts, oc - 1) in mm_oc:
                nc.tensor.wait_ge(mm.h, mm_oc[(ts, oc - 1)])
            elif ts >= 1 and oc == 0:
                nc.tensor.wait_ge(mm.h, mm_oc[(ts - 1, CD - 1)])
            for g in range(4):
                bk = g % 2
                bank = psB2[bk]
                if wupT_free.get(bk) is not None:
                    nc.tensor.wait_ge(dev.h, wupT_free[bk])
                for j in range(8):
                    c = g * 8 + j
                    ins = nc.tensor.transpose(bank[:, j * 128:(j + 1) * 128],
                                              wb[:, c, :], idtb[:])
                mm.inc(ins, 1)
                nc.vector.wait_ge(mm.h, mm.v)
                dev.inc(nc.vector.tensor_copy(
                    wubT[:, g * 8:(g + 1) * 8, :].rearrange("p c q -> p (c q)"),
                    bank[:]), 1)
                wupT_free[bk] = dev.v
            wup_tr[(ts, oc)] = mm.v
            nc.tensor.wait_ge(dev.h, dev.v)
            if (ts, oc - 2) in yf_ev:
                nc.tensor.wait_ge(acs.h, yf_ev[(ts, oc - 2)])
            elif ts >= 1 and oc < 2:
                nc.tensor.wait_ge(acs.h, yf_ev[(ts - 1, CD - 2 + oc)])
            for c in range(CD):
                nc.tensor.matmul(psY[par][:], wubT[:, c, :], buf[:, c, :],
                                 start=(c == 0), stop=False)
            ins = nc.tensor.matmul(psY[par][:], Ball_sb[0:32, oc * 128:(oc + 1) * 128],
                                   zmT[0:32, tsl], start=False, stop=True)
            mm.inc(ins, 1)
            mm_oc[(ts, oc)] = mm.v
            nc.scalar.wait_ge(mm.h, mm.v)
            if (ts, oc - 1) in trn_ev:
                nc.scalar.wait_ge(mm.h, trn_ev[(ts, oc - 1)])
            acs.inc(nc.scalar.copy(out=yf1[:], in_=psY[par][:]), 1)
            yf_ev[(ts, oc)] = acs.v
            nc.tensor.wait_ge(acs.h, acs.v)
            if (ts, oc - 2) in dev_ev:
                nc.tensor.wait_ge(dev.h, dev_ev[(ts, oc - 2)])
            for a in range(4):
                ins = nc.tensor.transpose(psT[par][:, a * 128:(a + 1) * 128],
                                          yf1[:, a * 128:(a + 1) * 128], idtf[:])
            mm.inc(ins, 1)
            trn_ev[(ts, oc)] = mm.v
            nc.vector.wait_ge(mm.h, mm.v)
            dev.inc(nc.vector.tensor_copy(y4[:, :, oc, :], psT[par][:]
                                          .rearrange("p (a q) -> p a q", a=4)), 1)
            dev_ev[(ts, oc)] = dev.v
        yav = [y4[:, a, :, :].rearrange("p c q -> p (c q)") for a in range(4)]
        nc.vector.wait_ge(dev.h, dev.v)
        parts1 = []
        for a in range(4):
            m_, r_, last_, fns = emit_ln_stats(nc, yav[a], xnb[:], st4[:, 16 * a:16 * a + 8])
            parts1.append((m_, r_, fns))
        dev.inc(last_, 1)
        nc.scalar.wait_ge(dev.h, dev.v)
        for a in range(4):
            ins_ = parts1[a][2][0]()
        acs.inc(ins_, 1)
        nc.vector.wait_ge(acs.h, acs.v)
        for a in range(4):
            ins_ = parts1[a][2][1]()
        dev.inc(ins_, 1)
        nc.scalar.wait_ge(dev.h, dev.v)
        for a in range(4):
            ins_ = parts1[a][2][2]()
        acs.inc(ins_, 1)
        nc.vector.wait_ge(acs.h, acs.v)
        for a in range(4):
            tt = ts * 4 + a
            m_, r_, _ = parts1[a]
            nc.vector.tensor_scalar(out=yav[a], in0=yav[a], scalar1=m_, scalar2=r_,
                                    op0=ALU.subtract, op1=ALU.mult)
            if tt >= 1:
                nc.sync.wait_ge(dev.h, dev.v)
                nc.sync.wait_ge(stx.h, xp_store1[tt - 1])
            _dma(nc.sync, xp[:], xprev[tt * 128:(tt + 1) * 128, :], lda)
            nc.vector.wait_ge(lda.h, lda.v)
            dev.inc(nc.vector.tensor_tensor(out=yav[a], in0=xp[:], in1=yav[a],
                                            op=ALU.add), 1)
            xp_store1[tt] = stx.v + 16
            nc.gpsimd.wait_ge(dev.h, dev.v)
            _dma(nc.gpsimd, xnew[tt * 128:(tt + 1) * 128, :], yav[a], stx)
    nc.sync.wait_ge(stx.h, stx.v)
    nc.gpsimd.wait_ge(stx.h, stx.v)


def emit_precast(C, src_f32, dst_bf):
    nc = C.nc
    buf = [C.carve([128, D], BF16) for _ in range(2)]
    ld, stx = C.s["ld1"], C.s["st1"]
    st_v = {}
    for i in range(CD):
        par = i % 2
        if i >= 2:
            nc.gpsimd.wait_ge(stx.h, st_v[i - 2])
        _dma(nc.gpsimd, buf[par][:], src_f32[i * 128:(i + 1) * 128, :], ld)
        nc.sync.wait_ge(ld.h, ld.v)
        _dma(nc.sync, dst_bf[i * 128:(i + 1) * 128, :], buf[par][:], stx)
        st_v[i] = stx.v
    nc.sync.wait_ge(stx.h, stx.v)
    nc.gpsimd.wait_ge(stx.h, stx.v)



def _newC():
    C = Ctx()
    C.setup()
    C.section()
    return C


def build_ln0():
    C = _newC()
    nc = C.nc
    x = nc.dram_tensor("x", [T, D], F32, kind="ExternalInput").ap()
    xn = nc.dram_tensor("xn", [CD, 128, T], BF16, kind="ExternalOutput").ap()
    emit_ln0(C, x, xn)
    return C.finish()


def build_qkv(layer=0):
    C = _newC()
    nc = C.nc
    xn = nc.dram_tensor("xn", [CD, 128, T], BF16, kind="ExternalInput").ap()
    Wq = nc.dram_tensor("Wq", [D, D], F32, kind="ExternalInput").ap()
    Wk = nc.dram_tensor("Wk", [D, HKV * DH], F32, kind="ExternalInput").ap()
    Wv = nc.dram_tensor("Wv", [D, HKV * DH], F32, kind="ExternalInput").ap()
    cq = nc.dram_tensor("cq", [128, T], F32, kind="ExternalInput").ap()
    sq = nc.dram_tensor("sq", [128, T], F32, kind="ExternalInput").ap()
    ck = nc.dram_tensor("ck", [128, T], F32, kind="ExternalInput").ap()
    sk = nc.dram_tensor("sk", [128, T], F32, kind="ExternalInput").ap()
    Qo = nc.dram_tensor("Q", [HQ, 128, T], BF16, kind="ExternalOutput").ap()
    Ko = nc.dram_tensor("K", [HKV, 128, T], BF16, kind="ExternalOutput").ap()
    Vo = nc.dram_tensor("V", [T, HKV * DH], BF16, kind="ExternalOutput").ap()
    emit_qkv(C, xn, Wq, Wk, Wv, cq, sq, ck, sk, Qo, Ko, Vo)
    return C.finish()


def build_attn():
    C = _newC()
    nc = C.nc
    Qi = nc.dram_tensor("Q", [HQ, 128, T], BF16, kind="ExternalInput").ap()
    Ko = nc.dram_tensor("Kown", [HKV, 128, T], BF16, kind="ExternalInput").ap()
    Kp = nc.dram_tensor("Kpart", [HKV, 128, T], BF16, kind="ExternalInput").ap()
    Vo_ = nc.dram_tensor("Vown", [T, HKV * DH], BF16, kind="ExternalInput").ap()
    Vp = nc.dram_tensor("Vpart", [T, HKV * DH], BF16, kind="ExternalInput").ap()
    Mi = nc.dram_tensor("masks", [NTT, 128, 512], BF16, kind="ExternalInput").ap()
    Co = nc.dram_tensor("ctx", [HQ, 128, T], BF16, kind="ExternalOutput").ap()
    emit_attn(C, Qi, Ko, Kp, Vo_, Vp, Mi, Co)
    return C.finish()


def build_combine(kind):
    C = _newC()
    nc = C.nc
    nK = CD if kind == "wo" else HB
    A = nc.dram_tensor("A", [nK, 128, T], BF16, kind="ExternalInput").ap()
    if kind == "wo":
        W = nc.dram_tensor("W", [D, D], BF16, kind="ExternalInput").ap()
    else:
        W = nc.dram_tensor("W", [CD, HB, 128, 128], BF16, kind="ExternalInput").ap()
    xprev = nc.dram_tensor("xprev", [T, D], F32, kind="ExternalInput").ap()
    xnew = nc.dram_tensor("xnew", [T, D], F32, kind="ExternalOutput").ap()
    xn2 = nc.dram_tensor("xn2", [CD, 128, T], BF16, kind="ExternalOutput").ap()
    emit_combine(C, kind, A, W, xprev, xnew, xn2)
    return C.finish()


def build_ffn0up():
    C = _newC()
    nc = C.nc
    xn = nc.dram_tensor("xn", [CD, 128, T], BF16, kind="ExternalInput").ap()
    w1 = nc.dram_tensor("w1", [352256, 64], I32, kind="ExternalInput").ap()
    s1 = nc.dram_tensor("s1", [704512], F32, kind="ExternalInput").ap()
    w3 = nc.dram_tensor("w3", [352256, 64], I32, kind="ExternalInput").ap()
    s3 = nc.dram_tensor("s3", [704512], F32, kind="ExternalInput").ap()
    ha = nc.dram_tensor("ha", [HB, 128, T], BF16, kind="ExternalOutput").ap()
    emit_ffn0up(C, xn, w1, s1, w3, s3, ha)
    return C.finish()


def build_w2t():
    C = _newC()
    nc = C.nc
    w2 = nc.dram_tensor("w2", [352256, 64], I32, kind="ExternalInput").ap()
    s2 = nc.dram_tensor("s2", [704512], F32, kind="ExternalInput").ap()
    W2T = nc.dram_tensor("W2T", [CD, HB, 128, 128], BF16, kind="ExternalOutput").ap()
    emit_w2t(C, w2, s2, W2T)
    return C.finish()


def build_ffn1():
    C = _newC()
    nc = C.nc
    xn = nc.dram_tensor("xn", [CD, 128, T], BF16, kind="ExternalInput").ap()
    Wup = nc.dram_tensor("Wup", [D, D], BF16, kind="ExternalInput").ap()
    gateT = nc.dram_tensor("gateT", [D, 4], F32, kind="ExternalInput").ap()
    ebias = nc.dram_tensor("ebias", [1, 4], F32, kind="ExternalInput").ap()
    AT = nc.dram_tensor("AT", [D, 32], F32, kind="ExternalInput").ap()
    Ball = nc.dram_tensor("Ball", [32, D], F32, kind="ExternalInput").ap()
    xprev = nc.dram_tensor("xprev", [T, D], F32, kind="ExternalInput").ap()
    xnew = nc.dram_tensor("xnew", [T, D], F32, kind="ExternalOutput").ap()
    emit_ffn1(C, xn, Wup, gateT, ebias, AT, Ball, xprev, xnew)
    return C.finish()


def build_megaA():
    C = _newC()
    nc = C.nc
    EI = lambda n, s, d: nc.dram_tensor(n, s, d, kind="ExternalInput").ap()
    EO = lambda n, s, d: nc.dram_tensor(n, s, d, kind="ExternalOutput").ap()
    IN = lambda n, s, d: nc.dram_tensor(n, s, d, kind="Internal").ap()
    x = EI("x", [T, D], F32)
    xpart = EI("xpart", [T, D], F32)
    Wq0 = EI("Wq0", [D, D], F32)
    Wk0 = EI("Wk0", [D, HKV * DH], F32)
    Wv0 = EI("Wv0", [D, HKV * DH], F32)
    Wo0 = EI("Wo0", [D, D], F32)
    w1 = EI("w1", [352256, 64], I32)
    s1 = EI("s1", [704512], F32)
    w2 = EI("w2", [352256, 64], I32)
    s2 = EI("s2", [704512], F32)
    w3 = EI("w3", [352256, 64], I32)
    s3 = EI("s3", [704512], F32)
    Wq1 = EI("Wq1", [D, D], F32)
    Wk1 = EI("Wk1", [D, HKV * DH], F32)
    Wv1 = EI("Wv1", [D, HKV * DH], F32)
    cq0 = EI("cq0", [128, T], F32)
    sq0 = EI("sq0", [128, T], F32)
    ck0 = EI("ck0", [128, T], F32)
    sk0 = EI("sk0", [128, T], F32)
    ck0p = EI("ck0p", [128, T], F32)
    sk0p = EI("sk0p", [128, T], F32)
    cq1 = EI("cq1", [128, T], F32)
    sq1 = EI("sq1", [128, T], F32)
    ck1 = EI("ck1", [128, T], F32)
    sk1 = EI("sk1", [128, T], F32)
    masks = EI("masks", [NTT, 128, 512], BF16)
    x2o = EO("x2", [T, D], F32)
    Q1o = EO("Q1", [HQ, 128, T], BF16)
    K1o = EO("K1", [HKV, 128, T], BF16)
    V1o = EO("V1", [T, HKV * DH], BF16)

    xn0 = IN("xn0", [CD, 128, T], BF16)
    xn0p = IN("xn0p", [CD, 128, T], BF16)
    Q0 = IN("Q0", [HQ, 128, T], BF16)
    K0 = IN("K0", [HKV, 128, T], BF16)
    V0 = IN("V0", [T, HKV * DH], BF16)
    K0p = IN("K0p", [HKV, 128, T], BF16)
    V0p = IN("V0p", [T, HKV * DH], BF16)
    ctx0 = IN("ctx0", [HQ, 128, T], BF16)
    Wo0b = IN("Wo0b", [D, D], BF16)
    x1 = IN("x1", [T, D], F32)
    xnf = IN("xnf", [CD, 128, T], BF16)
    ha = IN("ha", [HB, 128, T], BF16)
    W2T = IN("W2T", [CD, HB, 128, 128], BF16)
    x2i = IN("x2i", [T, D], F32)
    xn3 = IN("xn3", [CD, 128, T], BF16)

    emit_ln0(C, x, xn0)
    C.join(); C.section()
    emit_ln0(C, xpart, xn0p)
    C.join(); C.section()
    emit_qkv(C, xn0, Wq0, Wk0, Wv0, cq0, sq0, ck0, sk0, Q0, K0, V0)
    C.join(); C.section()
    emit_qkv(C, xn0p, None, Wk0, Wv0, None, None, ck0p, sk0p, None, K0p, V0p)
    C.join(); C.section()
    emit_attn(C, Q0, K0, K0p, V0, V0p, masks, ctx0)
    C.join(); C.section()
    emit_precast(C, Wo0, Wo0b)
    C.join(); C.section()
    emit_combine(C, "wo", ctx0, Wo0b, x, x1, xnf)
    C.join(); C.section()
    emit_w2t(C, w2, s2, W2T)
    C.join(); C.section()
    emit_ffn0up(C, xnf, w1, s1, w3, s3, ha)
    C.join(); C.section()
    emit_combine(C, "ffn0down", ha, W2T, x1, x2o, xn3)
    C.join(); C.section()
    emit_qkv(C, xn3, Wq1, Wk1, Wv1, cq1, sq1, ck1, sk1, Q1o, K1o, V1o)
    return C.finish()


def build_megaB():
    C = _newC()
    nc = C.nc
    EI = lambda n, s, d: nc.dram_tensor(n, s, d, kind="ExternalInput").ap()
    EO = lambda n, s, d: nc.dram_tensor(n, s, d, kind="ExternalOutput").ap()
    IN = lambda n, s, d: nc.dram_tensor(n, s, d, kind="Internal").ap()
    x2 = EI("x2", [T, D], F32)
    Q1 = EI("Q1", [HQ, 128, T], BF16)
    K1 = EI("K1", [HKV, 128, T], BF16)
    K1p = EI("K1p", [HKV, 128, T], BF16)
    V1 = EI("V1", [T, HKV * DH], BF16)
    V1p = EI("V1p", [T, HKV * DH], BF16)
    masks = EI("masks", [NTT, 128, 512], BF16)
    Wo1 = EI("Wo1", [D, D], F32)
    Wup = EI("Wup", [D, D], F32)
    gateT = EI("gateT", [D, 4], F32)
    ebias = EI("ebias", [1, 4], F32)
    AT = EI("AT", [D, 32], F32)
    Ball = EI("Ball", [32, D], F32)
    x4o = EO("x4", [T, D], F32)
    ctx1 = IN("ctx1", [HQ, 128, T], BF16)
    Wo1b = IN("Wo1b", [D, D], BF16)
    Wupb = IN("Wupb", [D, D], BF16)
    x3 = IN("x3", [T, D], F32)
    xn4 = IN("xn4", [CD, 128, T], BF16)

    emit_attn(C, Q1, K1, K1p, V1, V1p, masks, ctx1)
    C.join(); C.section()
    emit_precast(C, Wo1, Wo1b)
    C.join(); C.section()
    emit_precast(C, Wup, Wupb)
    C.join(); C.section()
    emit_combine(C, "wo", ctx1, Wo1b, x2, x3, xn4)
    C.join(); C.section()
    emit_ffn1(C, xn4, Wupb, gateT, ebias, AT, Ball, x3, x4o)
    return C.finish()



_PROGS = {}


def _prog(name, fn):
    if name not in _PROGS:
        _PROGS[name] = fn()
    return _PROGS[name]


def _run(name, fn, in_maps):
    nc = _prog(name, fn)
    return run_bass_kernel_spmd(nc, in_maps, core_ids=list(range(8))).results


def kernel(**inputs):
    ins = {k: np.asarray(v) for k, v in inputs.items()}
    x0 = ins["L0_attn_resid"]
    NCORE = 8
    roles = [c % 2 for c in range(NCORE)]
    batches = [c // 2 for c in range(NCORE)]
    pos = {r: tok_positions(r) for r in (0, 1)}
    masks = {r: diag_masks(r) for r in (0, 1)}
    sc = 1.0 / np.sqrt(128.0)
    tabs = {}
    for layer in (0, 1):
        for r in (0, 1):
            Ct, St = rope_tables(layer, r)
            tabs[(layer, r)] = (Ct, St)

    gateT = np.ascontiguousarray(ins["L1_ffn_gate_w"].T)
    AT = np.ascontiguousarray(ins["L1_ffn_lora_A"].transpose(2, 0, 1).reshape(D, 32))
    Ball = np.ascontiguousarray(ins["L1_ffn_lora_B"].transpose(0, 2, 1).reshape(32, D))
    ebias = np.ascontiguousarray(ins["L1_ffn_expert_bias"].reshape(1, 4))

    in_maps_A = []
    for c in range(NCORE):
        r, b = roles[c], batches[c]
        C0, S0 = tabs[(0, r)]
        C0p, S0p = tabs[(0, 1 - r)]
        C1, S1 = tabs[(1, r)]
        in_maps_A.append({
            "x": np.ascontiguousarray(x0[b][pos[r]]),
            "xpart": np.ascontiguousarray(x0[b][pos[1 - r]]),
            "Wq0": ins["L0_attn_Wq"], "Wk0": ins["L0_attn_Wk"],
            "Wv0": ins["L0_attn_Wv"], "Wo0": ins["L0_attn_Wo"],
            "w1": ins["L0_ffn_w1"], "s1": ins["L0_ffn_s1"],
            "w2": ins["L0_ffn_w2"], "s2": ins["L0_ffn_s2"],
            "w3": ins["L0_ffn_w3"], "s3": ins["L0_ffn_s3"],
            "Wq1": ins["L1_attn_Wq"], "Wk1": ins["L1_attn_Wk"],
            "Wv1": ins["L1_attn_Wv"],
            "cq0": C0 * sc, "sq0": S0 * sc, "ck0": C0, "sk0": S0,
            "ck0p": C0p, "sk0p": S0p,
            "cq1": C1 * sc, "sq1": S1 * sc, "ck1": C1, "sk1": S1,
            "masks": masks[r],
        })
    resA = _run("megaA", build_megaA, in_maps_A)

    in_maps_B = []
    for c in range(NCORE):
        r, b = roles[c], batches[c]
        partner = 2 * b + (1 - r)
        in_maps_B.append({
            "x2": resA[c]["x2"],
            "Q1": resA[c]["Q1"],
            "K1": resA[c]["K1"], "K1p": resA[partner]["K1"],
            "V1": resA[c]["V1"], "V1p": resA[partner]["V1"],
            "masks": masks[r],
            "Wo1": ins["L1_attn_Wo"],
            "Wup": ins["L1_ffn_W_up"], "gateT": gateT, "ebias": ebias,
            "AT": AT, "Ball": Ball,
        })
    resB = _run("megaB", build_megaB, in_maps_B)

    out = np.empty((4, TG, D), np.float32)
    for c in range(NCORE):
        out[batches[c]][pos[roles[c]]] = resB[c]["x4"]
    return out


# version 09
# speedup vs baseline: 1.3698x; 1.0767x over previous
import math
import contextlib
import numpy as np
import ml_dtypes
import concourse.bass as bass
import concourse.mybir as mybir
from concourse.bass_utils import run_bass_kernel_spmd

F32 = mybir.dt.float32
BF16 = mybir.dt.bfloat16
I32 = mybir.dt.int32
U8 = mybir.dt.uint8
AF = mybir.ActivationFunctionType
ALU = mybir.AluOpType
AX = mybir.AxisListType.X
BF = ml_dtypes.bfloat16

D = 4096
T = 2048
TG = 4096
NTT = T // 128
CD = D // 128
HQ, HKV, DH = 32, 8, 128
HID = 11008
HB = HID // 128
EPS = 1e-5
NEG = -1e30
DTS = {F32: 4, BF16: 2, I32: 4, U8: 1}
POOLB = 212736

SEMNAMES = ["ld1", "ld2", "wld", "mm", "dve", "acs", "pml", "st1", "st2",
            "x1", "x2", "x3", "bar"]

IDENT = np.eye(128).astype(BF)
IDENTF = np.eye(128, dtype=np.float32)


class Sem:
    def __init__(self, es, nc, name):
        self.h = es.enter_context(nc.semaphore(name))
        self.v = 0

    def inc(self, instr, n=1):
        instr.then_inc(self.h, n)
        self.v += n
        return instr


def _dma(q, out, in_, sem):
    sem.inc(q.dma_start(out=out, in_=in_), 16)


class Ctx:

    def __init__(self):
        self.nc = bass.Bass("TRN2", target_bir_lowering=False)
        self.es = contextlib.ExitStack()
        self._off = 0
        self._psum_i = 0
        self._inline = {}

    def setup(self):
        nc = self.nc
        self.pool = self.es.enter_context(nc.sbuf_tensor("pool", [128, POOLB], U8))
        self.banks = [self.es.enter_context(nc.psum_tensor(f"pp{i}", [128, 512], F32))
                      for i in range(8)]
        self.s = {n: Sem(self.es, nc, n) for n in SEMNAMES}
        self.es.enter_context(nc.Block())

    def inline(self, name, arr):
        if name not in self._inline:
            self._inline[name] = self.nc.inline_tensor(arr, name).ap()
        return self._inline[name]

    def section(self):
        self._off = 0
        self._psum_i = 0

    def carve(self, shape, dtype):
        nb = int(np.prod(shape[1:])) * DTS[dtype]
        off = (self._off + 63) & ~63
        self._off = off + nb
        assert self._off <= POOLB, f"pool overflow: {self._off}"
        v = self.pool[0:shape[0], off:off + nb].bitcast(dtype)
        if len(shape) > 2:
            dims = " ".join(f"d{i}" for i in range(1, len(shape)))
            kw = {f"d{i}": shape[i] for i in range(1, len(shape))}
            v = v.rearrange(f"p ({dims}) -> p {dims}", **kw)
        return v

    def psum(self, dtype=F32):
        t = self.banks[self._psum_i]
        self._psum_i += 1
        if dtype == F32:
            return t[:]
        return t[:].bitcast(dtype)[:, 0:512]

    def psumb(self):
        t = self.banks[self._psum_i]
        self._psum_i += 1
        return t[:].bitcast(BF16)

    def join(self):
        nc = self.nc
        b = self.s["bar"]
        engs = [nc.tensor, nc.vector, nc.scalar, nc.gpsimd, nc.sync]
        for e in engs:
            e.nop()
            b.inc(e.nop(), 1)
        for e in engs:
            e.wait_ge(b.h, b.v)

    def finish(self):
        self.es.close()
        return self.nc


def tok_positions(role):
    return np.concatenate([np.arange(256 * i + 128 * role, 256 * i + 128 * (role + 1))
                           for i in range(NTT)])


def key_globals(role):
    g = np.empty(TG, np.int64)
    for u in range(16):
        g[256 * u:256 * u + 128] = 256 * u + 128 * role + np.arange(128)
        g[256 * u + 128:256 * u + 256] = 256 * u + 128 * (1 - role) + np.arange(128)
    return g


def diag_masks(role):
    kg = key_globals(role)
    m = np.zeros((NTT, 128, 512), np.float32)
    for ti in range(NTT):
        g0 = 256 * ti + 128 * role
        nk5 = ti // 2 + 1
        k0 = 512 * (nk5 - 1)
        kj = kg[k0:k0 + 512][None, :]
        qi = np.arange(128)[:, None]
        m[ti] = np.where(kj <= g0 + qi, 0.0, NEG)
    return m.astype(BF)


def rope_tables(layer, role):
    pos = tok_positions(role).astype(np.float64)
    return rope_tables_pos(layer, pos)


def rope_tables_pos(layer, pos):
    if layer == 0:
        theta = 500000.0
        idx = np.arange(0, DH, 2, dtype=np.float64) / DH
        freq = 1.0 / (2.0 * math.pi) * theta ** (-idx)
        factor, lo, hi, L0 = 32.0, 1.0, 4.0, 8192.0
        freq_low, freq_high = lo / L0, hi / L0
        freq_scaled = np.where(freq < freq_low, freq / factor, freq)
        smooth = np.clip((L0 * freq - lo) / (hi - lo), 0.0, 1.0)
        freq_smooth = (1.0 - smooth) * (freq / factor) + smooth * freq
        is_mid = (freq >= freq_low) & (freq <= freq_high)
        freq = np.where(is_mid, freq_smooth, freq_scaled)
        phase = 2.0 * math.pi * pos[:, None] * freq[None, :]
    else:
        theta = 10000.0
        inv = 1.0 / theta ** (np.arange(0, DH, 2, dtype=np.float64)[: DH // 2] / DH)
        phase = pos[:, None] * inv[None, :]
    cos = np.cos(phase)
    sin = np.sin(phase)
    Ct = np.empty((128, T), np.float64)
    St = np.empty((128, T), np.float64)
    Ct[:64] = cos.T
    Ct[64:] = cos.T
    St[:64] = -sin.T
    St[64:] = sin.T
    return Ct.astype(np.float32), St.astype(np.float32)


def emit_ln_stats(nc, xt, xsq, st):
    sx, sxx, m, r, m2, sxxd, ve = (st[:, i:i + 1] for i in range(7))
    nc.vector.tensor_scalar(out=xsq, in0=xt, scalar1=1.0, scalar2=0.0,
                            op0=ALU.mult, op1=ALU.add, accum_out=sx)
    last = nc.vector.scalar_tensor_tensor(out=xsq, in0=xt, scalar=1.0, in1=xt,
                                          op0=ALU.mult, op1=ALU.mult, accum_out=sxx)

    def act1(d=D):
        nc.scalar.activation(out=m2, in_=sx, func=AF.Square, scale=1.0 / d)
        nc.scalar.activation(out=sxxd, in_=sxx, func=AF.Copy, scale=1.0 / d, bias=EPS)
        return nc.scalar.activation(out=m, in_=sx, func=AF.Copy, scale=1.0 / d)

    def vec2():
        return nc.vector.tensor_tensor(out=ve, in0=sxxd, in1=m2, op=ALU.subtract)

    def act2():
        nc.scalar.activation(out=ve, in_=ve, func=AF.Ln)
        return nc.scalar.activation(out=r, in_=ve, func=AF.Exp, scale=-0.5)

    return m, r, last, (act1, vec2, act2)



def emit_ln0(C, x, xn):
    nc = C.nc
    xt = [C.carve([128, D], F32) for _ in range(2)]
    xb = C.carve([128, D], BF16)
    xsq = C.carve([128, D], F32)
    st = C.carve([128, 8], F32)
    idt = C.carve([128, 128], BF16)
    stage = [C.carve([128, CD, 128], BF16) for _ in range(2)]
    pst = [C.psumb() for _ in range(2)]
    ldx, cmp, act, trn, evc, stx = (C.s[n] for n in
                                    ("ld1", "x1", "acs", "mm", "x2", "st1"))
    trn0, evc0, stx0 = trn.v, evc.v, stx.v
    _dma(nc.sync, idt[:], C.inline("identb", IDENT)[:], ldx)
    applied_v = {}
    for tt in range(NTT):
        b = tt % 2
        if tt >= 2:
            nc.sync.wait_ge(cmp.h, applied_v[tt - 2])
        _dma(nc.sync, xt[b][:], x[tt * 128:(tt + 1) * 128, :], ldx)
        nc.vector.wait_ge(ldx.h, ldx.v)
        if tt >= 1:
            nc.vector.wait_ge(trn.h, trn0 + 32 * tt)
        m, r, last, (a1, v2, a3) = emit_ln_stats(nc, xt[b][:], xsq[:], st[:])
        cmp.inc(last, 1)
        nc.scalar.wait_ge(cmp.h, cmp.v)
        act.inc(a1(), 1)
        nc.vector.wait_ge(act.h, act.v)
        cmp.inc(v2(), 1)
        nc.scalar.wait_ge(cmp.h, cmp.v)
        act.inc(a3(), 1)
        nc.vector.wait_ge(act.h, act.v)
        cmp.inc(nc.vector.tensor_scalar(out=xb[:], in0=xt[b][:], scalar1=m, scalar2=r,
                                        op0=ALU.subtract, op1=ALU.mult), 1)
        applied_v[tt] = cmp.v
        nc.tensor.wait_ge(cmp.h, cmp.v)
        if tt >= 2:
            nc.scalar.wait_ge(stx.h, stx0 + 16 * (tt - 1))
        for g in range(4):
            bank = pst[g % 2]
            need = 4 * tt + g - 1
            if need > 0:
                nc.tensor.wait_ge(evc.h, evc0 + need)
            for j in range(8):
                c = g * 8 + j
                trn.inc(nc.tensor.transpose(bank[:, j * 128:(j + 1) * 128],
                                            xb[:, c * 128:(c + 1) * 128], idt[:]), 1)
            nc.scalar.wait_ge(trn.h, trn.v)
            evc.inc(nc.scalar.copy(
                out=stage[b][:, g * 8:(g + 1) * 8, :].rearrange("p c q -> p (c q)"),
                in_=bank[:]), 1)
        nc.gpsimd.wait_ge(evc.h, evc.v)
        _dma(nc.gpsimd,
             xn[:, :, tt * 128:(tt + 1) * 128].rearrange("c p q -> p c q"),
             stage[b][:], stx)
    nc.sync.wait_ge(stx.h, stx.v)
    nc.gpsimd.wait_ge(stx.h, stx.v)


def emit_qkv(C, xn, Wq, Wk, Wv, cq, sq, ck, sk, Qo, Ko, Vo):
    nc = C.nc
    xn_sb = C.carve([128, CD, T], BF16)
    tabs = {n: C.carve([128, T], F32) for n in ("cq", "sq", "ck", "sk")}
    wbuf = [C.carve([128, CD, 128], BF16) for _ in range(2)]
    rp1 = [C.carve([128, 512], F32) for _ in range(2)]
    prt = [C.carve([128, 512], F32) for _ in range(2)]
    rp2 = [C.carve([128, 512], F32) for _ in range(2)]
    qstage = [C.carve([128, 512], BF16) for _ in range(2)]
    vf = C.carve([128, T], BF16)
    idt = C.carve([128, 128], BF16)
    vts = [C.carve([128, 8, 128], BF16) for _ in range(2)]
    ps = [C.psum(F32) for _ in range(8)]
    psV = [ps[6].bitcast(BF16), ps[7].bitcast(BF16)]
    ldc, wld, mm, evq, pse, stq = (C.s[n] for n in
                                   ("ld1", "wld", "mm", "x1", "x2", "st1"))
    wld0, evq0, stq0 = wld.v, evq.v, stq.v

    _dma(nc.sync, xn_sb[:], xn[:].rearrange("c p t -> p c t"), ldc)
    for nm, src in (("cq", cq), ("sq", sq), ("ck", ck), ("sk", sk)):
        if src is not None:
            _dma(nc.sync, tabs[nm][:], src[:], ldc)
    _dma(nc.sync, idt[:], C.inline("identb", IDENT)[:], ldc)

    mmv = {}
    rope_i = 0
    nload = 0

    def qk_part(W, nOC, ctab, stab, OUT, base_oc, wslice):
        nonlocal rope_i, nload
        for j in range(nOC):
            oc = base_oc + j
            par = oc % 2
            banks = ps[par * 4:par * 4 + 4]
            if oc - 2 in mmv:
                nc.gpsimd.wait_ge(mm.h, mmv[oc - 2])
            _dma(nc.gpsimd, wbuf[par][:],
                 W[:, wslice + j * 128:wslice + (j + 1) * 128]
                 .rearrange("(c p) o -> p c o", p=128), wld)
            nload += 1
            nc.tensor.wait_ge(wld.h, wld0 + 16 * nload)
            if oc == 0:
                nc.tensor.wait_ge(ldc.h, ldc.v)
            if oc >= 2:
                nc.tensor.wait_ge(evq.h, evq0 + 4 * (oc - 1))
            for c in range(CD):
                for t4 in range(4):
                    ins = nc.tensor.matmul(banks[t4][:], wbuf[par][:, c, :],
                                           xn_sb[:, c, t4 * 512:(t4 + 1) * 512],
                                           start=(c == 0), stop=(c == CD - 1))
            mm.inc(ins, 1)
            mmv[oc] = mm.v
            for t4 in range(4):
                rpar = rope_i % 2
                sl = slice(t4 * 512, (t4 + 1) * 512)
                nc.vector.wait_ge(mm.h, mmv[oc])
                if rope_i >= 2:
                    nc.vector.wait_ge(stq.h, stq0 + 16 * (rope_i - 1))
                nc.vector.tensor_copy(prt[rpar][0:64, :], banks[t4][64:128, :])
                nc.vector.tensor_copy(prt[rpar][64:128, :], banks[t4][0:64, :])
                nc.vector.tensor_tensor(out=rp1[rpar][:], in0=banks[t4][:],
                                        in1=ctab[:, sl], op=ALU.mult)
                evq.inc(nc.vector.tensor_tensor(out=rp2[rpar][:], in0=prt[rpar][:],
                                                in1=stab[:, sl], op=ALU.mult), 1)
                pse.inc(nc.vector.tensor_tensor(out=qstage[rpar][:], in0=rp1[rpar][:],
                                                in1=rp2[rpar][:], op=ALU.add), 1)
                nc.gpsimd.wait_ge(pse.h, pse.v)
                _dma(nc.gpsimd, OUT[j, :, sl], qstage[rpar][:], stq)
                rope_i += 1

    nq = 0
    if Qo is not None:
        qk_part(Wq, HQ, tabs["cq"], tabs["sq"], Qo, 0, 0)
        nq = HQ
    qk_part(Wk, HKV, tabs["ck"], tabs["sk"], Ko, nq, 0)

    vstore = {}
    vevac = {}
    for j in range(HKV):
        oc = nq + HKV + j
        par = oc % 2
        banks = ps[par * 4:par * 4 + 4]
        if oc - 2 in mmv:
            nc.gpsimd.wait_ge(mm.h, mmv[oc - 2])
        _dma(nc.gpsimd, wbuf[par][:],
             Wv[:, j * 128:(j + 1) * 128].rearrange("(c p) o -> p c o", p=128), wld)
        nload += 1
        nc.tensor.wait_ge(wld.h, wld0 + 16 * nload)
        if j < 2:
            nc.tensor.wait_ge(evq.h, evq0 + 4 * (nq + HKV))
        if j >= 2:
            nc.tensor.wait_ge(evq.h, vevac[(j - 2, 1)])
        for c in range(CD):
            for t4 in range(4):
                ins = nc.tensor.matmul(banks[t4][:], wbuf[par][:, c, :],
                                       xn_sb[:, c, t4 * 512:(t4 + 1) * 512],
                                       start=(c == 0), stop=(c == CD - 1))
        mm.inc(ins, 1)
        mmv[oc] = mm.v
        nc.scalar.wait_ge(mm.h, mmv[oc])
        for t4 in range(4):
            evq.inc(nc.scalar.copy(out=vf[:, t4 * 512:(t4 + 1) * 512],
                                   in_=banks[t4][:]), 1)
        nc.tensor.wait_ge(evq.h, evq.v)
        for g in range(2):
            slot = g % 2
            tb = psV[slot]
            if (j - 1, slot) in vevac:
                nc.tensor.wait_ge(evq.h, vevac[(j - 1, slot)])
            for t4 in range(8):
                tt = g * 8 + t4
                ins = nc.tensor.transpose(tb[:, t4 * 128:(t4 + 1) * 128],
                                          vf[:, tt * 128:(tt + 1) * 128], idt[:])
            mm.inc(ins, 1)
            nc.vector.wait_ge(mm.h, mm.v)
            if slot in vstore:
                nc.vector.wait_ge(stq.h, vstore[slot])
            evq.inc(nc.vector.tensor_copy(
                vts[slot][:].rearrange("p a d -> p (a d)"), tb[:]), 1)
            vevac[(j, slot)] = evq.v
            nc.gpsimd.wait_ge(evq.h, evq.v)
            _dma(nc.gpsimd,
                 Vo[g * 1024:(g + 1) * 1024, j * 128:(j + 1) * 128]
                 .rearrange("(a p) d -> p a d", p=128),
                 vts[slot][:], stq)
            vstore[slot] = stq.v
    nc.sync.wait_ge(stq.h, stq.v)
    nc.gpsimd.wait_ge(stq.h, stq.v)


def emit_attn(C, Qi, Kown, Kpart, Vown, Vpart, Mi, Co):
    nc = C.nc
    K_sb = C.carve([128, HKV, TG], BF16)
    V_sb = C.carve([128, TG // 128, HKV * DH], BF16)
    M_sb = [C.carve([128, 512], BF16) for _ in range(2)]
    idt = C.carve([128, 128], BF16)
    qa = C.carve([128, HQ, 128], BF16)
    S_sb = C.carve([128, TG], F32)
    Pb = [C.carve([128, TG], BF16) for _ in range(2)]
    PT = C.carve([128, TG // 128, 4, 128], BF16)
    ctst = [C.carve([128, 4, 128], BF16) for _ in range(2)]
    st = C.carve([128, 32], F32)
    psS = [C.psum(F32) for _ in range(3)]
    psT = [C.psumb() for _ in range(2)]
    psC = [C.psum(F32) for _ in range(2)]
    ldc, ldq, mm, dmx, aev, pml, stc = (C.s[n] for n in
                                        ("ld1", "ld2", "mm", "dve", "acs", "pml", "st1"))

    for h in range(HKV):
        for s_, Ksrc in ((0, Kown), (1, Kpart)):
            _dma(nc.sync,
                 K_sb[:, h, :].rearrange("p (u s q) -> p u s q", s=2, q=128)[:, :, s_, :],
                 Ksrc[h].rearrange("p (u q) -> p u q", q=128), ldc)
    for s_, Vsrc in ((0, Vown), (1, Vpart)):
        _dma(nc.sync,
             V_sb.rearrange("p (u s) d -> p u s d", s=2)[:, :, s_, :],
             Vsrc[:].rearrange("(u p) d -> p u d", p=128), ldc)
    _dma(nc.sync, idt[:], C.inline("identb", IDENT)[:], ldc)

    negmax = [st[:, 16 + i:17 + i] for i in range(2)]
    lsum = [st[:, 18 + i:19 + i] for i in range(2)]
    linv = [st[:, 20 + i:21 + i] for i in range(2)]
    linv2 = [st[:, 22 + i:23 + i] for i in range(2)]

    ctx_store = {}
    qtile_mm_done = {}
    bank_free = [None, None, None]
    ptbank_free = [None, None]
    ctx_evac = {}
    first = [True]
    prev_exp = [None]
    pb_free = {0: None, 1: None}
    pending = [None]

    def s_stage(ti, kvh, h4, nk5, hpar):
        qh = kvh * 4 + h4
        for kt in range(nk5):
            bk = kt % 3
            bank = psS[bk]
            if bank_free[bk] is not None:
                s_, v_ = bank_free[bk]
                nc.tensor.wait_ge(s_.h, v_)
            ins = nc.tensor.matmul(bank[:, :], qa[:, qh, :],
                                   K_sb[:, kvh, kt * 512:(kt + 1) * 512],
                                   start=True, stop=True)
            mm.inc(ins, 1)
            sl = slice(kt * 512, (kt + 1) * 512)
            if kt == nk5 - 1:
                nc.vector.wait_ge(mm.h, mm.v)
                if prev_exp[0] is not None:
                    nc.vector.wait_ge(aev.h, prev_exp[0])
                dmx.inc(nc.vector.scalar_tensor_tensor(
                    out=S_sb[:, sl], in0=bank[:, :], scalar=1.0,
                    in1=M_sb[ti % 2][:], op0=ALU.mult, op1=ALU.add), 1)
                bank_free[bk] = (dmx, dmx.v)
            else:
                nc.scalar.wait_ge(mm.h, mm.v)
                if prev_exp[0] is not None:
                    pass
                aev.inc(nc.scalar.copy(out=S_sb[:, sl], in_=bank[:, :]), 1)
                bank_free[bk] = (aev, aev.v)
        nc.scalar.wait_ge(dmx.h, dmx.v)
        if pb_free[hpar] is not None:
            nc.scalar.wait_ge(mm.h, pb_free[hpar])
        aev.inc(nc.scalar.activation(
            out=Pb[hpar][:, 0:nk5 * 512], in_=S_sb[:, 0:nk5 * 512], func=AF.Exp,
            bias=0.0, accum_out=lsum[hpar]), 1)
        prev_exp[0] = aev.v
        nc.vector.wait_ge(aev.h, aev.v)
        dmx.inc(nc.vector.reciprocal(linv[hpar], lsum[hpar]), 1)
        nc.gpsimd.wait_ge(dmx.h, dmx.v)
        pml.inc(nc.gpsimd.tensor_copy(linv2[hpar], linv[hpar]), 1)
        nc.vector.wait_ge(pml.h, pml.v)
        dmx.inc(nc.vector.tensor_scalar(
            out=Pb[hpar][:, 0:nk5 * 512], in0=Pb[hpar][:, 0:nk5 * 512],
            scalar1=linv2[hpar], scalar2=None, op0=ALU.mult), 1)
        return dmx.v

    def t_stage(h4, nk1, hpar, pscale_v):
        nc.tensor.wait_ge(dmx.h, pscale_v)
        for g in range((nk1 + 7) // 8):
            bk = g % 2
            tb = psT[bk]
            if ptbank_free[bk] is not None:
                nc.tensor.wait_ge(dmx.h, ptbank_free[bk])
            nkk = min(8, nk1 - g * 8)
            for j in range(nkk):
                kc = g * 8 + j
                ins = nc.tensor.transpose(
                    tb[:, j * 128:(j + 1) * 128],
                    Pb[hpar][:, kc * 128:(kc + 1) * 128], idt[:])
            mm.inc(ins, 1)
            nc.vector.wait_ge(mm.h, mm.v)
            dmx.inc(nc.vector.tensor_copy(
                PT[:, g * 8:g * 8 + nkk, h4, :],
                tb[:].rearrange("p (a q) -> p a q", a=8)[:, 0:nkk, :]), 1)
            ptbank_free[bk] = dmx.v
        pb_free[hpar] = mm.v

    for ti in range(NTT):
        lq0 = ti * 128
        qpar = ti % 2
        nk5 = ti // 2 + 1
        nk1 = 4 * nk5
        if ti >= 1:
            nc.sync.wait_ge(mm.h, qtile_mm_done[ti - 1])
        _dma(nc.sync, qa[:], Qi[:, :, lq0:lq0 + 128].rearrange("h p q -> p h q"), ldq)
        _dma(nc.sync, M_sb[qpar][:], Mi[ti], ldq)
        nc.tensor.wait_ge(ldq.h, ldq.v)
        if first[0]:
            nc.tensor.wait_ge(ldc.h, ldc.v)
            first[0] = False
        for kvh in range(HKV):
            cpar = (ti * HKV + kvh) % 2
            pv = {}
            pv[0] = s_stage(ti, kvh, 0, nk5, 0)
            pv[1] = s_stage(ti, kvh, 1, nk5, 1)
            if pending[0] is not None:
                pending[0]()
                pending[0] = None
            t_stage(0, nk1, 0, pv[0])
            pv[2] = s_stage(ti, kvh, 2, nk5, 0)
            t_stage(1, nk1, 1, pv[1])
            pv[3] = s_stage(ti, kvh, 3, nk5, 1)
            t_stage(2, nk1, 0, pv[2])
            t_stage(3, nk1, 1, pv[3])
            pt_ready = dmx.v

            def mk_pv(ti_=ti, kvh_=kvh, nk1_=nk1, cpar_=cpar, lq0_=lq0,
                      ptv_=pt_ready):
                def do():
                    nc.tensor.wait_ge(dmx.h, ptv_)
                    idx = ti_ * HKV + kvh_
                    if idx >= 2:
                        nc.tensor.wait_ge(aev.h, ctx_evac[idx - 2])
                    for kc in range(nk1_):
                        ins = nc.tensor.matmul(
                            psC[cpar_][:, :], V_sb[:, kc, kvh_ * 128:(kvh_ + 1) * 128],
                            PT[:, kc, :, :].rearrange("p a q -> p (a q)"),
                            start=(kc == 0), stop=(kc == nk1_ - 1))
                    mm.inc(ins, 1)
                    nc.scalar.wait_ge(mm.h, mm.v)
                    if cpar_ in ctx_store:
                        nc.scalar.wait_ge(stc.h, ctx_store[cpar_])
                    aev.inc(nc.scalar.copy(out=ctst[cpar_][:].rearrange("p a q -> p (a q)"),
                                           in_=psC[cpar_][:, :]), 1)
                    ctx_evac[idx] = aev.v
                    nc.gpsimd.wait_ge(aev.h, aev.v)
                    _dma(nc.gpsimd,
                         Co[kvh_ * 4:(kvh_ + 1) * 4, :, lq0_:lq0_ + 128]
                         .rearrange("h p q -> p h q"),
                         ctst[cpar_][:], stc)
                    ctx_store[cpar_] = stc.v
                return do
            pending[0] = mk_pv()
        qtile_mm_done[ti] = mm.v
    if pending[0] is not None:
        pending[0]()
        pending[0] = None
    nc.sync.wait_ge(stc.h, stc.v)
    nc.gpsimd.wait_ge(stc.h, stc.v)


def emit_combine(C, kind, A, W, xprev, xnew, xn2):
    nc = C.nc
    nK = CD if kind == "wo" else HB
    nKh = nK // 2
    nwb = 4 if kind == "wo" else 2
    A_sb = C.carve([128, nK, 512], BF16)
    wbuf = [C.carve([128, nKh, 128], BF16) for _ in range(nwb)]
    yf1 = C.carve([128, 512], F32)
    yf = [yf1, yf1]
    y4 = C.carve([128, 4, CD, 128], F32)
    xp = C.carve([128, D], F32)
    xnb = C.carve([128, D], BF16)
    st4 = C.carve([128, 64], F32)
    stage = C.carve([128, CD, 128], BF16)
    idt = C.carve([128, 128], BF16)
    idtf = C.carve([128, 128], F32)
    psY = [C.psum(F32) for _ in range(2)]
    psT = [C.psum(F32) for _ in range(2)]
    psB = [C.psumb() for _ in range(2)]
    lda, wld, mm, aev, dev, acs, stx = (C.s[n] for n in
                                        ("ld1", "wld", "mm", "acs", "dve", "x1", "st1"))
    wld0 = wld.v
    _dma(nc.sync, idt[:], C.inline("identb", IDENT)[:], lda)
    _dma(nc.sync, idtf[:], C.inline("identf", IDENTF)[:], lda)

    nwld = 0
    mm_oc = {}
    yf_ev = {}
    trn_ev = {}
    dev_ev = {}
    xnb_ev = {}
    xnb_tr = {}
    stage_store = {}
    xp_store = {}
    pend_tr = [None]
    for ts in range(4):
        tsl = slice(ts * 512, (ts + 1) * 512)
        if ts >= 1:
            nc.sync.wait_ge(mm.h, mm_oc[(ts - 1, CD - 1)])
        _dma(nc.sync, A_sb[:], A[:, :, tsl].rearrange("c p t -> p c t"), lda)
        for oc in range(CD):
            par = oc % 2
            obuf = (oc % 2) * 2 if kind == "wo" else 0
            for half in range(2):
                buf = wbuf[obuf + half]
                wprev = oc - 2 if kind == "wo" else oc - 1
                if (ts, wprev) in mm_oc:
                    nc.sync.wait_ge(mm.h, mm_oc[(ts, wprev)])
                elif ts >= 1 and oc == 0:
                    nc.sync.wait_ge(mm.h, mm_oc[(ts - 1, CD - 1)])
                if kind == "wo":
                    src = W[:, oc * 128:(oc + 1) * 128] \
                        .rearrange("(c p) o -> p c o", p=128)[:, half * nKh:(half + 1) * nKh, :]
                    _dma(nc.sync, buf[:], src, wld)
                else:
                    src = W[oc, half * nKh:(half + 1) * nKh, :, :].rearrange("c p o -> p c o")
                    _dma(nc.sync, buf[:], src, wld)
                nwld += 1
            nc.tensor.wait_ge(wld.h, wld0 + 16 * nwld)
            nc.tensor.wait_ge(lda.h, lda.v)
            if (ts, oc - 2) in yf_ev:
                nc.tensor.wait_ge(aev.h, yf_ev[(ts, oc - 2)])
            elif ts >= 1 and oc < 2:
                nc.tensor.wait_ge(aev.h, yf_ev[(ts - 1, CD - 2 + oc)])
            for c in range(nK):
                ins = nc.tensor.matmul(psY[par][:], wbuf[obuf + c // nKh][:, c % nKh, :],
                                       A_sb[:, c, :],
                                       start=(c == 0), stop=(c == nK - 1))
            mm.inc(ins, 1)
            mm_oc[(ts, oc)] = mm.v
            if pend_tr[0] is not None:
                pend_tr[0]()
            nc.scalar.wait_ge(mm.h, mm.v)
            if (ts, oc - 1) in trn_ev:
                nc.scalar.wait_ge(mm.h, trn_ev[(ts, oc - 1)])
            elif ts >= 1 and oc == 0:
                nc.scalar.wait_ge(mm.h, trn_ev[(ts - 1, CD - 1)])
            aev.inc(nc.scalar.copy(out=yf[par][:], in_=psY[par][:]), 1)
            yf_ev[(ts, oc)] = aev.v
            def mk_tr(ts_=ts, oc_=oc, par_=par, aevv=aev.v):
                def do():
                    nc.tensor.wait_ge(aev.h, aevv)
                    if (ts_, oc_ - 2) in dev_ev:
                        nc.tensor.wait_ge(dev.h, dev_ev[(ts_, oc_ - 2)])
                    elif ts_ >= 1 and oc_ < 2:
                        nc.tensor.wait_ge(dev.h, dev_ev[(ts_ - 1, CD - 2 + oc_)])
                    for a in range(4):
                        ins = nc.tensor.transpose(psT[par_][:, a * 128:(a + 1) * 128],
                                                  yf[par_][:, a * 128:(a + 1) * 128], idtf[:])
                    mm.inc(ins, 1)
                    trn_ev[(ts_, oc_)] = mm.v
                    nc.vector.wait_ge(mm.h, mm.v)
                    dev.inc(nc.vector.tensor_copy(y4[:, :, oc_, :], psT[par_][:]
                                                  .rearrange("p (a q) -> p a q", a=4)), 1)
                    dev_ev[(ts_, oc_)] = dev.v
                return do
            pend_tr[0] = mk_tr()
        if pend_tr[0] is not None:
            pend_tr[0]()
            pend_tr[0] = None
        yav = [y4[:, a, :, :].rearrange("p c q -> p (c q)") for a in range(4)]
        nc.vector.wait_ge(dev.h, dev.v)
        parts1 = []
        for a in range(4):
            m_, r_, last_, fns = emit_ln_stats(nc, yav[a], xnb[:], st4[:, 16 * a:16 * a + 8])
            parts1.append((m_, r_, fns))
        dev.inc(last_, 1)
        nc.scalar.wait_ge(dev.h, dev.v)
        for a in range(4):
            ins_ = parts1[a][2][0]()
        acs.inc(ins_, 1)
        nc.vector.wait_ge(acs.h, acs.v)
        for a in range(4):
            ins_ = parts1[a][2][1]()
        dev.inc(ins_, 1)
        nc.scalar.wait_ge(dev.h, dev.v)
        for a in range(4):
            ins_ = parts1[a][2][2]()
        acs.inc(ins_, 1)
        nc.vector.wait_ge(acs.h, acs.v)
        for a in range(4):
            tt = ts * 4 + a
            m_, r_, _ = parts1[a]
            nc.vector.tensor_scalar(out=yav[a], in0=yav[a], scalar1=m_, scalar2=r_,
                                    op0=ALU.subtract, op1=ALU.mult)
            if tt >= 1:
                nc.sync.wait_ge(dev.h, dev.v)
                nc.sync.wait_ge(stx.h, xp_store[tt - 1])
            _dma(nc.sync, xp[:], xprev[tt * 128:(tt + 1) * 128, :], lda)
            nc.vector.wait_ge(lda.h, lda.v)
            dev.inc(nc.vector.tensor_tensor(out=yav[a], in0=xp[:], in1=yav[a],
                                            op=ALU.add), 1)
            xp_store[tt] = stx.v + 16
            nc.gpsimd.wait_ge(dev.h, dev.v)
            _dma(nc.gpsimd, xnew[tt * 128:(tt + 1) * 128, :], yav[a], stx)
        if xn2 is not None:
            parts2 = []
            for a in range(4):
                m_, r_, last_, fns = emit_ln_stats(nc, yav[a], xnb[:],
                                                   st4[:, 16 * a + 8:16 * a + 16])
                parts2.append((m_, r_, fns))
            dev.inc(last_, 1)
            nc.scalar.wait_ge(dev.h, dev.v)
            for a in range(4):
                ins_ = parts2[a][2][0]()
            acs.inc(ins_, 1)
            nc.vector.wait_ge(acs.h, acs.v)
            for a in range(4):
                ins_ = parts2[a][2][1]()
            dev.inc(ins_, 1)
            nc.scalar.wait_ge(dev.h, dev.v)
            for a in range(4):
                ins_ = parts2[a][2][2]()
            acs.inc(ins_, 1)
            for a in range(4):
                tt = ts * 4 + a
                m_, r_, _ = parts2[a]
                nc.vector.wait_ge(acs.h, acs.v)
                if tt >= 1 and (tt - 1) in xnb_tr:
                    nc.vector.wait_ge(mm.h, xnb_tr[tt - 1])
                dev.inc(nc.vector.tensor_scalar(out=xnb[:], in0=yav[a], scalar1=m_,
                                                scalar2=r_, op0=ALU.subtract,
                                                op1=ALU.mult), 1)
                nc.tensor.wait_ge(dev.h, dev.v)
                stv = stage_store.get(tt - 1) if tt >= 1 else None
                if stv is not None:
                    nc.vector.wait_ge(stx.h, stv)
                for g in range(4):
                    bank = psB[g % 2]
                    need = None
                    if g >= 2:
                        need = xnb_ev[(tt, g - 2)]
                    elif tt >= 1:
                        need = xnb_ev[(tt - 1, 2 + g)]
                    if need is not None:
                        nc.tensor.wait_ge(dev.h, need)
                    for j in range(8):
                        c = g * 8 + j
                        ins = nc.tensor.transpose(bank[:, j * 128:(j + 1) * 128],
                                                  xnb[:, c * 128:(c + 1) * 128], idt[:])
                    mm.inc(ins, 1)
                    nc.vector.wait_ge(mm.h, mm.v)
                    dev.inc(nc.vector.tensor_copy(
                        stage[:, g * 8:(g + 1) * 8, :].rearrange("p c q -> p (c q)"),
                        bank[:]), 1)
                    xnb_ev[(tt, g)] = dev.v
                xnb_tr[tt] = mm.v
                nc.gpsimd.wait_ge(dev.h, dev.v)
                _dma(nc.gpsimd, xn2[:, :, tt * 128:(tt + 1) * 128].rearrange("c p q -> p c q"),
                     stage[:], stx)
                stage_store[tt] = stx.v
    nc.sync.wait_ge(stx.h, stx.v)
    nc.gpsimd.wait_ge(stx.h, stx.v)


def emit_dequant(nc, eng, pk, s_sb, Wb, mi, ni, ngrp):
    se = s_sb.rearrange("p (q two) -> p q two", two=2)
    eng.tensor_scalar(out=mi, in0=pk, scalar1=4, scalar2=None,
                      op0=ALU.arith_shift_right)
    eng.tensor_scalar(out=ni, in0=pk, scalar1=15, scalar2=8,
                      op0=ALU.bitwise_and, op1=ALU.bitwise_xor)
    eng.tensor_scalar(out=ni, in0=ni, scalar1=-8, scalar2=None, op0=ALU.add)
    eng.tensor_tensor(out=Wb[:, :, 0, :],
                      in0=mi.rearrange("p (q j) -> p q j", j=64),
                      in1=se[:, :, 0:1].broadcast_to([128, ngrp, 64]), op=ALU.mult)
    return eng.tensor_tensor(out=Wb[:, :, 1, :],
                             in0=ni.rearrange("p (q j) -> p q j", j=64),
                             in1=se[:, :, 1:2].broadcast_to([128, ngrp, 64]),
                             op=ALU.mult)


def emit_ffn0up(C, xn, w1, s1, w3, s3, ha):
    nc = C.nc
    xn_sb = C.carve([128, CD, T], BF16)
    pk1 = C.carve([128, CD * 64], I32)
    pk3 = C.carve([128, CD * 64], I32)
    mi1 = C.carve([128, CD * 64], I32)
    s1t = C.carve([128, 64], F32)
    s3t = C.carve([128, 64], F32)
    W1b = C.carve([128, CD, 2, 64], BF16)
    W3b = C.carve([128, CD, 2, 64], BF16)
    W1T = [C.carve([128, CD, 128], BF16) for _ in range(2)]
    W3T = [C.carve([128, CD, 128], BF16) for _ in range(2)]
    sil = C.carve([128, 512], BF16)
    hst = [C.carve([128, 512], BF16) for _ in range(2)]
    idt = C.carve([128, 128], BF16)
    psF = [C.psum(F32) for _ in range(4)]
    psX = [C.psumb() for _ in range(2)]
    ldc, ldp, dqq, trn, ev1, ev3, mmf, sth = (C.s[n] for n in
        ("ld1", "ld2", "x1", "mm", "acs", "dve", "x2", "st1"))

    _dma(nc.sync, xn_sb[:], xn[:].rearrange("c p t -> p c t"), ldc)
    _dma(nc.sync, idt[:], C.inline("identb", IDENT)[:], ldc)

    dq_done = {}
    mmfv = {}
    trn_done = {}
    bank_free = {0: None, 1: None}
    psf_free = {0: None, 1: None}
    sthv = {}
    last_mul = None
    for hb in range(HB):
        par = hb % 2
        if hb >= 1:
            nc.sync.wait_ge(dqq.h, dq_done[hb - 1])
        _dma(nc.sync, pk1[:].rearrange("p (a j) -> p a j", j=64),
             w1[hb * 4096:(hb + 1) * 4096, :].rearrange("(p a) j -> p a j", p=128), ldp)
        _dma(nc.sync, pk3[:].rearrange("p (a j) -> p a j", j=64),
             w3[hb * 4096:(hb + 1) * 4096, :].rearrange("(p a) j -> p a j", p=128), ldp)
        _dma(nc.sync, s1t[:], s1.rearrange("(h g) -> h g", g=64)[hb * 128:(hb + 1) * 128, :], ldp)
        _dma(nc.sync, s3t[:], s3.rearrange("(h g) -> h g", g=64)[hb * 128:(hb + 1) * 128, :], ldp)
        nc.vector.wait_ge(ldp.h, ldp.v)
        if hb >= 1:
            nc.vector.wait_ge(trn.h, trn_done[hb - 1])
        dqq.inc(emit_dequant(nc, nc.vector, pk1[:], s1t[:], W1b[:], mi1[:], pk1[:], CD), 1)
        dq1_v = dqq.v
        dqq.inc(emit_dequant(nc, nc.vector, pk3[:], s3t[:], W3b[:], mi1[:], pk3[:], CD), 1)
        dq_done[hb] = dqq.v
        if hb == 0:
            nc.tensor.wait_ge(ldc.h, ldc.v)
        if hb >= 2:
            nc.tensor.wait_ge(mmf.h, mmfv[hb - 2])
        for (W_b, W_T, on_act, dqv_) in ((W1b, W1T[par], True, dq1_v),
                                         (W3b, W3T[par], False, dq_done[hb])):
            nc.tensor.wait_ge(dqq.h, dqv_)
            for g in range(4):
                bk = g % 2
                bank = psX[bk]
                if bank_free[bk] is not None:
                    s_, v_ = bank_free[bk]
                    nc.tensor.wait_ge(s_.h, v_)
                for j in range(8):
                    c = g * 8 + j
                    ins = nc.tensor.transpose(
                        bank[:, j * 128:(j + 1) * 128],
                        W_b[:].rearrange("p a two j -> p (a two j)")[:, c * 128:(c + 1) * 128],
                        idt[:])
                trn.inc(ins, 1)
                dst = W_T[:, g * 8:(g + 1) * 8, :].rearrange("p c q -> p (c q)")
                if on_act:
                    nc.scalar.wait_ge(trn.h, trn.v)
                    ev1.inc(nc.scalar.copy(out=dst, in_=bank[:]), 1)
                    bank_free[bk] = (ev1, ev1.v)
                else:
                    nc.vector.wait_ge(trn.h, trn.v)
                    ev3.inc(nc.vector.tensor_copy(dst, bank[:]), 1)
                    bank_free[bk] = (ev3, ev3.v)
        trn_done[hb] = trn.v
        w_ready = (ev1.v, ev3.v)
        nc.tensor.wait_ge(ev1.h, w_ready[0])
        nc.tensor.wait_ge(ev3.h, w_ready[1])
        for t4 in range(4):
            tsl = slice(t4 * 512, (t4 + 1) * 512)
            pp = t4 % 2
            pa, pb = psF[2 * pp], psF[2 * pp + 1]
            if psf_free[pp] is not None:
                nc.tensor.wait_ge(ev1.h, psf_free[pp][0])
                nc.tensor.wait_ge(ev3.h, psf_free[pp][1])
            for c in range(CD):
                nc.tensor.matmul(pa[:], W1T[par][:, c, :], xn_sb[:, c, tsl],
                                 start=(c == 0), stop=(c == CD - 1))
            for c in range(CD):
                ins = nc.tensor.matmul(pb[:], W3T[par][:, c, :], xn_sb[:, c, tsl],
                                       start=(c == 0), stop=(c == CD - 1))
            mmf.inc(ins, 1)
            nc.scalar.wait_ge(mmf.h, mmf.v)
            if last_mul is not None:
                nc.scalar.wait_ge(ev3.h, last_mul)
            ev1.inc(nc.scalar.activation(out=sil[:], in_=pa[:], func=AF.Silu), 1)
            silu_v = ev1.v
            nc.vector.wait_ge(ev1.h, silu_v)
            it = hb * 4 + t4
            if it >= 2:
                nc.vector.wait_ge(sth.h, sthv[it - 2])
            ev3.inc(nc.vector.tensor_tensor(out=hst[pp][:], in0=sil[:],
                                            in1=pb[:], op=ALU.mult), 1)
            last_mul = ev3.v
            psf_free[pp] = (silu_v, last_mul)
            nc.gpsimd.wait_ge(ev3.h, last_mul)
            _dma(nc.gpsimd, ha[hb, :, tsl], hst[pp][:], sth)
            sthv[it] = sth.v
        mmfv[hb] = mmf.v
    nc.sync.wait_ge(sth.h, sth.v)
    nc.gpsimd.wait_ge(sth.h, sth.v)


def emit_w2t(C, w2, s2, W2T):
    nc = C.nc
    NG = HB
    pk = [C.carve([128, NG * 64], I32) for _ in range(2)]
    s2t = [C.carve([128, 2 * NG], F32) for _ in range(2)]
    mi = C.carve([128, NG * 64], I32)
    ni = C.carve([128, NG * 64], I32)
    Wb = C.carve([128, NG, 2, 64], BF16)
    stage = [C.carve([128, NG, 128], BF16) for _ in range(2)]
    idt = C.carve([128, 128], BF16)
    psX = [C.psumb() for _ in range(2)]
    ldc, ldp, dqd, trn, evc, stw = (C.s[n] for n in
                                    ("ld1", "ld2", "x1", "mm", "acs", "st1"))
    _dma(nc.sync, idt[:], C.inline("identb", IDENT)[:], ldc)
    dqv = {}
    trv = {}
    stv = {}
    bank_free = {0: None, 1: None}
    for ob in range(CD):
        par = ob % 2
        if ob >= 2:
            nc.sync.wait_ge(dqd.h, dqv[ob - 2])
        _dma(nc.sync, pk[par][:].rearrange("p (a j) -> p a j", j=64),
             w2[ob * 11008:(ob + 1) * 11008, :].rearrange("(p a) j -> p a j", p=128), ldp)
        _dma(nc.sync, s2t[par][:],
             s2.rearrange("(h g) -> h g", g=2 * NG)[ob * 128:(ob + 1) * 128, :], ldp)
        nc.vector.wait_ge(ldp.h, ldp.v)
        if ob >= 1:
            nc.vector.wait_ge(trn.h, trv[ob - 1])
        dqd.inc(emit_dequant(nc, nc.vector, pk[par][:], s2t[par][:], Wb[:],
                             mi[:], ni[:], NG), 1)
        dqv[ob] = dqd.v
        nc.tensor.wait_ge(dqd.h, dqd.v)
        if ob == 0:
            nc.tensor.wait_ge(ldc.h, ldc.v)
        for g in range((NG + 7) // 8):
            bk = g % 2
            bank = psX[bk]
            if bank_free[bk] is not None:
                nc.tensor.wait_ge(evc.h, bank_free[bk])
            nkk = min(8, NG - g * 8)
            for j in range(nkk):
                c = g * 8 + j
                ins = nc.tensor.transpose(
                    bank[:, j * 128:(j + 1) * 128],
                    Wb[:].rearrange("p a two j -> p (a two j)")[:, c * 128:(c + 1) * 128],
                    idt[:])
            trn.inc(ins, 1)
            nc.scalar.wait_ge(trn.h, trn.v)
            if g == 0 and ob >= 2:
                nc.scalar.wait_ge(stw.h, stv[ob - 2])
            evc.inc(nc.scalar.copy(
                out=stage[par][:, g * 8:g * 8 + nkk, :].rearrange("p c q -> p (c q)"),
                in_=bank[:, 0:nkk * 128]), 1)
            bank_free[bk] = evc.v
        trv[ob] = trn.v
        nc.gpsimd.wait_ge(evc.h, evc.v)
        _dma(nc.gpsimd, W2T[ob].rearrange("c p o -> p c o"), stage[par][:], stw)
        stv[ob] = stw.v
    nc.sync.wait_ge(stw.h, stw.v)
    nc.gpsimd.wait_ge(stw.h, stw.v)


def emit_ffn1(C, xn, Wup, gateT, ebias, AT, Ball, xprev, xnew):
    nc = C.nc
    xn_ts = [C.carve([128, CD, 512], BF16) for _ in range(2)]
    gA = C.carve([128, CD, 36], BF16)
    ob1 = C.carve([1, 132], BF16)
    Ball_sb = C.carve([32, D], BF16)
    zsb = C.carve([128, 16, 32], F32)
    lgsb = C.carve([128, 16, 4], F32)
    lg2 = C.carve([128, 16, 4], F32)
    mx = C.carve([128, 16], F32)
    mx2 = C.carve([128, 16], F32)
    oh = C.carve([128, 16, 4], F32)
    zmb = C.carve([128, 16, 32], BF16)
    zmT = C.carve([32, T], BF16)
    wub = [C.carve([128, CD, 128], BF16) for _ in range(2)]
    wubT = C.carve([128, CD, 128], BF16)
    yf1 = C.carve([128, 512], F32)
    y4 = C.carve([128, 4, CD, 128], F32)
    xp = C.carve([128, D], F32)
    xnb = C.carve([128, D], BF16)
    st4 = C.carve([128, 64], F32)
    idtf = C.carve([128, 128], F32)
    idtb = C.carve([128, 128], BF16)
    psY = [C.psum(F32) for _ in range(2)]
    psT = [C.psum(F32) for _ in range(2)]
    psZ = C.psum(F32)
    psL = C.psum(F32)[:, 0:64]
    psB2 = [C.psumb() for _ in range(2)]
    lda, wld, mm, dev, acs, pml, stx = (C.s[n] for n in
                                        ("ld1", "wld", "mm", "dve", "acs", "pml", "st1"))
    wld0 = wld.v

    _dma(nc.sync, idtf[:], C.inline("identf", IDENTF)[:], lda)
    _dma(nc.sync, idtb[:], C.inline("identb", IDENT)[:], lda)
    _dma(nc.gpsimd, ob1[:, 0:128], C.inline("onesb", np.ones((1, 128), dtype=BF))[:], lda)
    _dma(nc.gpsimd, ob1[:, 128:132], ebias[:], lda)
    _dma(nc.gpsimd, gA[:, :, 0:4], gateT[:].rearrange("(c p) e -> p c e", p=128), lda)
    _dma(nc.gpsimd, gA[:, :, 4:36], AT[:].rearrange("(c p) e -> p c e", p=128), lda)
    _dma(nc.gpsimd, Ball_sb[:], Ball[:], lda)

    pre_mm = {}
    for ts in range(4):
        tsl = slice(ts * 512, (ts + 1) * 512)
        buf = xn_ts[ts % 2]
        if ts >= 2:
            nc.sync.wait_ge(mm.h, pre_mm[ts - 2])
        _dma(nc.sync, buf[:], xn[:, :, tsl].rearrange("c p t -> p c t"), lda)
        nc.tensor.wait_ge(lda.h, lda.v)
        for tt4 in range(4):
            tt = ts * 4 + tt4
            sl = slice(tt4 * 128, (tt4 + 1) * 128)
            for c in range(CD):
                nc.tensor.matmul(psL[:, tt * 4:(tt + 1) * 4],
                                 buf[:, c, sl], gA[:, c, 0:4],
                                 start=(c == 0), stop=False)
            nc.tensor.matmul(psL[:, tt * 4:(tt + 1) * 4],
                             ob1[:, 0:128], ob1[:, 128:132],
                             start=False, stop=True)
            for c in range(CD):
                ins = nc.tensor.matmul(psZ[:, tt * 32:(tt + 1) * 32],
                                       buf[:, c, sl], gA[:, c, 4:36],
                                       start=(c == 0), stop=(c == CD - 1))
            mm.inc(ins, 1)
        pre_mm[ts] = mm.v
    nc.vector.wait_ge(mm.h, mm.v)
    nc.vector.tensor_copy(lgsb[:], psL[:].rearrange("p (t e) -> p t e", e=4))
    dev.inc(nc.vector.tensor_copy(zsb[:], psZ[:].rearrange("p (t e) -> p t e", e=32)), 1)
    nc.gpsimd.wait_ge(dev.h, dev.v)
    pml.inc(nc.gpsimd.tensor_copy(lg2[:], lgsb[:]), 1)
    nc.vector.wait_ge(pml.h, pml.v)
    dev.inc(nc.vector.tensor_reduce(mx[:].unsqueeze(2), lg2[:], AX, ALU.max), 1)
    nc.gpsimd.wait_ge(dev.h, dev.v)
    pml.inc(nc.gpsimd.tensor_copy(mx2[:], mx[:]), 1)
    nc.vector.wait_ge(pml.h, pml.v)
    dev.inc(nc.vector.tensor_tensor(
        out=oh[:], in0=lg2[:],
        in1=mx2[:].unsqueeze(2).broadcast_to([128, 16, 4]), op=ALU.is_ge), 1)
    nc.gpsimd.wait_ge(dev.h, dev.v)
    pml.inc(nc.gpsimd.tensor_tensor(
        out=zmb[:].rearrange("p t (e r) -> p t e r", e=4),
        in0=zsb[:].rearrange("p t (e r) -> p t e r", e=4),
        in1=oh[:].unsqueeze(3).broadcast_to([128, 16, 4, 8]), op=ALU.mult), 1)
    nc.tensor.wait_ge(pml.h, pml.v)
    zmt_ev = {}
    for g in range(4):
        bank = psB2[g % 2]
        if g >= 2:
            nc.tensor.wait_ge(dev.h, zmt_ev[g - 2])
        for j in range(4):
            tt = g * 4 + j
            ins = nc.tensor.transpose(bank[0:32, j * 128:(j + 1) * 128],
                                      zmb[:, tt, :], idtb[:])
        mm.inc(ins, 1)
        nc.vector.wait_ge(mm.h, mm.v)
        dev.inc(nc.vector.tensor_copy(zmT[:, g * 512:(g + 1) * 512],
                                      bank[0:32, 0:512]), 1)
        zmt_ev[g] = dev.v

    nwld = 0
    mm_oc = {}
    yf_ev = {}
    trn_ev = {}
    dev_ev = {}
    wup_tr = {}
    wupT_free = {0: zmt_ev.get(2), 1: zmt_ev.get(3)}
    xp_store1 = {}
    pre_done_mm = mm.v
    for ts in range(4):
        tsl = slice(ts * 512, (ts + 1) * 512)
        buf = xn_ts[ts % 2]
        if ts >= 2:
            nc.sync.wait_ge(mm.h, mm_oc[(ts - 1, CD - 1)])
        else:
            nc.sync.wait_ge(mm.h, pre_done_mm)
        _dma(nc.sync, buf[:], xn[:, :, tsl].rearrange("c p t -> p c t"), lda)
        for oc in range(CD):
            par = oc % 2
            wb = wub[par]
            if (ts, oc - 2) in wup_tr:
                nc.sync.wait_ge(mm.h, wup_tr[(ts, oc - 2)])
            elif ts >= 1 and oc < 2:
                nc.sync.wait_ge(mm.h, wup_tr[(ts - 1, CD - 2 + oc)])
            _dma(nc.sync, wb[:],
                 Wup[oc * 128:(oc + 1) * 128, :].rearrange("o (c p) -> o c p", p=128),
                 wld)
            nwld += 1
            nc.tensor.wait_ge(wld.h, wld0 + 16 * nwld)
            nc.tensor.wait_ge(lda.h, lda.v)
            if (ts, oc - 1) in mm_oc:
                nc.tensor.wait_ge(mm.h, mm_oc[(ts, oc - 1)])
            elif ts >= 1 and oc == 0:
                nc.tensor.wait_ge(mm.h, mm_oc[(ts - 1, CD - 1)])
            for g in range(4):
                bk = g % 2
                bank = psB2[bk]
                if wupT_free.get(bk) is not None:
                    nc.tensor.wait_ge(dev.h, wupT_free[bk])
                for j in range(8):
                    c = g * 8 + j
                    ins = nc.tensor.transpose(bank[:, j * 128:(j + 1) * 128],
                                              wb[:, c, :], idtb[:])
                mm.inc(ins, 1)
                nc.vector.wait_ge(mm.h, mm.v)
                dev.inc(nc.vector.tensor_copy(
                    wubT[:, g * 8:(g + 1) * 8, :].rearrange("p c q -> p (c q)"),
                    bank[:]), 1)
                wupT_free[bk] = dev.v
            wup_tr[(ts, oc)] = mm.v
            nc.tensor.wait_ge(dev.h, dev.v)
            if (ts, oc - 2) in yf_ev:
                nc.tensor.wait_ge(acs.h, yf_ev[(ts, oc - 2)])
            elif ts >= 1 and oc < 2:
                nc.tensor.wait_ge(acs.h, yf_ev[(ts - 1, CD - 2 + oc)])
            for c in range(CD):
                nc.tensor.matmul(psY[par][:], wubT[:, c, :], buf[:, c, :],
                                 start=(c == 0), stop=False)
            ins = nc.tensor.matmul(psY[par][:], Ball_sb[0:32, oc * 128:(oc + 1) * 128],
                                   zmT[0:32, tsl], start=False, stop=True)
            mm.inc(ins, 1)
            mm_oc[(ts, oc)] = mm.v
            nc.scalar.wait_ge(mm.h, mm.v)
            if (ts, oc - 1) in trn_ev:
                nc.scalar.wait_ge(mm.h, trn_ev[(ts, oc - 1)])
            acs.inc(nc.scalar.copy(out=yf1[:], in_=psY[par][:]), 1)
            yf_ev[(ts, oc)] = acs.v
            nc.tensor.wait_ge(acs.h, acs.v)
            if (ts, oc - 2) in dev_ev:
                nc.tensor.wait_ge(dev.h, dev_ev[(ts, oc - 2)])
            for a in range(4):
                ins = nc.tensor.transpose(psT[par][:, a * 128:(a + 1) * 128],
                                          yf1[:, a * 128:(a + 1) * 128], idtf[:])
            mm.inc(ins, 1)
            trn_ev[(ts, oc)] = mm.v
            nc.vector.wait_ge(mm.h, mm.v)
            dev.inc(nc.vector.tensor_copy(y4[:, :, oc, :], psT[par][:]
                                          .rearrange("p (a q) -> p a q", a=4)), 1)
            dev_ev[(ts, oc)] = dev.v
        yav = [y4[:, a, :, :].rearrange("p c q -> p (c q)") for a in range(4)]
        nc.vector.wait_ge(dev.h, dev.v)
        parts1 = []
        for a in range(4):
            m_, r_, last_, fns = emit_ln_stats(nc, yav[a], xnb[:], st4[:, 16 * a:16 * a + 8])
            parts1.append((m_, r_, fns))
        dev.inc(last_, 1)
        nc.scalar.wait_ge(dev.h, dev.v)
        for a in range(4):
            ins_ = parts1[a][2][0]()
        acs.inc(ins_, 1)
        nc.vector.wait_ge(acs.h, acs.v)
        for a in range(4):
            ins_ = parts1[a][2][1]()
        dev.inc(ins_, 1)
        nc.scalar.wait_ge(dev.h, dev.v)
        for a in range(4):
            ins_ = parts1[a][2][2]()
        acs.inc(ins_, 1)
        nc.vector.wait_ge(acs.h, acs.v)
        for a in range(4):
            tt = ts * 4 + a
            m_, r_, _ = parts1[a]
            nc.vector.tensor_scalar(out=yav[a], in0=yav[a], scalar1=m_, scalar2=r_,
                                    op0=ALU.subtract, op1=ALU.mult)
            if tt >= 1:
                nc.sync.wait_ge(dev.h, dev.v)
                nc.sync.wait_ge(stx.h, xp_store1[tt - 1])
            _dma(nc.sync, xp[:], xprev[tt * 128:(tt + 1) * 128, :], lda)
            nc.vector.wait_ge(lda.h, lda.v)
            dev.inc(nc.vector.tensor_tensor(out=yav[a], in0=xp[:], in1=yav[a],
                                            op=ALU.add), 1)
            xp_store1[tt] = stx.v + 16
            nc.gpsimd.wait_ge(dev.h, dev.v)
            _dma(nc.gpsimd, xnew[tt * 128:(tt + 1) * 128, :], yav[a], stx)
    nc.sync.wait_ge(stx.h, stx.v)
    nc.gpsimd.wait_ge(stx.h, stx.v)


def emit_precast(C, src_f32, dst_bf):
    nc = C.nc
    buf = [C.carve([128, D], BF16) for _ in range(2)]
    ld, stx = C.s["ld1"], C.s["st1"]
    st_v = {}
    for i in range(CD):
        par = i % 2
        if i >= 2:
            nc.gpsimd.wait_ge(stx.h, st_v[i - 2])
        _dma(nc.gpsimd, buf[par][:], src_f32[i * 128:(i + 1) * 128, :], ld)
        nc.sync.wait_ge(ld.h, ld.v)
        _dma(nc.sync, dst_bf[i * 128:(i + 1) * 128, :], buf[par][:], stx)
        st_v[i] = stx.v
    nc.sync.wait_ge(stx.h, stx.v)
    nc.gpsimd.wait_ge(stx.h, stx.v)



def _newC():
    C = Ctx()
    C.setup()
    C.section()
    return C


def build_ln0():
    C = _newC()
    nc = C.nc
    x = nc.dram_tensor("x", [T, D], F32, kind="ExternalInput").ap()
    xn = nc.dram_tensor("xn", [CD, 128, T], BF16, kind="ExternalOutput").ap()
    emit_ln0(C, x, xn)
    return C.finish()


def build_qkv(layer=0):
    C = _newC()
    nc = C.nc
    xn = nc.dram_tensor("xn", [CD, 128, T], BF16, kind="ExternalInput").ap()
    Wq = nc.dram_tensor("Wq", [D, D], F32, kind="ExternalInput").ap()
    Wk = nc.dram_tensor("Wk", [D, HKV * DH], F32, kind="ExternalInput").ap()
    Wv = nc.dram_tensor("Wv", [D, HKV * DH], F32, kind="ExternalInput").ap()
    cq = nc.dram_tensor("cq", [128, T], F32, kind="ExternalInput").ap()
    sq = nc.dram_tensor("sq", [128, T], F32, kind="ExternalInput").ap()
    ck = nc.dram_tensor("ck", [128, T], F32, kind="ExternalInput").ap()
    sk = nc.dram_tensor("sk", [128, T], F32, kind="ExternalInput").ap()
    Qo = nc.dram_tensor("Q", [HQ, 128, T], BF16, kind="ExternalOutput").ap()
    Ko = nc.dram_tensor("K", [HKV, 128, T], BF16, kind="ExternalOutput").ap()
    Vo = nc.dram_tensor("V", [T, HKV * DH], BF16, kind="ExternalOutput").ap()
    emit_qkv(C, xn, Wq, Wk, Wv, cq, sq, ck, sk, Qo, Ko, Vo)
    return C.finish()


def build_attn():
    C = _newC()
    nc = C.nc
    Qi = nc.dram_tensor("Q", [HQ, 128, T], BF16, kind="ExternalInput").ap()
    Ko = nc.dram_tensor("Kown", [HKV, 128, T], BF16, kind="ExternalInput").ap()
    Kp = nc.dram_tensor("Kpart", [HKV, 128, T], BF16, kind="ExternalInput").ap()
    Vo_ = nc.dram_tensor("Vown", [T, HKV * DH], BF16, kind="ExternalInput").ap()
    Vp = nc.dram_tensor("Vpart", [T, HKV * DH], BF16, kind="ExternalInput").ap()
    Mi = nc.dram_tensor("masks", [NTT, 128, 512], BF16, kind="ExternalInput").ap()
    Co = nc.dram_tensor("ctx", [HQ, 128, T], BF16, kind="ExternalOutput").ap()
    emit_attn(C, Qi, Ko, Kp, Vo_, Vp, Mi, Co)
    return C.finish()


def build_combine(kind):
    C = _newC()
    nc = C.nc
    nK = CD if kind == "wo" else HB
    A = nc.dram_tensor("A", [nK, 128, T], BF16, kind="ExternalInput").ap()
    if kind == "wo":
        W = nc.dram_tensor("W", [D, D], BF16, kind="ExternalInput").ap()
    else:
        W = nc.dram_tensor("W", [CD, HB, 128, 128], BF16, kind="ExternalInput").ap()
    xprev = nc.dram_tensor("xprev", [T, D], F32, kind="ExternalInput").ap()
    xnew = nc.dram_tensor("xnew", [T, D], F32, kind="ExternalOutput").ap()
    xn2 = nc.dram_tensor("xn2", [CD, 128, T], BF16, kind="ExternalOutput").ap()
    emit_combine(C, kind, A, W, xprev, xnew, xn2)
    return C.finish()


def build_ffn0up():
    C = _newC()
    nc = C.nc
    xn = nc.dram_tensor("xn", [CD, 128, T], BF16, kind="ExternalInput").ap()
    w1 = nc.dram_tensor("w1", [352256, 64], I32, kind="ExternalInput").ap()
    s1 = nc.dram_tensor("s1", [704512], F32, kind="ExternalInput").ap()
    w3 = nc.dram_tensor("w3", [352256, 64], I32, kind="ExternalInput").ap()
    s3 = nc.dram_tensor("s3", [704512], F32, kind="ExternalInput").ap()
    ha = nc.dram_tensor("ha", [HB, 128, T], BF16, kind="ExternalOutput").ap()
    emit_ffn0up(C, xn, w1, s1, w3, s3, ha)
    return C.finish()


def build_w2t():
    C = _newC()
    nc = C.nc
    w2 = nc.dram_tensor("w2", [352256, 64], I32, kind="ExternalInput").ap()
    s2 = nc.dram_tensor("s2", [704512], F32, kind="ExternalInput").ap()
    W2T = nc.dram_tensor("W2T", [CD, HB, 128, 128], BF16, kind="ExternalOutput").ap()
    emit_w2t(C, w2, s2, W2T)
    return C.finish()


def build_ffn1():
    C = _newC()
    nc = C.nc
    xn = nc.dram_tensor("xn", [CD, 128, T], BF16, kind="ExternalInput").ap()
    Wup = nc.dram_tensor("Wup", [D, D], BF16, kind="ExternalInput").ap()
    gateT = nc.dram_tensor("gateT", [D, 4], F32, kind="ExternalInput").ap()
    ebias = nc.dram_tensor("ebias", [1, 4], F32, kind="ExternalInput").ap()
    AT = nc.dram_tensor("AT", [D, 32], F32, kind="ExternalInput").ap()
    Ball = nc.dram_tensor("Ball", [32, D], F32, kind="ExternalInput").ap()
    xprev = nc.dram_tensor("xprev", [T, D], F32, kind="ExternalInput").ap()
    xnew = nc.dram_tensor("xnew", [T, D], F32, kind="ExternalOutput").ap()
    emit_ffn1(C, xn, Wup, gateT, ebias, AT, Ball, xprev, xnew)
    return C.finish()


def build_megaA():
    C = _newC()
    nc = C.nc
    EI = lambda n, s, d: nc.dram_tensor(n, s, d, kind="ExternalInput").ap()
    EO = lambda n, s, d: nc.dram_tensor(n, s, d, kind="ExternalOutput").ap()
    IN = lambda n, s, d: nc.dram_tensor(n, s, d, kind="Internal").ap()
    x = EI("x", [T, D], F32)
    xpart = EI("xpart", [T, D], F32)
    Wq0 = EI("Wq0", [D, D], F32)
    Wk0 = EI("Wk0", [D, HKV * DH], F32)
    Wv0 = EI("Wv0", [D, HKV * DH], F32)
    Wo0 = EI("Wo0", [D, D], F32)
    w1 = EI("w1", [352256, 64], I32)
    s1 = EI("s1", [704512], F32)
    w2 = EI("w2", [352256, 64], I32)
    s2 = EI("s2", [704512], F32)
    w3 = EI("w3", [352256, 64], I32)
    s3 = EI("s3", [704512], F32)
    Wq1 = EI("Wq1", [D, D], F32)
    Wk1 = EI("Wk1", [D, HKV * DH], F32)
    Wv1 = EI("Wv1", [D, HKV * DH], F32)
    cq0 = EI("cq0", [128, T], F32)
    sq0 = EI("sq0", [128, T], F32)
    ck0 = EI("ck0", [128, T], F32)
    sk0 = EI("sk0", [128, T], F32)
    ck0p = EI("ck0p", [128, T], F32)
    sk0p = EI("sk0p", [128, T], F32)
    cq1 = EI("cq1", [128, T], F32)
    sq1 = EI("sq1", [128, T], F32)
    ck1 = EI("ck1", [128, T], F32)
    sk1 = EI("sk1", [128, T], F32)
    masks = EI("masks", [NTT, 128, 512], BF16)
    x2o = EO("x2", [T, D], F32)
    Q1o = EO("Q1", [HQ, 128, T], BF16)
    K1o = EO("K1", [HKV, 128, T], BF16)
    V1o = EO("V1", [T, HKV * DH], BF16)

    xn0 = IN("xn0", [CD, 128, T], BF16)
    xn0p = IN("xn0p", [CD, 128, T], BF16)
    Q0 = IN("Q0", [HQ, 128, T], BF16)
    K0 = IN("K0", [HKV, 128, T], BF16)
    V0 = IN("V0", [T, HKV * DH], BF16)
    K0p = IN("K0p", [HKV, 128, T], BF16)
    V0p = IN("V0p", [T, HKV * DH], BF16)
    ctx0 = IN("ctx0", [HQ, 128, T], BF16)
    Wo0b = IN("Wo0b", [D, D], BF16)
    x1 = IN("x1", [T, D], F32)
    xnf = IN("xnf", [CD, 128, T], BF16)
    ha = IN("ha", [HB, 128, T], BF16)
    W2T = IN("W2T", [CD, HB, 128, 128], BF16)
    x2i = IN("x2i", [T, D], F32)
    xn3 = IN("xn3", [CD, 128, T], BF16)

    emit_ln0(C, x, xn0)
    C.join(); C.section()
    emit_ln0(C, xpart, xn0p)
    C.join(); C.section()
    emit_qkv(C, xn0, Wq0, Wk0, Wv0, cq0, sq0, ck0, sk0, Q0, K0, V0)
    C.join(); C.section()
    emit_qkv(C, xn0p, None, Wk0, Wv0, None, None, ck0p, sk0p, None, K0p, V0p)
    C.join(); C.section()
    emit_attn(C, Q0, K0, K0p, V0, V0p, masks, ctx0)
    C.join(); C.section()
    emit_precast(C, Wo0, Wo0b)
    C.join(); C.section()
    emit_combine(C, "wo", ctx0, Wo0b, x, x1, xnf)
    C.join(); C.section()
    emit_w2t(C, w2, s2, W2T)
    C.join(); C.section()
    emit_ffn0up(C, xnf, w1, s1, w3, s3, ha)
    C.join(); C.section()
    emit_combine(C, "ffn0down", ha, W2T, x1, x2o, xn3)
    C.join(); C.section()
    emit_qkv(C, xn3, Wq1, Wk1, Wv1, cq1, sq1, ck1, sk1, Q1o, K1o, V1o)
    return C.finish()


def build_megaB():
    C = _newC()
    nc = C.nc
    EI = lambda n, s, d: nc.dram_tensor(n, s, d, kind="ExternalInput").ap()
    EO = lambda n, s, d: nc.dram_tensor(n, s, d, kind="ExternalOutput").ap()
    IN = lambda n, s, d: nc.dram_tensor(n, s, d, kind="Internal").ap()
    x2 = EI("x2", [T, D], F32)
    Q1 = EI("Q1", [HQ, 128, T], BF16)
    K1 = EI("K1", [HKV, 128, T], BF16)
    K1p = EI("K1p", [HKV, 128, T], BF16)
    V1 = EI("V1", [T, HKV * DH], BF16)
    V1p = EI("V1p", [T, HKV * DH], BF16)
    masks = EI("masks", [NTT, 128, 512], BF16)
    Wo1 = EI("Wo1", [D, D], F32)
    Wup = EI("Wup", [D, D], F32)
    gateT = EI("gateT", [D, 4], F32)
    ebias = EI("ebias", [1, 4], F32)
    AT = EI("AT", [D, 32], F32)
    Ball = EI("Ball", [32, D], F32)
    x4o = EO("x4", [T, D], F32)
    ctx1 = IN("ctx1", [HQ, 128, T], BF16)
    Wo1b = IN("Wo1b", [D, D], BF16)
    Wupb = IN("Wupb", [D, D], BF16)
    x3 = IN("x3", [T, D], F32)
    xn4 = IN("xn4", [CD, 128, T], BF16)

    emit_attn(C, Q1, K1, K1p, V1, V1p, masks, ctx1)
    C.join(); C.section()
    emit_precast(C, Wo1, Wo1b)
    C.join(); C.section()
    emit_precast(C, Wup, Wupb)
    C.join(); C.section()
    emit_combine(C, "wo", ctx1, Wo1b, x2, x3, xn4)
    C.join(); C.section()
    emit_ffn1(C, xn4, Wupb, gateT, ebias, AT, Ball, x3, x4o)
    return C.finish()



_PROGS = {}


def _prog(name, fn):
    if name not in _PROGS:
        _PROGS[name] = fn()
    return _PROGS[name]


def _run(name, fn, in_maps):
    nc = _prog(name, fn)
    last = None
    for attempt in range(3):
        try:
            return run_bass_kernel_spmd(nc, in_maps, core_ids=list(range(8))).results
        except Exception as e:
            last = e
            if "UNAVAILABLE" not in str(e) and "unrecoverable" not in str(e) \
                    and "axon_stop_nrt_profile" not in str(e):
                raise
            import time as _t
            _t.sleep(5)
    raise last


def kernel(**inputs):
    ins = {k: np.asarray(v) for k, v in inputs.items()}
    x0 = ins["L0_attn_resid"]
    NCORE = 8
    roles = [c % 2 for c in range(NCORE)]
    batches = [c // 2 for c in range(NCORE)]
    pos = {r: tok_positions(r) for r in (0, 1)}
    masks = {r: diag_masks(r) for r in (0, 1)}
    sc = 1.0 / np.sqrt(128.0)
    tabs = {}
    for layer in (0, 1):
        for r in (0, 1):
            Ct, St = rope_tables(layer, r)
            tabs[(layer, r)] = (Ct, St)

    gateT = np.ascontiguousarray(ins["L1_ffn_gate_w"].T)
    AT = np.ascontiguousarray(ins["L1_ffn_lora_A"].transpose(2, 0, 1).reshape(D, 32))
    Ball = np.ascontiguousarray(ins["L1_ffn_lora_B"].transpose(0, 2, 1).reshape(32, D))
    ebias = np.ascontiguousarray(ins["L1_ffn_expert_bias"].reshape(1, 4))

    in_maps_A = []
    for c in range(NCORE):
        r, b = roles[c], batches[c]
        C0, S0 = tabs[(0, r)]
        C0p, S0p = tabs[(0, 1 - r)]
        C1, S1 = tabs[(1, r)]
        in_maps_A.append({
            "x": np.ascontiguousarray(x0[b][pos[r]]),
            "xpart": np.ascontiguousarray(x0[b][pos[1 - r]]),
            "Wq0": ins["L0_attn_Wq"], "Wk0": ins["L0_attn_Wk"],
            "Wv0": ins["L0_attn_Wv"], "Wo0": ins["L0_attn_Wo"],
            "w1": ins["L0_ffn_w1"], "s1": ins["L0_ffn_s1"],
            "w2": ins["L0_ffn_w2"], "s2": ins["L0_ffn_s2"],
            "w3": ins["L0_ffn_w3"], "s3": ins["L0_ffn_s3"],
            "Wq1": ins["L1_attn_Wq"], "Wk1": ins["L1_attn_Wk"],
            "Wv1": ins["L1_attn_Wv"],
            "cq0": C0 * sc, "sq0": S0 * sc, "ck0": C0, "sk0": S0,
            "ck0p": C0p, "sk0p": S0p,
            "cq1": C1 * sc, "sq1": S1 * sc, "ck1": C1, "sk1": S1,
            "masks": masks[r],
        })
    resA = _run("megaA", build_megaA, in_maps_A)

    in_maps_B = []
    for c in range(NCORE):
        r, b = roles[c], batches[c]
        partner = 2 * b + (1 - r)
        in_maps_B.append({
            "x2": resA[c]["x2"],
            "Q1": resA[c]["Q1"],
            "K1": resA[c]["K1"], "K1p": resA[partner]["K1"],
            "V1": resA[c]["V1"], "V1p": resA[partner]["V1"],
            "masks": masks[r],
            "Wo1": ins["L1_attn_Wo"],
            "Wup": ins["L1_ffn_W_up"], "gateT": gateT, "ebias": ebias,
            "AT": AT, "Ball": Ball,
        })
    resB = _run("megaB", build_megaB, in_maps_B)

    out = np.empty((4, TG, D), np.float32)
    for c in range(NCORE):
        out[batches[c]][pos[roles[c]]] = resB[c]["x4"]
    return out
